# version 17
# baseline (speedup 1.0000x reference)
import numpy as np
from contextlib import ExitStack

import concourse.bass as bass
import concourse.mybir as mybir
from concourse.bass_utils import run_bass_kernel_spmd
from concourse.hw_specs import get_activation_tables


def _load_act_set(nc, funcs):
    for idx, fs in enumerate(get_activation_tables(nc.m.arch).values()):
        if all(f in fs for f in funcs):
            nc.scalar.add_instruction(mybir.InstLoadActFuncSet(
                name=nc.get_next_instruction_name(),
                act_func_set_id=idx, ins=[], outs=[]))
            return
    raise ValueError(f"no table set contains {funcs}")

F32 = mybir.dt.float32
BF16 = mybir.dt.bfloat16
AF = mybir.ActivationFunctionType
ALU = mybir.AluOpType
AX = mybir.AxisListType

NTOK = 4096
D = 1024
NT = 32
KD = 8
DFF = 4096
NEXP = 8
EPS = 1e-5

CONSTS = [-0.5, EPS, 1024.0, 1.0 / 32.0, -1.0, 1.0 / 1024.0, 1e-24, -EPS]

VOFF = 32768
VSTRIDE = 1024
ONESCOL = 65536


def build(debug_taps=False):
    nc = bass.Bass("TRN2", target_bir_lowering=False)

    x_h = nc.dram_tensor("x", [NTOK, D], F32, kind="ExternalInput").ap()
    wqT_h = nc.dram_tensor("wqT", [D, D], F32, kind="ExternalInput").ap()
    wkT_h = nc.dram_tensor("wkT", [D, D], F32, kind="ExternalInput").ap()
    wvT_h = nc.dram_tensor("wvT", [D, D], F32, kind="ExternalInput").ap()
    w1s_h = nc.dram_tensor("w1s", [D, DFF], F32, kind="ExternalInput").ap()
    w2s_h = nc.dram_tensor("w2s", [DFF, D], F32, kind="ExternalInput").ap()
    b1m_h = nc.dram_tensor("b1m", [128, 32], F32, kind="ExternalInput").ap()
    b2_h = nc.dram_tensor("b2", [NEXP, D], F32, kind="ExternalInput").ap()
    cent_h = nc.dram_tensor("cent", [NEXP, D], F32, kind="ExternalInput").ap()
    wrt_h = nc.dram_tensor("wrt", [NEXP, D], F32, kind="ExternalInput").ap()
    ident_h = nc.dram_tensor("ident", [128, 128], BF16, kind="ExternalInput").ap()
    sel_h = nc.dram_tensor("sel", [NEXP, NEXP * 128], BF16, kind="ExternalInput").ap()
    cmask_h = nc.dram_tensor("cmask", [128, 128], BF16, kind="ExternalInput").ap()
    consts_h = nc.dram_tensor("consts", [1, len(CONSTS)], F32, kind="ExternalInput").ap()

    skind = "ExternalOutput" if debug_taps else "Internal"
    qt_scr = nc.dram_tensor("qt_scr", [8, 128, 4096], BF16, kind=skind).ap()
    ln2t_scr = nc.dram_tensor("ln2t_scr", [16, 128, 2048], BF16, kind=skind).ap()
    x2_scr = nc.dram_tensor("x2_scr", [16, 128, 2048], F32, kind=skind).ap()

    out_h = nc.dram_tensor("out", [NTOK, D], F32, kind="ExternalOutput").ap()

    V = {}

    with ExitStack() as topctx:
        E = topctx.enter_context

        arena = E(nc.sbuf_tensor("arena", [128, 65544], BF16))
        gates_tm = E(nc.sbuf_tensor("gates_tm", [128, 256], BF16))
        ident = E(nc.sbuf_tensor("ident_s", [128, 128], BF16))
        cmask = E(nc.sbuf_tensor("cmask_s", [128, 128], BF16))
        cw = E(nc.sbuf_tensor("cw_s", [128, 128], BF16))
        sel = E(nc.sbuf_tensor("sel_s", [NEXP, NEXP * 128], BF16))
        b1s = E(nc.sbuf_tensor("b1_s", [128, 32], F32))
        consts = E(nc.sbuf_tensor("consts_s", [1, len(CONSTS)], F32))
        stt = E(nc.sbuf_tensor("stt", [128, 20], F32))
        rt16 = E(nc.sbuf_tensor("rt16", [128, 16], F32))
        logit = E(nc.sbuf_tensor("logit", [128, 8], F32))
        top8 = E(nc.sbuf_tensor("top8", [128, 8], F32))
        expl = E(nc.sbuf_tensor("expl", [128, 8], F32))
        maskg = E(nc.sbuf_tensor("maskg", [128, 8], F32))
        gated = E(nc.sbuf_tensor("gated", [128, 8], F32))
        cstat = E(nc.sbuf_tensor("cstat", [128, 4], F32))

        class Ctr:
            def __init__(self, name):
                self.h = E(nc.semaphore(name))
                self.n = 0

            def inc(self, instr, k=1):
                instr.then_inc(self.h, k)
                self.n += k
                return self.n

        sIn = Ctr("sIn")
        sWt = [Ctr("sWt%d" % i) for i in range(3)]
        sSp = Ctr("sSp")
        sPe = Ctr("sPe")
        sAc = Ctr("sAc")
        sVe = Ctr("sVe")
        sX = [Ctr("sX0"), Ctr("sX1")]
        sQb = [Ctr("sQ0"), Ctr("sQ1")]
        sXr = [Ctr("sXr0"), Ctr("sXr1")]
        sSl = [Ctr("sSl0"), Ctr("sSl1")]
        sG = [Ctr("sG0"), Ctr("sG1")]
        sW1p = [Ctr("sW1p%d" % i) for i in range(4)]
        sW2c = Ctr("sW2c")
        sB2 = Ctr("sB2")
        sSel = Ctr("sSel")
        sF = [Ctr("sF0"), Ctr("sF1")]
        sMs = Ctr("sMs")
        sGp = Ctr("sGp")

        blk = E(nc.Block())

        def _preload_lnexp(sc):
            _load_act_set(nc, [AF.Ln, AF.Exp, AF.Identity])
        blk.scalar(_preload_lnexp)

        with ExitStack() as pset:
            Es = pset.enter_context
            centf = Es(nc.sbuf_tensor("centf", [NEXP, D], F32))
            wrtf = Es(nc.sbuf_tensor("wrtf", [NEXP, D], F32))
            cscr = Es(nc.sbuf_tensor("cscr", [NEXP, D], F32))
            cnormbf = Es(nc.sbuf_tensor("cnormbf", [NEXP, D], BF16))
            wrtbf = Es(nc.sbuf_tensor("wrtbf", [NEXP, D], BF16))
            ptmp = Es(nc.psum_tensor("ptmp", [128, 1024], BF16))

            def setup_sync(sync):
                for dst, src in [(ident, ident_h), (cmask, cmask_h),
                                 (b1s, b1m_h), (consts, consts_h),
                                 (centf, cent_h), (wrtf, wrt_h),
                                 (sel, sel_h)]:
                    sIn.inc(sync.dma_start(out=dst[:], in_=src), 16)
                V["setup_in"] = sIn.n
            blk.sync(setup_sync)

            for i, cv in enumerate(CONSTS):
                nc.const_aps.register(F32, cv, consts[0:1, i:i + 1])

            def setup_vec(v):
                v.wait_ge(sIn.h, V["setup_in"])
                w = sVe.inc(v.scalar_tensor_tensor(
                    cscr[:], centf[:], 1.0, centf[:],
                    ALU.bypass, ALU.mult, accum_out=cstat[0:8, 0:1]))
                v.wait_ge(sVe.h, w)
                V["cstat"] = sVe.inc(v.tensor_scalar(
                    cstat[0:8, 3:4], cstat[0:8, 0:1], 1e-24, None, ALU.add))
            blk.vector(setup_vec)

            def setup_scal(s):
                s.wait_ge(sVe.h, V["cstat"])
                w = sAc.inc(s.activation(cstat[0:8, 1:2], cstat[0:8, 3:4], AF.Ln))
                s.wait_ge(sAc.h, w)
                V["c_rc"] = sAc.inc(
                    s.activation(cstat[0:8, 2:3], cstat[0:8, 1:2], AF.Exp, scale=-0.5))
            blk.scalar(setup_scal)

            def setup_vec2(v):
                v.wait_ge(sAc.h, V["c_rc"])
                v.tensor_scalar(cnormbf[:], centf[:], cstat[0:8, 2:3], None, ALU.mult)
                V["cnorm"] = sVe.inc(v.tensor_copy(wrtbf[:], wrtf[:]))
            blk.vector(setup_vec2)

            def cw_pe(pe):
                pe.wait_ge(sVe.h, V["cnorm"])
                for kd in range(KD):
                    pe.transpose(ptmp[:, kd * 16:kd * 16 + 8],
                                 cnormbf[0:8, kd * 128:(kd + 1) * 128], ident[0:8, 0:8])
                    mm = pe.transpose(ptmp[:, kd * 16 + 8:kd * 16 + 16],
                                      wrtbf[0:8, kd * 128:(kd + 1) * 128], ident[0:8, 0:8])
                V["cw_tr"] = sPe.inc(mm)
            blk.tensor(cw_pe)

            def cw_ve(v):
                v.wait_ge(sPe.h, V["cw_tr"])
                V["cw"] = sVe.inc(v.tensor_copy(cw[:], ptmp[:, 0:128]))
            blk.vector(cw_ve)

            if debug_taps:
                cw0dump = nc.dram_tensor("cw0dump", [128, 128], BF16,
                                         kind="ExternalOutput").ap()
                cn0dump = nc.dram_tensor("cn0dump", [NEXP, D], BF16,
                                         kind="ExternalOutput").ap()

                def dump_cw0(gp):
                    gp.wait_ge(sVe.h, V["cw"])
                    sSp.inc(gp.dma_start(out=cw0dump, in_=cw[:]), 16)
                    sSp.inc(gp.dma_start(out=cn0dump, in_=cnormbf[:]), 16)
                    V["cw0d"] = sSp.n
                blk.gpsimd(dump_cw0)

        with ExitStack() as pa1:
            Ea = pa1.enter_context
            wqkv = Ea(nc.sbuf_tensor("wqkv", [128, 3, KD, D], BF16))
            scr = Ea(nc.sbuf_tensor("scr", [128, D], F32))
            xbuf = Ea(nc.sbuf_tensor("xbuf", [128, 2, D], F32))
            xln = Ea(nc.sbuf_tensor("xln", [128, 2, D], BF16))
            xlnT = Ea(nc.sbuf_tensor("xlnT", [128, 2, KD, 256], BF16))
            qstage = Ea(nc.sbuf_tensor("qstage", [128, KD, 256], BF16))
            pqk = [Ea(nc.psum_tensor("pqk0", [128, 512], F32)),
                   Ea(nc.psum_tensor("pqk1", [128, 512], F32))]
            pv = [Ea(nc.psum_tensor("pv0", [128, 1024], F32)),
                  Ea(nc.psum_tensor("pv1", [128, 1024], F32))]
            ptr = [Ea(nc.psum_tensor("ptr0", [128, 1024], BF16)),
                   Ea(nc.psum_tensor("ptr1", [128, 1024], BF16))]

            def a1_gpsimd(g):
                g.wait_ge(sVe.h, V["cw"])
                for i, wh in enumerate([wqT_h, wkT_h, wvT_h]):
                    V["wqkv%d" % i] = sWt[i].inc(g.dma_start(
                        out=wqkv[:, i, :, :],
                        in_=wh.rearrange("(kd p) m -> p kd m", p=128)), 16)
                V["vones"] = sMs.inc(
                    g.memset(arena[:, ONESCOL:ONESCOL + 1], 1.0))
            blk.gpsimd(a1_gpsimd)

            x_ld = [None] * NT
            xln_done = [None] * NT
            sq_done = [None] * NT
            stats_done = [None] * NT
            negmr_done = [None] * NT
            tr_done = [None] * NT
            xlt_done = [None] * NT
            qk_done = {}
            qk_ev = {}
            v_done = [None] * NT
            v_ev = [None] * NT
            qspill = [None] * 16

            def emit_a1_head(t):
                def a1_sync_t(sync, t=t):
                    if t == 0:
                        sync.wait_ge(sVe.h, V["cw"])
                    if t >= 2:
                        sync.wait_ge(sAc.h, xln_done[t - 2])
                    x_ld[t] = sX[t % 2].inc(sync.dma_start(
                        out=xbuf[:, t % 2, :],
                        in_=x_h[t * 128:(t + 1) * 128, :]), 16)
                blk.sync(a1_sync_t)

                def a1_vec_t(v, t=t):
                    v.wait_ge(sX[t % 2].h, x_ld[t])
                    if t >= 1:
                        v.wait_ge(sAc.h, V["rstd%d" % (t - 1)])
                    xb = xbuf[:, t % 2, :]
                    v.reduce_sum(stt[:, 0:1], xb, AX.X)
                    w = sVe.inc(v.scalar_tensor_tensor(
                        scr[:], xb, 1.0, xb, ALU.bypass, ALU.mult,
                        accum_out=stt[:, 1:2]))
                    v.wait_ge(sVe.h, w)
                    w = sVe.inc(v.tensor_scalar(stt[:, 2:3], stt[:, 0:1], 1.0 / 1024.0,
                                                None, ALU.mult))
                    v.wait_ge(sVe.h, w)
                    w = sVe.inc(v.tensor_scalar(stt[:, 7:8], stt[:, 2:3], stt[:, 2:3],
                                                -EPS, ALU.mult, ALU.add))
                    v.wait_ge(sVe.h, w)
                    stats_done[t] = sVe.inc(
                        v.tensor_scalar(stt[:, 3:4], stt[:, 1:2], 1.0 / 1024.0,
                                        stt[:, 7:8], ALU.mult, ALU.subtract))
                blk.vector(a1_vec_t)

                def a1_scal_t(s, t=t):
                    s.wait_ge(sVe.h, stats_done[t])
                    w = sAc.inc(s.activation(stt[:, 4:5], stt[:, 3:4], AF.Ln))
                    s.wait_ge(sAc.h, w)
                    V["rstd%d" % t] = sAc.inc(
                        s.activation(stt[:, 5:6], stt[:, 4:5], AF.Exp, scale=-0.5))
                blk.scalar(a1_scal_t)

                def a1_vec2_t(v, t=t):
                    v.wait_ge(sAc.h, V["rstd%d" % t])
                    negmr_done[t] = sVe.inc(
                        v.tensor_scalar(stt[:, 6:7], stt[:, 2:3], -1.0,
                                        stt[:, 5:6], ALU.mult, ALU.mult))
                blk.vector(a1_vec2_t)

                def a1_scal2_t(s, t=t):
                    s.wait_ge(sVe.h, negmr_done[t])
                    if t >= 2:
                        s.wait_ge(sPe.h, tr_done[t - 2])
                    xln_done[t] = sAc.inc(
                        s.activation(xln[:, t % 2, :], xbuf[:, t % 2, :],
                                     AF.Identity,
                                     bias=stt[:, 6:7], scale=stt[:, 5:6]))
                blk.scalar(a1_scal2_t)

            def emit_a1_trx(t):
                def a1_pe_t(pe, t=t):
                    pe.wait_ge(sAc.h, xln_done[t])
                    if t >= 2:
                        pe.wait_ge(sVe.h, xlt_done[t - 2])
                    for kd in range(KD):
                        mm = pe.transpose(ptr[t % 2][:, kd * 128:(kd + 1) * 128],
                                          xln[:, t % 2, kd * 128:(kd + 1) * 128],
                                          ident[:])
                    tr_done[t] = sPe.inc(mm)
                blk.tensor(a1_pe_t)

                def a1_vec3_t(v, t=t):
                    v.wait_ge(sPe.h, tr_done[t])
                    if t // 2 >= 2:
                        v.wait_ge(sPe.h, v_done[2 * (t // 2 - 2) + 1])
                    xlt_done[t] = sVe.inc(v.tensor_copy(
                        xlnT[:, (t // 2) % 2, :, (t % 2) * 128:(t % 2) * 128 + 128],
                        ptr[t % 2][:].rearrange("p (kd c) -> p kd c", kd=KD)))
                blk.vector(a1_vec3_t)

            def emit_a1_vburst(g):
                for tl in range(2):
                    def a1_pe_v(pe, g=g, tl=tl):
                        if g == 0 and tl == 0:
                            pe.wait_ge(sWt[2].h, V["wqkv2"])
                        if g >= 1:
                            pe.wait_ge(sAc.h, v_ev[2 * (g - 1) + tl])
                        for kd in range(KD):
                            for half in range(2):
                                mm = pe.matmul(
                                    pv[tl][:, half * 512:(half + 1) * 512],
                                    xlnT[:, g % 2, kd, tl * 128:(tl + 1) * 128],
                                    wqkv[:, 2, kd, half * 512:(half + 1) * 512],
                                    start=(kd == 0), stop=(kd == KD - 1))
                        v_done[2 * g + tl] = sPe.inc(mm)
                    blk.tensor(a1_pe_v)

                    def a1_sc_v(s, g=g, tl=tl):
                        j = 2 * g + tl
                        s.wait_ge(sPe.h, v_done[j])
                        base = VOFF + j * VSTRIDE
                        v_ev[j] = sAc.inc(
                            s.copy(arena[:, base:base + 1024], pv[tl][:]))
                    blk.scalar(a1_sc_v)

                def a1_gp_g(gp, g=g):
                    gq, sub = g // 2, g % 2
                    gp.wait_ge(sAc.h, qk_ev[(g, 7)])
                    dst = qt_scr[gq].rearrange("p (m tc) -> p m tc", m=KD)[
                        :, :, sub * 256:sub * 256 + 256]
                    qspill[g] = sSp.inc(
                        gp.dma_start(out=dst, in_=qstage[:]), 16)
                blk.gpsimd(a1_gp_g)

            def emit_a1_qkburst(g):
                for u in range(16):
                    proj, m = u // 8, u % 8

                    def a1_pe_u(pe, g=g, u=u, proj=proj, m=m):
                        if u == 0:
                            pe.wait_ge(sVe.h, xlt_done[2 * g + 1])
                        if g == 0 and u % 8 == 0:
                            pe.wait_ge(sWt[proj].h, V["wqkv%d" % proj])
                        prev = qk_ev.get((g, u - 2)) if u >= 2 else (
                            qk_ev.get((g - 1, 14 + u)) if g >= 1 else None)
                        if prev is not None:
                            pe.wait_ge(sAc.h, prev)
                        for kd in range(KD):
                            mm = pe.matmul(
                                pqk[u % 2][:, 0:256],
                                wqkv[:, proj, kd, m * 128:(m + 1) * 128],
                                xlnT[:, g % 2, kd, :],
                                start=(kd == 0), stop=(kd == KD - 1))
                        qk_done[(g, u)] = sPe.inc(mm)
                    blk.tensor(a1_pe_u)

                    def a1_sc_u(s, g=g, u=u, proj=proj, m=m):
                        s.wait_ge(sPe.h, qk_done[(g, u)])
                        if u == 0 and g >= 1:
                            s.wait_ge(sSp.h, qspill[g - 1])
                        if proj == 0:
                            dst = qstage[:, m, :]
                        else:
                            dst = arena[:, m * 4096 + g * 256:
                                        m * 4096 + g * 256 + 256]
                        qk_ev[(g, u)] = sAc.inc(
                            s.copy(dst, pqk[u % 2][:, 0:256]))
                    blk.scalar(a1_sc_u)

            emit_a1_head(0)
            emit_a1_head(1)
            for g in range(16):
                emit_a1_trx(2 * g)
                emit_a1_trx(2 * g + 1)
                if 2 * g + 2 < NT:
                    emit_a1_head(2 * g + 2)
                    emit_a1_head(2 * g + 3)
                if g >= 1:
                    emit_a1_vburst(g - 1)
                emit_a1_qkburst(g)
            emit_a1_vburst(15)

            a1_pe_last = sPe.n
            a1_ac_last = sAc.n
            a1_last_ev = sVe.n

        with ExitStack() as pa2:
            Ea = pa2.enter_context
            probsT = Ea(nc.sbuf_tensor("probsT", [128, 32, 512], BF16))
            qbuf = Ea(nc.sbuf_tensor("qbuf", [128, 2, KD, 512], BF16))
            xr = Ea(nc.sbuf_tensor("xr", [128, 2, D], F32))
            x2b = Ea(nc.sbuf_tensor("x2b", [128, 2, D], F32))
            ln2 = Ea(nc.sbuf_tensor("ln2", [128, 2, D], BF16))
            ln2Ts = Ea(nc.sbuf_tensor("ln2Ts", [128, 2, D], BF16))
            psc = [Ea(nc.psum_tensor("psc0", [128, 512], F32)),
                   Ea(nc.psum_tensor("psc1", [128, 512], F32)),
                   Ea(nc.psum_tensor("psc2", [128, 512], F32))]
            pctx = Ea(nc.psum_tensor("pctx", [128, 1536], F32))
            ptrA = Ea(nc.psum_tensor("ptrA", [128, 1024], BF16))
            prt = Ea(nc.psum_tensor("prt", [128, 512], F32))

            q_ld = [None] * 8
            sc_seq = []
            xr_ld = [None] * NT
            mask_done = {}
            ctx_done = [None] * NT
            ctx_ev = [None] * NT
            ln2_done = [None] * NT
            ln2T_tr = [None] * NT
            ln2T_cp = [None] * NT
            rt_done = [None] * NT
            gates_done = [None] * NT
            spill_done = [None] * NT
            x2_store = [None] * NT

            def emit_a2_tailA(t):
                def a2_pe_tr(pe, t=t):
                    pe.wait_ge(sGp.h, ln2_done[t])
                    if t >= 1:
                        pe.wait_ge(sVe.h, ln2T_cp[t - 1])
                    for kd in range(KD):
                        mm = pe.transpose(ptrA[:, kd * 128:(kd + 1) * 128],
                                          ln2[:, t % 2, kd * 128:(kd + 1) * 128],
                                          ident[:])
                    ln2T_tr[t] = sPe.inc(mm)
                blk.tensor(a2_pe_tr)

                def a2_ve3_t(v, t=t):
                    v.wait_ge(sPe.h, ln2T_tr[t])
                    if t >= 2:
                        v.wait_ge(sSl[t % 2].h, x2_store[t - 2])
                        v.wait_ge(sPe.h, rt_done[t - 2])
                    ln2T_cp[t] = sVe.inc(
                        v.tensor_copy(ln2Ts[:, t % 2, :], ptrA[:]))
                blk.vector(a2_ve3_t)

                def a2_gp_t(gp, t=t):
                    gc, sub = t // 2, t % 2
                    gp.wait_ge(sVe.h, ln2T_cp[t])
                    dst = ln2t_scr[gc].rearrange("p (kd tc) -> p kd tc", kd=KD)[
                        :, :, sub * 128:sub * 128 + 128]
                    src = ln2Ts[:, t % 2, :].rearrange("p (kd c) -> p kd c", kd=KD)
                    spill_done[t] = sSl[t % 2].inc(gp.dma_start(out=dst, in_=src), 16)
                    gp.wait_ge(sVe.h, V["st2_%d" % t])
                    x2_store[t] = sSl[t % 2].inc(gp.dma_start(
                        out=x2_scr[gc, :, sub * 1024:sub * 1024 + 1024],
                        in_=x2b[:, t % 2, :]), 16)
                blk.gpsimd(a2_gp_t)


            def emit_a2_tailB(t):
                def a2_pe_rt(pe, t=t):
                    pe.wait_ge(sVe.h, ln2T_cp[t])
                    if t >= 1:
                        pe.wait_ge(sVe.h, V["rtcp_%d" % (t - 1)])
                    for kd in range(KD):
                        mm = pe.matmul(prt[:, 0:16],
                                       ln2Ts[:, t % 2, kd * 128:(kd + 1) * 128],
                                       cw[:, kd * 16:kd * 16 + 16],
                                       start=(kd == 0), stop=(kd == KD - 1))
                    rt_done[t] = sPe.inc(mm)
                blk.tensor(a2_pe_rt)

                def a2_ve4_t(v, t=t):
                    v.wait_ge(sPe.h, rt_done[t])
                    v.wait_ge(sAc.h, V["rr_%d" % t])
                    V["rtcp_%d" % t] = sVe.inc(v.tensor_copy(rt16[:], prt[:, 0:16]))
                    v.wait_ge(sVe.h, V["rtcp_%d" % t])
                    rc = 16 + (t % 4)
                    w = sVe.inc(v.scalar_tensor_tensor(
                        logit[:], rt16[:, 0:8], stt[:, rc:rc + 1],
                        rt16[:, 8:16], ALU.mult, ALU.add))
                    v.wait_ge(sVe.h, w)
                    w = sVe.inc(v.max(top8[:], logit[:]))
                    v.wait_ge(sVe.h, w)
                    V["negm1_%d" % t] = sVe.inc(
                        v.tensor_scalar(stt[:, 12:13], top8[:, 0:1], -1.0,
                                        None, ALU.mult))
                blk.vector(a2_ve4_t)

                def a2_ac3_t(s, t=t):
                    s.wait_ge(sVe.h, V["negm1_%d" % t])
                    V["expl_%d" % t] = sAc.inc(
                        s.activation(expl[:], logit[:], AF.Exp, bias=stt[:, 12:13]))
                blk.scalar(a2_ac3_t)

                def a2_ve5_t(v, t=t):
                    v.wait_ge(sAc.h, V["expl_%d" % t])
                    w = sVe.inc(v.tensor_scalar(maskg[:], logit[:], top8[:, 1:2],
                                                None, ALU.is_ge))
                    v.wait_ge(sVe.h, w)
                    w = sVe.inc(v.tensor_tensor(gated[:], expl[:], maskg[:],
                                                ALU.mult))
                    v.wait_ge(sVe.h, w)
                    w = sVe.inc(v.tensor_scalar(expl[:], gated[:], 1.0, 0.0,
                                                ALU.mult, ALU.add,
                                                accum_out=stt[:, 13:14]))
                    v.wait_ge(sVe.h, w)
                    w = sVe.inc(v.reciprocal(stt[:, 14:15], stt[:, 13:14]))
                    v.wait_ge(sVe.h, w)
                    gates_done[t] = sVe.inc(
                        v.tensor_scalar(gates_tm[:, t * 8:(t + 1) * 8], gated[:],
                                        stt[:, 14:15], None, ALU.mult))
                blk.vector(a2_ve5_t)

            for g2 in range(8):
                JN = 4 * g2 + 4

                def a2_sync_q(sync, g2=g2):
                    if g2 <= 1:
                        sync.wait_ge(sAc.h, a1_ac_last)
                        sync.wait_ge(sPe.h, a1_pe_last)
                        sync.wait_ge(sVe.h, a1_last_ev)
                        sync.wait_ge(sSp.h, qspill[15])
                    else:
                        sync.wait_ge(sPe.h, V["sc_last_%d" % (g2 - 2)])
                    q_ld[g2] = sQb[g2 % 2].inc(sync.dma_start(
                        out=qbuf[:, g2 % 2, :, :], in_=qt_scr[g2].rearrange(
                            "p (m tc) -> p m tc", m=KD)), 16)
                blk.sync(a2_sync_q)

                for j in range(JN):
                    jj = len(sc_seq)
                    sc_seq.append((g2, j))

                    def a2_pe_sc(pe, g2=g2, j=j, JN=JN, jj=jj):
                        if j == 0:
                            pe.wait_ge(sQb[g2 % 2].h, q_ld[g2])
                        if jj >= 3:
                            pg, pj = sc_seq[jj - 3]
                            pe.wait_ge(sAc.h, V["exp_%d_%d" % (pg, pj)])
                        for kd in range(KD):
                            mm = pe.matmul(
                                psc[jj % 3][:],
                                arena[:, kd * 4096 + j * 128:kd * 4096 + j * 128 + 128],
                                qbuf[:, g2 % 2, kd, :],
                                start=(kd == 0), stop=(kd == KD - 1))
                        V["sc_%d_%d" % (g2, j)] = sPe.inc(mm)
                        if j == JN - 1:
                            V["sc_last_%d" % g2] = sPe.n
                    blk.tensor(a2_pe_sc)

                    def a2_ac_exp(s, g2=g2, j=j, jj=jj):
                        s.wait_ge(sPe.h, V["sc_%d_%d" % (g2, j)])
                        V["exp_%d_%d" % (g2, j)] = sAc.inc(
                            s.activation(probsT[:, j, :], psc[jj % 3][:],
                                         AF.Exp, scale=1.0 / 32.0))
                    blk.scalar(a2_ac_exp)

                def a2_ve_mask(v, g2=g2):
                    for j in range(4 * g2, 4 * g2 + 4):
                        off = (j - 4 * g2) * 128
                        v.wait_ge(sAc.h, V["exp_%d_%d" % (g2, j)])
                        mask_done[j] = sVe.inc(v.tensor_tensor(
                            probsT[:, j, off:off + 128],
                            probsT[:, j, off:off + 128],
                            cmask[:], ALU.mult))
                blk.vector(a2_ve_mask)

                for tl in range(4):
                    t = 4 * g2 + tl

                    def a2_sync_x(sync, t=t):
                        if t >= 2:
                            sync.wait_ge(sVe.h, V["st2_%d" % (t - 2)])
                        xr_ld[t] = sXr[t % 2].inc(sync.dma_start(
                            out=xr[:, t % 2, :],
                            in_=x_h[t * 128:(t + 1) * 128, :]), 16)
                    blk.sync(a2_sync_x)

                    def a2_pe_pv(pe, t=t, g2=g2, tl=tl):
                        jmax = 4 * g2 + tl
                        pe.wait_ge(sAc.h, V["exp_%d_%d" % (g2, jmax)])
                        pe.wait_ge(sVe.h, mask_done[jmax])
                        if t == 0:
                            pe.wait_ge(sMs.h, V["vones"])
                        if t >= 1:
                            pe.wait_ge(sVe.h, ctx_ev[t - 1])
                        for j in range(jmax + 1):
                            lhs = probsT[:, j, tl * 128:tl * 128 + 128]
                            base = VOFF + j * VSTRIDE
                            st, sp = (j == 0), (j == jmax)
                            pe.matmul(pctx[:, 0:512], lhs,
                                      arena[:, base:base + 512], start=st, stop=sp)
                            pe.matmul(pctx[:, 512:1024], lhs,
                                      arena[:, base + 512:base + 1024],
                                      start=st, stop=sp)
                            mm = pe.matmul(pctx[:, 1024:1025], lhs,
                                           arena[:, ONESCOL:ONESCOL + 1],
                                           start=st, stop=sp)
                        ctx_done[t] = sPe.inc(mm)
                    blk.tensor(a2_pe_pv)

                    def a2_ve_t(v, t=t):
                        v.wait_ge(sPe.h, ctx_done[t])
                        v.wait_ge(sXr[t % 2].h, xr_ld[t])
                        if t >= 2:
                            v.wait_ge(sSl[t % 2].h, x2_store[t - 2])
                            v.wait_ge(sGp.h, ln2_done[t - 2])
                        if t >= 1:
                            v.wait_ge(sAc.h, V["rstd2_%d" % (t - 1)])
                        w = sVe.inc(v.reciprocal(stt[:, 15:16], pctx[:, 1024:1025]))
                        v.wait_ge(sVe.h, w)
                        x2t = x2b[:, t % 2, :]
                        ctx_ev[t] = sVe.inc(
                            v.scalar_tensor_tensor(x2t, pctx[:, 0:1024], stt[:, 15:16],
                                                   xr[:, t % 2, :], ALU.mult, ALU.add))
                        v.wait_ge(sVe.h, ctx_ev[t])
                        v.reduce_sum(stt[:, 0:1], x2t, AX.X)
                        w = sVe.inc(v.scalar_tensor_tensor(
                            xr[:, t % 2, :], x2t, 1.0, x2t, ALU.bypass, ALU.mult,
                            accum_out=stt[:, 1:2]))
                        v.wait_ge(sVe.h, w)
                        w = sVe.inc(v.tensor_scalar(stt[:, 2:3], stt[:, 0:1],
                                                    1.0 / 1024.0, None, ALU.mult))
                        v.wait_ge(sVe.h, w)
                        w = sVe.inc(v.tensor_scalar(stt[:, 7:8], stt[:, 2:3],
                                                    stt[:, 2:3], -EPS, ALU.mult, ALU.add))
                        v.wait_ge(sVe.h, w)
                        V["st2_%d" % t] = sVe.inc(
                            v.tensor_scalar(stt[:, 3:4], stt[:, 1:2], 1.0 / 1024.0,
                                            stt[:, 7:8], ALU.mult, ALU.subtract))
                    blk.vector(a2_ve_t)

                    def a2_ac_t(s, t=t):
                        s.wait_ge(sVe.h, V["st2_%d" % t])
                        if t >= 1:
                            s.wait_ge(sGp.h, ln2_done[t - 1])
                        w = sAc.inc(s.activation(stt[:, 4:5], stt[:, 3:4], AF.Ln))
                        s.wait_ge(sAc.h, w)
                        V["rstd2_%d" % t] = sAc.inc(
                            s.activation(stt[:, 5:6], stt[:, 4:5], AF.Exp, scale=-0.5))
                    blk.scalar(a2_ac_t)

                    def a2_ve2_t(v, t=t):
                        v.wait_ge(sAc.h, V["rstd2_%d" % t])
                        V["negmr2_%d" % t] = sVe.inc(
                            v.tensor_scalar(stt[:, 6:7], stt[:, 2:3], -1.0,
                                            stt[:, 5:6], ALU.mult, ALU.mult))
                        w = sVe.inc(v.tensor_scalar(stt[:, 8:9], stt[:, 3:4], -EPS,
                                                    stt[:, 5:6], ALU.add, ALU.mult))
                        v.wait_ge(sVe.h, w)
                        w = sVe.inc(v.tensor_scalar(stt[:, 9:10], stt[:, 8:9],
                                                    stt[:, 5:6], 1024.0,
                                                    ALU.mult, ALU.mult))
                        v.wait_ge(sVe.h, w)
                        V["t4_%d" % t] = sVe.inc(
                            v.tensor_scalar(stt[:, 9:10], stt[:, 9:10], 1.0, 1e-24,
                                            ALU.mult, ALU.add))
                    blk.vector(a2_ve2_t)

                    def a2_ac2_t(s, t=t):
                        s.wait_ge(sVe.h, V["t4_%d" % t])
                        w = sAc.inc(s.activation(stt[:, 10:11], stt[:, 9:10], AF.Ln))
                        s.wait_ge(sAc.h, w)
                        rc = 16 + (t % 4)
                        V["rr_%d" % t] = sAc.inc(
                            s.activation(stt[:, rc:rc + 1], stt[:, 10:11], AF.Exp,
                                         scale=-0.5))
                    blk.scalar(a2_ac2_t)

                    if t >= 1:
                        emit_a2_tailA(t - 1)

                    def a2_gp_ln2(gp, t=t):
                        gp.wait_ge(sVe.h, V["negmr2_%d" % t])
                        if t >= 2:
                            gp.wait_ge(sPe.h, ln2T_tr[t - 2])
                        ln2_done[t] = sGp.inc(
                            gp.tensor_scalar(ln2[:, t % 2, :], x2b[:, t % 2, :],
                                             stt[:, 5:6], stt[:, 6:7],
                                             ALU.mult, ALU.add))
                    blk.gpsimd(a2_gp_ln2)

                    if t >= 2:
                        emit_a2_tailB(t - 2)

            emit_a2_tailA(NT - 1)

            def c_gp_w1e(gp):
                gp.wait_ge(sPe.h, V["sc_last_7"])
                w1r = w1s_h.rearrange("(kd p) h -> p kd h", p=128)
                a1v = arena[:, 0:32768].rearrange("p (kd h) -> p kd h", kd=KD)
                for part in range(4):
                    V["w1_%d" % part] = sW1p[part].inc(gp.dma_start(
                        out=a1v[:, :, part * 1024:(part + 1) * 1024],
                        in_=w1r[:, :, part * 1024:(part + 1) * 1024]), 16)
            blk.gpsimd(c_gp_w1e)

            emit_a2_tailB(NT - 2)
            emit_a2_tailB(NT - 1)

            def c_gp_w2l(gp):
                gp.wait_ge(sPe.h, ctx_done[NT - 1])
                w2r = w2s_h.rearrange("(hc p) d -> p hc d", p=128)
                a2v = arena[:, VOFF:VOFF + 32768].rearrange(
                    "p (hc d) -> p hc d", hc=32)
                for part in range(4):
                    sW2c.inc(gp.dma_start(
                        out=a2v[:, part * 8:(part + 1) * 8, :],
                        in_=w2r[:, part * 8:(part + 1) * 8, :]), 16)
            blk.gpsimd(c_gp_w2l)

            if debug_taps:
                gdump = nc.dram_tensor("gdump", [128, 256], BF16,
                                       kind="ExternalOutput").ap()
                cwdump = nc.dram_tensor("cwdump", [128, 128], BF16,
                                        kind="ExternalOutput").ap()
                lgdump = nc.dram_tensor("lgdump", [128, 16], F32,
                                        kind="ExternalOutput").ap()

                def dump_g(gp):
                    gp.wait_ge(sVe.h, gates_done[31])
                    sSp.inc(gp.dma_start(out=gdump, in_=gates_tm[:]), 16)
                    sSp.inc(gp.dma_start(out=cwdump, in_=cw[:]), 16)
                    sSp.inc(gp.dma_start(out=lgdump, in_=rt16[:]), 16)
                blk.gpsimd(dump_g)

            a2_pe_end = sPe.n
            a2_ac_end = sAc.n
            a2_ve_end = sVe.n

        with ExitStack() as pc:
            Ec = pc.enter_context
            gT2 = Ec(nc.sbuf_tensor("gT2", [NEXP, 2, 256], BF16))
            gbc = Ec(nc.sbuf_tensor("gbc", [128, 8, 256], BF16))
            ln2Tg = Ec(nc.sbuf_tensor("ln2Tg", [128, 2, KD, 256], BF16))
            x2g = Ec(nc.sbuf_tensor("x2g", [128, 2, 2048], F32))
            outsb = Ec(nc.sbuf_tensor("outsb", [128, 2, D], F32))
            hT = Ec(nc.sbuf_tensor("hT", [128, 2, 32, 256], BF16))
            b2s = Ec(nc.sbuf_tensor("b2_s", [NEXP, D], BF16))
            ph = [Ec(nc.psum_tensor("ph0", [128, 512], F32)),
                  Ec(nc.psum_tensor("ph1", [128, 512], F32))]
            pout = [Ec(nc.psum_tensor("pout0", [128, 1024], F32)),
                    Ec(nc.psum_tensor("pout1", [128, 1024], F32))]
            pbc = Ec(nc.psum_tensor("pbc", [128, 512], F32))
            ptG = Ec(nc.psum_tensor("ptG", [128, 1024], BF16))

            def c_gp_w(gp):
                gp.wait_ge(sPe.h, a2_pe_end)
                gp.wait_ge(sAc.h, a2_ac_end)
                gp.wait_ge(sVe.h, a2_ve_end)
                gp.wait_ge(sSl[0].h, sSl[0].n)
                gp.wait_ge(sSl[1].h, sSl[1].n)
                V["w2b2"] = sB2.inc(gp.dma_start(out=b2s[:], in_=b2_h), 16)
            blk.gpsimd(c_gp_w)

            def c_sync_gate(sync):
                sync.wait_ge(sAc.h, a2_ac_end)
                sync.wait_ge(sVe.h, a2_ve_end)
                sync.wait_ge(sPe.h, a2_pe_end)
            blk.sync(c_sync_gate)

            def _preload_gelu(sc):
                _load_act_set(nc, [AF.Gelu])
            blk.scalar(_preload_gelu)

            ln2g_ld = [None] * 16
            x2g_ld = [None] * 16
            w2_done = [None] * NT
            out_ev = [None] * NT
            out_st = [None] * NT

            def emit_c_gates(g):
                def c_pe_gT(pe, g=g):
                    pe.wait_ge(sVe.h, gates_done[2 * g + 1])
                    if g == 0:
                        pe.wait_ge(sVe.h, a2_ve_end)
                        pe.wait_ge(sAc.h, a2_ac_end)
                    if g >= 1:
                        pe.wait_ge(sVe.h, V["gT2cp_%d" % (g - 1)])
                    for tl in range(2):
                        mm = pe.transpose(ptG[0:8, tl * 128:tl * 128 + 128],
                                          gates_tm[:, (2 * g + tl) * 8:
                                                   (2 * g + tl + 1) * 8],
                                          ident[:])
                    V["gTtr_%d" % g] = sPe.inc(mm)
                blk.tensor(c_pe_gT)

                def c_ve_gT(v, g=g):
                    v.wait_ge(sPe.h, V["gTtr_%d" % g])
                    if g >= 2:
                        v.wait_ge(sPe.h, w2_done[2 * (g - 2) + 1])
                    V["gT2cp_%d" % g] = sVe.inc(
                        v.tensor_copy(gT2[:, g % 2, :], ptG[0:8, 0:256]))
                blk.vector(c_ve_gT)

            def emit_c_tail(gc):
                for tl in range(2):
                    t = 2 * gc + tl

                    def c_pe_w2(pe, gc=gc, tl=tl, t=t):
                        pe.wait_ge(sVe.h, V["gm_last_%d" % gc])
                        if gc == 0 and tl == 0:
                            pe.wait_ge(sW2c.h, sW2c.n)
                            pe.wait_ge(sB2.h, V["w2b2"])
                        if t >= 2:
                            pe.wait_ge(sVe.h, out_ev[t - 2])
                        for hc in range(32):
                            lhs = hT[:, gc % 2, hc, tl * 128:tl * 128 + 128]
                            b = VOFF + hc * 1024
                            pe.matmul(pout[tl][:, 0:512], lhs, arena[:, b:b + 512],
                                      start=(hc == 0), stop=False)
                            pe.matmul(pout[tl][:, 512:1024], lhs,
                                      arena[:, b + 512:b + 1024],
                                      start=(hc == 0), stop=False)
                        pe.matmul(pout[tl][:, 0:512],
                                  gT2[0:8, gc % 2, tl * 128:tl * 128 + 128],
                                  b2s[0:8, 0:512], start=False, stop=True)
                        mm = pe.matmul(pout[tl][:, 512:1024],
                                       gT2[0:8, gc % 2, tl * 128:tl * 128 + 128],
                                       b2s[0:8, 512:1024], start=False, stop=True)
                        w2_done[t] = sPe.inc(mm)
                    blk.tensor(c_pe_w2)

                    def c_ve_out(v, gc=gc, tl=tl, t=t):
                        v.wait_ge(sPe.h, w2_done[t])
                        v.wait_ge(sG[gc % 2].h, x2g_ld[gc])
                        if t >= 2:
                            v.wait_ge(sF[t % 2].h, out_st[t - 2])
                        out_ev[t] = sVe.inc(v.tensor_tensor(
                            outsb[:, tl, :], pout[tl][:],
                            x2g[:, gc % 2, tl * 1024:(tl + 1) * 1024], ALU.add))
                    blk.vector(c_ve_out)

                    def c_gp_out(gp, t=t, tl=tl):
                        gp.wait_ge(sVe.h, out_ev[t])
                        out_st[t] = sF[t % 2].inc(gp.dma_start(
                            out=out_h[t * 128:(t + 1) * 128, :],
                            in_=outsb[:, tl, :]), 16)
                    blk.gpsimd(c_gp_out)

            emit_c_gates(0)

            for gc in range(16):
                def c_sync_g(sync, gc=gc):
                    sync.wait_ge(sSl[0].h, x2_store[2 * gc])
                    sync.wait_ge(sSl[1].h, x2_store[2 * gc + 1])
                    if gc >= 2:
                        sync.wait_ge(sPe.h, V["h_last_%d" % (gc - 2)])
                        sync.wait_ge(sVe.h, out_ev[2 * (gc - 2) + 1])
                    ln2g_ld[gc] = sG[gc % 2].inc(sync.dma_start(
                        out=ln2Tg[:, gc % 2, :, :],
                        in_=ln2t_scr[gc].rearrange("p (kd tc) -> p kd tc", kd=KD)), 16)
                    x2g_ld[gc] = sG[gc % 2].inc(sync.dma_start(
                        out=x2g[:, gc % 2, :], in_=x2_scr[gc]), 16)
                blk.sync(c_sync_g)

                gbanks = [pbc, ph[0], ph[1], pbc]
                for pair in range(4):
                    def c_pe_gbc(pe, gc=gc, pair=pair):
                        if pair == 0:
                            pe.wait_ge(sVe.h, V["gT2cp_%d" % gc])
                        if pair == 3:
                            pe.wait_ge(sVe.h, V["gbc_cp_%d_0" % gc])
                        if gc >= 1:
                            if pair == 1:
                                pe.wait_ge(sAc.h, V["gelu_%d_%d" % (gc - 1, 30)])
                            elif pair == 2:
                                pe.wait_ge(sAc.h, V["gelu_%d_%d" % (gc - 1, 31)])
                        for e2 in range(2):
                            e = pair * 2 + e2
                            mm = pe.matmul(gbanks[pair][:, e2 * 256:(e2 + 1) * 256],
                                           sel[0:8, e * 128:(e + 1) * 128],
                                           gT2[0:8, gc % 2, :],
                                           start=True, stop=True)
                        V["gbc_mm_%d_%d" % (gc, pair)] = sPe.inc(mm)
                    blk.tensor(c_pe_gbc)

                    def c_ve_gbc(v, gc=gc, pair=pair):
                        v.wait_ge(sPe.h, V["gbc_mm_%d_%d" % (gc, pair)])
                        V["gbc_cp_%d_%d" % (gc, pair)] = sVe.inc(v.tensor_copy(
                            gbc[:, pair * 2:(pair + 1) * 2, :].rearrange(
                                "p e c -> p (e c)"),
                            gbanks[pair][:, 0:512]))
                    blk.vector(c_ve_gbc)

                for hc in range(32):
                    def c_pe_h(pe, gc=gc, hc=hc):
                        if hc == 0:
                            pe.wait_ge(sG[gc % 2].h, x2g_ld[gc])
                            pe.wait_ge(sVe.h, V["gbc_cp_%d_1" % gc])
                        if hc == 1:
                            pe.wait_ge(sVe.h, V["gbc_cp_%d_2" % gc])
                        if gc == 0:
                            pe.wait_ge(sW1p[hc // 8].h, V["w1_%d" % (hc // 8)])
                        if hc >= 2:
                            pe.wait_ge(sAc.h, V["gelu_%d_%d" % (gc, hc - 2)])
                        elif gc >= 1:
                            pe.wait_ge(sAc.h, V["gelu_%d_%d" % (gc - 1, 30 + hc)])
                        for kd in range(KD):
                            mm = pe.matmul(
                                ph[hc % 2][:, 0:256],
                                arena[:, kd * 4096 + hc * 128:
                                      kd * 4096 + hc * 128 + 128],
                                ln2Tg[:, gc % 2, kd, :],
                                start=(kd == 0), stop=(kd == KD - 1))
                        V["h_%d_%d" % (gc, hc)] = sPe.inc(mm)
                        if hc == 31:
                            V["h_last_%d" % gc] = sPe.n
                    blk.tensor(c_pe_h)

                    def c_ac_gelu(s, gc=gc, hc=hc):
                        if gc == 0 and hc == 0:
                            s.wait_ge(sSl[0].h, sSl[0].n)
                            s.wait_ge(sSl[1].h, sSl[1].n)
                            s.wait_ge(sVe.h, a2_ve_end)
                        if hc == 0 and gc >= 2:
                            s.wait_ge(sPe.h, w2_done[2 * (gc - 2) + 1])
                        s.wait_ge(sPe.h, V["h_%d_%d" % (gc, hc)])
                        V["gelu_%d_%d" % (gc, hc)] = sAc.inc(
                            s.activation(hT[:, gc % 2, hc, :], ph[hc % 2][:, 0:256],
                                         AF.Gelu, bias=b1s[:, hc:hc + 1]))
                    blk.scalar(c_ac_gelu)

                    def c_ve_gm(v, gc=gc, hc=hc):
                        v.wait_ge(sAc.h, V["gelu_%d_%d" % (gc, hc)])
                        if hc == 0:
                            v.wait_ge(sVe.h, V["gbc_cp_%d_3" % gc])
                        e = hc // 4
                        V["gm_%d_%d" % (gc, hc)] = sVe.inc(v.tensor_tensor(
                            hT[:, gc % 2, hc, :], hT[:, gc % 2, hc, :],
                            gbc[:, e, :], ALU.mult))
                        if hc == 31:
                            V["gm_last_%d" % gc] = sVe.n
                    blk.vector(c_ve_gm)

                if gc >= 1:
                    emit_c_tail(gc - 1)

                if gc + 1 < 16:
                    emit_c_gates(gc + 1)

            emit_c_tail(15)

            def fin_sync(sync):
                sync.wait_ge(sF[0].h, sF[0].n)
                sync.wait_ge(sF[1].h, sF[1].n)
            blk.sync(fin_sync)

            def fin_gp(gp):
                gp.wait_ge(sF[0].h, sF[0].n)
                gp.wait_ge(sF[1].h, sF[1].n)
            blk.gpsimd(fin_gp)

    return nc


def _host_consts():
    import ml_dtypes
    ident = np.eye(128, dtype=ml_dtypes.bfloat16)
    cmask = np.triu(np.ones((128, 128), dtype=np.float32)).astype(ml_dtypes.bfloat16)
    consts = np.array([CONSTS], dtype=np.float32)
    selm = np.zeros((NEXP, NEXP * 128), dtype=np.float32)
    for e in range(NEXP):
        selm[e, e * 128:(e + 1) * 128] = 1.0
    selm = selm.astype(ml_dtypes.bfloat16)
    return ident, cmask, consts, selm


def kernel(L0_attn_x, L0_attn_Wq, L0_attn_Wk, L0_attn_Wv,
           L0_ffn_w1, L0_ffn_b1, L0_ffn_w2, L0_ffn_b2,
           L0_ffn_centroids, L0_ffn_w_route):
    x = np.ascontiguousarray(np.asarray(L0_attn_x, dtype=np.float32))
    wqT = np.ascontiguousarray(np.asarray(L0_attn_Wq, dtype=np.float32).T)
    wkT = np.ascontiguousarray(np.asarray(L0_attn_Wk, dtype=np.float32).T)
    wvT = np.ascontiguousarray(np.asarray(L0_attn_Wv, dtype=np.float32).T)
    w1s = np.ascontiguousarray(
        np.asarray(L0_ffn_w1, dtype=np.float32).transpose(1, 0, 2).reshape(D, DFF))
    w2s = np.ascontiguousarray(np.asarray(L0_ffn_w2, dtype=np.float32).reshape(DFF, D))
    b1m = np.ascontiguousarray(
        np.asarray(L0_ffn_b1, dtype=np.float32).reshape(DFF).reshape(32, 128).T)
    b2 = np.ascontiguousarray(np.asarray(L0_ffn_b2, dtype=np.float32))
    cent = np.ascontiguousarray(np.asarray(L0_ffn_centroids, dtype=np.float32))
    wrt = np.ascontiguousarray(np.asarray(L0_ffn_w_route, dtype=np.float32))
    ident, cmask, consts, selm = _host_consts()

    nc = build(debug_taps=False)
    shared = dict(wqT=wqT, wkT=wkT, wvT=wvT, w1s=w1s, w2s=w2s, b1m=b1m, b2=b2,
                  cent=cent, wrt=wrt, ident=ident, cmask=cmask,
                  consts=consts, sel=selm)
    in_maps = [dict(shared, x=np.ascontiguousarray(x[b])) for b in range(8)]
    res = run_bass_kernel_spmd(nc, in_maps, core_ids=list(range(8)))
    return np.stack([r["out"] for r in res.results], axis=0)


# version 18
# speedup vs baseline: 1.1269x; 1.1269x over previous
import numpy as np
from contextlib import ExitStack

import concourse.bass as bass
import concourse.mybir as mybir
from concourse.bass_utils import run_bass_kernel_spmd
from concourse.hw_specs import get_activation_tables


def _load_act_set(nc, funcs):
    for idx, fs in enumerate(get_activation_tables(nc.m.arch).values()):
        if all(f in fs for f in funcs):
            nc.scalar.add_instruction(mybir.InstLoadActFuncSet(
                name=nc.get_next_instruction_name(),
                act_func_set_id=idx, ins=[], outs=[]))
            return
    raise ValueError(f"no table set contains {funcs}")

F32 = mybir.dt.float32
BF16 = mybir.dt.bfloat16
AF = mybir.ActivationFunctionType
ALU = mybir.AluOpType
AX = mybir.AxisListType

NTOK = 4096
D = 1024
NT = 32
KD = 8
DFF = 4096
NEXP = 8
EPS = 1e-5

CONSTS = [-0.5, EPS, 1024.0, 1.0 / 32.0, -1.0, 1.0 / 1024.0, 1e-24, -EPS]

VOFF = 32768
VSTRIDE = 1024
ONESCOL = 65536


def build(debug_taps=False):
    nc = bass.Bass("TRN2", target_bir_lowering=False)

    x_h = nc.dram_tensor("x", [NTOK, D], F32, kind="ExternalInput").ap()
    wqT_h = nc.dram_tensor("wqT", [D, D], F32, kind="ExternalInput").ap()
    wkT_h = nc.dram_tensor("wkT", [D, D], F32, kind="ExternalInput").ap()
    wvT_h = nc.dram_tensor("wvT", [D, D], F32, kind="ExternalInput").ap()
    w1s_h = nc.dram_tensor("w1s", [D, DFF], F32, kind="ExternalInput").ap()
    w2s_h = nc.dram_tensor("w2s", [DFF, D], F32, kind="ExternalInput").ap()
    b1m_h = nc.dram_tensor("b1m", [128, 32], F32, kind="ExternalInput").ap()
    b2_h = nc.dram_tensor("b2", [NEXP, D], F32, kind="ExternalInput").ap()
    cent_h = nc.dram_tensor("cent", [NEXP, D], F32, kind="ExternalInput").ap()
    wrt_h = nc.dram_tensor("wrt", [NEXP, D], F32, kind="ExternalInput").ap()
    ident_h = nc.dram_tensor("ident", [128, 128], BF16, kind="ExternalInput").ap()
    sel_h = nc.dram_tensor("sel", [NEXP, NEXP * 128], BF16, kind="ExternalInput").ap()
    cmask_h = nc.dram_tensor("cmask", [128, 128], BF16, kind="ExternalInput").ap()
    consts_h = nc.dram_tensor("consts", [1, len(CONSTS)], F32, kind="ExternalInput").ap()

    skind = "ExternalOutput" if debug_taps else "Internal"
    qt_scr = nc.dram_tensor("qt_scr", [8, 128, 4096], BF16, kind=skind).ap()
    ln2t_scr = nc.dram_tensor("ln2t_scr", [16, 128, 2048], BF16, kind=skind).ap()
    x2_scr = nc.dram_tensor("x2_scr", [16, 128, 2048], F32, kind=skind).ap()

    out_h = nc.dram_tensor("out", [NTOK, D], F32, kind="ExternalOutput").ap()

    V = {}

    with ExitStack() as topctx:
        E = topctx.enter_context

        arena = E(nc.sbuf_tensor("arena", [128, 65544], BF16))
        gates_tm = E(nc.sbuf_tensor("gates_tm", [128, 256], BF16))
        ident = E(nc.sbuf_tensor("ident_s", [128, 128], BF16))
        cmask = E(nc.sbuf_tensor("cmask_s", [128, 128], BF16))
        cw = E(nc.sbuf_tensor("cw_s", [128, 128], BF16))
        sel = E(nc.sbuf_tensor("sel_s", [NEXP, NEXP * 128], BF16))
        b1s = E(nc.sbuf_tensor("b1_s", [128, 32], F32))
        consts = E(nc.sbuf_tensor("consts_s", [1, len(CONSTS)], F32))
        stt = E(nc.sbuf_tensor("stt", [128, 20], F32))
        rt16 = E(nc.sbuf_tensor("rt16", [128, 16], F32))
        logit = E(nc.sbuf_tensor("logit", [128, 8], F32))
        top8 = E(nc.sbuf_tensor("top8", [128, 8], F32))
        expl = E(nc.sbuf_tensor("expl", [128, 8], F32))
        maskg = E(nc.sbuf_tensor("maskg", [128, 8], F32))
        gated = E(nc.sbuf_tensor("gated", [128, 8], F32))
        cstat = E(nc.sbuf_tensor("cstat", [128, 4], F32))

        class Ctr:
            def __init__(self, name):
                self.h = E(nc.semaphore(name))
                self.n = 0

            def inc(self, instr, k=1):
                instr.then_inc(self.h, k)
                self.n += k
                return self.n

        sIn = Ctr("sIn")
        sWt = [Ctr("sWt%d" % i) for i in range(3)]
        sSp = Ctr("sSp")
        sPe = Ctr("sPe")
        sAc = Ctr("sAc")
        sVe = Ctr("sVe")
        sX = [Ctr("sX0"), Ctr("sX1")]
        sQb = [Ctr("sQ0"), Ctr("sQ1")]
        sXr = [Ctr("sXr0"), Ctr("sXr1")]
        sSl = [Ctr("sSl0"), Ctr("sSl1")]
        sGl = [Ctr("sGl0"), Ctr("sGl1")]
        sGx = [Ctr("sGx0"), Ctr("sGx1")]
        sW1p = [Ctr("sW1p%d" % i) for i in range(4)]
        sW2c = Ctr("sW2c")
        sB2 = Ctr("sB2")
        sSel = Ctr("sSel")
        sF = [Ctr("sF0"), Ctr("sF1")]
        sMs = Ctr("sMs")
        sGp = Ctr("sGp")

        blk = E(nc.Block())

        def _preload_lnexp(sc):
            _load_act_set(nc, [AF.Ln, AF.Exp, AF.Identity])
        blk.scalar(_preload_lnexp)

        with ExitStack() as pset:
            Es = pset.enter_context
            centf = Es(nc.sbuf_tensor("centf", [NEXP, D], F32))
            wrtf = Es(nc.sbuf_tensor("wrtf", [NEXP, D], F32))
            cscr = Es(nc.sbuf_tensor("cscr", [NEXP, D], F32))
            cnormbf = Es(nc.sbuf_tensor("cnormbf", [NEXP, D], BF16))
            wrtbf = Es(nc.sbuf_tensor("wrtbf", [NEXP, D], BF16))
            ptmp = Es(nc.psum_tensor("ptmp", [128, 1024], BF16))

            def setup_sync(sync):
                for dst, src in [(ident, ident_h), (cmask, cmask_h),
                                 (b1s, b1m_h), (consts, consts_h),
                                 (centf, cent_h), (wrtf, wrt_h),
                                 (sel, sel_h)]:
                    sIn.inc(sync.dma_start(out=dst[:], in_=src), 16)
                V["setup_in"] = sIn.n
            blk.sync(setup_sync)

            for i, cv in enumerate(CONSTS):
                nc.const_aps.register(F32, cv, consts[0:1, i:i + 1])

            def setup_vec(v):
                v.wait_ge(sIn.h, V["setup_in"])
                w = sVe.inc(v.scalar_tensor_tensor(
                    cscr[:], centf[:], 1.0, centf[:],
                    ALU.bypass, ALU.mult, accum_out=cstat[0:8, 0:1]))
                v.wait_ge(sVe.h, w)
                V["cstat"] = sVe.inc(v.tensor_scalar(
                    cstat[0:8, 3:4], cstat[0:8, 0:1], 1e-24, None, ALU.add))
            blk.vector(setup_vec)

            def setup_scal(s):
                s.wait_ge(sVe.h, V["cstat"])
                w = sAc.inc(s.activation(cstat[0:8, 1:2], cstat[0:8, 3:4], AF.Ln))
                s.wait_ge(sAc.h, w)
                V["c_rc"] = sAc.inc(
                    s.activation(cstat[0:8, 2:3], cstat[0:8, 1:2], AF.Exp, scale=-0.5))
            blk.scalar(setup_scal)

            def setup_vec2(v):
                v.wait_ge(sAc.h, V["c_rc"])
                v.tensor_scalar(cnormbf[:], centf[:], cstat[0:8, 2:3], None, ALU.mult)
                V["cnorm"] = sVe.inc(v.tensor_copy(wrtbf[:], wrtf[:]))
            blk.vector(setup_vec2)

            def cw_pe(pe):
                pe.wait_ge(sVe.h, V["cnorm"])
                for kd in range(KD):
                    pe.transpose(ptmp[:, kd * 16:kd * 16 + 8],
                                 cnormbf[0:8, kd * 128:(kd + 1) * 128], ident[0:8, 0:8])
                    mm = pe.transpose(ptmp[:, kd * 16 + 8:kd * 16 + 16],
                                      wrtbf[0:8, kd * 128:(kd + 1) * 128], ident[0:8, 0:8])
                V["cw_tr"] = sPe.inc(mm)
            blk.tensor(cw_pe)

            def cw_ve(v):
                v.wait_ge(sPe.h, V["cw_tr"])
                V["cw"] = sVe.inc(v.tensor_copy(cw[:], ptmp[:, 0:128]))
            blk.vector(cw_ve)

            if debug_taps:
                cw0dump = nc.dram_tensor("cw0dump", [128, 128], BF16,
                                         kind="ExternalOutput").ap()
                cn0dump = nc.dram_tensor("cn0dump", [NEXP, D], BF16,
                                         kind="ExternalOutput").ap()

                def dump_cw0(gp):
                    gp.wait_ge(sVe.h, V["cw"])
                    sSp.inc(gp.dma_start(out=cw0dump, in_=cw[:]), 16)
                    sSp.inc(gp.dma_start(out=cn0dump, in_=cnormbf[:]), 16)
                    V["cw0d"] = sSp.n
                blk.gpsimd(dump_cw0)

        with ExitStack() as pa1:
            Ea = pa1.enter_context
            wqkv = Ea(nc.sbuf_tensor("wqkv", [128, 3, KD, D], BF16))
            scr = Ea(nc.sbuf_tensor("scr", [128, D], F32))
            xbuf = Ea(nc.sbuf_tensor("xbuf", [128, 2, D], F32))
            xln = Ea(nc.sbuf_tensor("xln", [128, 2, D], BF16))
            xlnT = Ea(nc.sbuf_tensor("xlnT", [128, 2, KD, 256], BF16))
            qstage = Ea(nc.sbuf_tensor("qstage", [128, KD, 256], BF16))
            pqk = [Ea(nc.psum_tensor("pqk0", [128, 512], F32)),
                   Ea(nc.psum_tensor("pqk1", [128, 512], F32))]
            pv = [Ea(nc.psum_tensor("pv0", [128, 1024], F32)),
                  Ea(nc.psum_tensor("pv1", [128, 1024], F32))]
            ptr = [Ea(nc.psum_tensor("ptr0", [128, 1024], BF16)),
                   Ea(nc.psum_tensor("ptr1", [128, 1024], BF16))]

            def a1_gpsimd(g):
                g.wait_ge(sVe.h, V["cw"])
                for i, wh in enumerate([wqT_h, wkT_h, wvT_h]):
                    V["wqkv%d" % i] = sWt[i].inc(g.dma_start(
                        out=wqkv[:, i, :, :],
                        in_=wh.rearrange("(kd p) m -> p kd m", p=128)), 16)
                V["vones"] = sMs.inc(
                    g.memset(arena[:, ONESCOL:ONESCOL + 1], 1.0))
            blk.gpsimd(a1_gpsimd)

            x_ld = [None] * NT
            xln_done = [None] * NT
            sq_done = [None] * NT
            stats_done = [None] * NT
            negmr_done = [None] * NT
            tr_done = [None] * NT
            xlt_done = [None] * NT
            qk_done = {}
            qk_ev = {}
            v_done = [None] * NT
            v_ev = [None] * NT
            qspill = [None] * 16

            def emit_a1_head(t):
                def a1_sync_t(sync, t=t):
                    if t == 0:
                        sync.wait_ge(sVe.h, V["cw"])
                    if t >= 2:
                        sync.wait_ge(sAc.h, xln_done[t - 2])
                    x_ld[t] = sX[t % 2].inc(sync.dma_start(
                        out=xbuf[:, t % 2, :],
                        in_=x_h[t * 128:(t + 1) * 128, :]), 16)
                blk.sync(a1_sync_t)

                def a1_vec_t(v, t=t):
                    v.wait_ge(sX[t % 2].h, x_ld[t])
                    if t >= 1:
                        v.wait_ge(sAc.h, V["rstd%d" % (t - 1)])
                    xb = xbuf[:, t % 2, :]
                    v.reduce_sum(stt[:, 0:1], xb, AX.X)
                    w = sVe.inc(v.scalar_tensor_tensor(
                        scr[:], xb, 1.0, xb, ALU.bypass, ALU.mult,
                        accum_out=stt[:, 1:2]))
                    v.wait_ge(sVe.h, w)
                    w = sVe.inc(v.tensor_scalar(stt[:, 2:3], stt[:, 0:1], 1.0 / 1024.0,
                                                None, ALU.mult))
                    v.wait_ge(sVe.h, w)
                    w = sVe.inc(v.tensor_scalar(stt[:, 7:8], stt[:, 2:3], stt[:, 2:3],
                                                -EPS, ALU.mult, ALU.add))
                    v.wait_ge(sVe.h, w)
                    stats_done[t] = sVe.inc(
                        v.tensor_scalar(stt[:, 3:4], stt[:, 1:2], 1.0 / 1024.0,
                                        stt[:, 7:8], ALU.mult, ALU.subtract))
                blk.vector(a1_vec_t)

                def a1_scal_t(s, t=t):
                    s.wait_ge(sVe.h, stats_done[t])
                    w = sAc.inc(s.activation(stt[:, 4:5], stt[:, 3:4], AF.Ln))
                    s.wait_ge(sAc.h, w)
                    V["rstd%d" % t] = sAc.inc(
                        s.activation(stt[:, 5:6], stt[:, 4:5], AF.Exp, scale=-0.5))
                blk.scalar(a1_scal_t)

                def a1_vec2_t(v, t=t):
                    v.wait_ge(sAc.h, V["rstd%d" % t])
                    negmr_done[t] = sVe.inc(
                        v.tensor_scalar(stt[:, 6:7], stt[:, 2:3], -1.0,
                                        stt[:, 5:6], ALU.mult, ALU.mult))
                blk.vector(a1_vec2_t)

                def a1_scal2_t(s, t=t):
                    s.wait_ge(sVe.h, negmr_done[t])
                    if t >= 2:
                        s.wait_ge(sPe.h, tr_done[t - 2])
                    xln_done[t] = sAc.inc(
                        s.activation(xln[:, t % 2, :], xbuf[:, t % 2, :],
                                     AF.Identity,
                                     bias=stt[:, 6:7], scale=stt[:, 5:6]))
                blk.scalar(a1_scal2_t)

            def emit_a1_trx(t):
                def a1_pe_t(pe, t=t):
                    pe.wait_ge(sAc.h, xln_done[t])
                    if t >= 2:
                        pe.wait_ge(sVe.h, xlt_done[t - 2])
                    for kd in range(KD):
                        mm = pe.transpose(ptr[t % 2][:, kd * 128:(kd + 1) * 128],
                                          xln[:, t % 2, kd * 128:(kd + 1) * 128],
                                          ident[:])
                    tr_done[t] = sPe.inc(mm)
                blk.tensor(a1_pe_t)

                def a1_vec3_t(v, t=t):
                    v.wait_ge(sPe.h, tr_done[t])
                    if t // 2 >= 2:
                        v.wait_ge(sPe.h, v_done[2 * (t // 2 - 2) + 1])
                    xlt_done[t] = sVe.inc(v.tensor_copy(
                        xlnT[:, (t // 2) % 2, :, (t % 2) * 128:(t % 2) * 128 + 128],
                        ptr[t % 2][:].rearrange("p (kd c) -> p kd c", kd=KD)))
                blk.vector(a1_vec3_t)

            def emit_a1_vburst(g):
                for tl in range(2):
                    def a1_pe_v(pe, g=g, tl=tl):
                        if g == 0 and tl == 0:
                            pe.wait_ge(sWt[2].h, V["wqkv2"])
                        if g >= 1:
                            pe.wait_ge(sAc.h, v_ev[2 * (g - 1) + tl])
                        for kd in range(KD):
                            for half in range(2):
                                mm = pe.matmul(
                                    pv[tl][:, half * 512:(half + 1) * 512],
                                    xlnT[:, g % 2, kd, tl * 128:(tl + 1) * 128],
                                    wqkv[:, 2, kd, half * 512:(half + 1) * 512],
                                    start=(kd == 0), stop=(kd == KD - 1))
                        v_done[2 * g + tl] = sPe.inc(mm)
                    blk.tensor(a1_pe_v)

                    def a1_sc_v(s, g=g, tl=tl):
                        j = 2 * g + tl
                        s.wait_ge(sPe.h, v_done[j])
                        base = VOFF + j * VSTRIDE
                        v_ev[j] = sAc.inc(
                            s.copy(arena[:, base:base + 1024], pv[tl][:]))
                    blk.scalar(a1_sc_v)

                def a1_gp_g(gp, g=g):
                    gq, sub = g // 2, g % 2
                    gp.wait_ge(sAc.h, qk_ev[(g, 7)])
                    dst = qt_scr[gq].rearrange("p (m tc) -> p m tc", m=KD)[
                        :, :, sub * 256:sub * 256 + 256]
                    qspill[g] = sSp.inc(
                        gp.dma_start(out=dst, in_=qstage[:]), 16)
                blk.gpsimd(a1_gp_g)

            def emit_a1_qkburst(g):
                for u in range(16):
                    proj, m = u // 8, u % 8

                    def a1_pe_u(pe, g=g, u=u, proj=proj, m=m):
                        if u == 0:
                            pe.wait_ge(sVe.h, xlt_done[2 * g + 1])
                        if g == 0 and u % 8 == 0:
                            pe.wait_ge(sWt[proj].h, V["wqkv%d" % proj])
                        prev = qk_ev.get((g, u - 2)) if u >= 2 else (
                            qk_ev.get((g - 1, 14 + u)) if g >= 1 else None)
                        if prev is not None:
                            pe.wait_ge(sAc.h, prev)
                        for kd in range(KD):
                            mm = pe.matmul(
                                pqk[u % 2][:, 0:256],
                                wqkv[:, proj, kd, m * 128:(m + 1) * 128],
                                xlnT[:, g % 2, kd, :],
                                start=(kd == 0), stop=(kd == KD - 1))
                        qk_done[(g, u)] = sPe.inc(mm)
                    blk.tensor(a1_pe_u)

                    def a1_sc_u(s, g=g, u=u, proj=proj, m=m):
                        s.wait_ge(sPe.h, qk_done[(g, u)])
                        if u == 0 and g >= 1:
                            s.wait_ge(sSp.h, qspill[g - 1])
                        if proj == 0:
                            dst = qstage[:, m, :]
                        else:
                            dst = arena[:, m * 4096 + g * 256:
                                        m * 4096 + g * 256 + 256]
                        qk_ev[(g, u)] = sAc.inc(
                            s.copy(dst, pqk[u % 2][:, 0:256]))
                    blk.scalar(a1_sc_u)

            emit_a1_head(0)
            emit_a1_head(1)
            for g in range(16):
                emit_a1_trx(2 * g)
                emit_a1_trx(2 * g + 1)
                if 2 * g + 2 < NT:
                    emit_a1_head(2 * g + 2)
                    emit_a1_head(2 * g + 3)
                if g >= 1:
                    emit_a1_vburst(g - 1)
                emit_a1_qkburst(g)
            emit_a1_vburst(15)

            a1_pe_last = sPe.n
            a1_ac_last = sAc.n
            a1_last_ev = sVe.n

        with ExitStack() as pa2:
            Ea = pa2.enter_context
            probsT = Ea(nc.sbuf_tensor("probsT", [128, 32, 512], BF16))
            qbuf = Ea(nc.sbuf_tensor("qbuf", [128, 2, KD, 512], BF16))
            xr = Ea(nc.sbuf_tensor("xr", [128, 2, D], F32))
            x2b = Ea(nc.sbuf_tensor("x2b", [128, 2, D], F32))
            ln2 = Ea(nc.sbuf_tensor("ln2", [128, 2, D], BF16))
            ln2Ts = Ea(nc.sbuf_tensor("ln2Ts", [128, 2, D], BF16))
            psc = [Ea(nc.psum_tensor("psc0", [128, 512], F32)),
                   Ea(nc.psum_tensor("psc1", [128, 512], F32)),
                   Ea(nc.psum_tensor("psc2", [128, 512], F32))]
            pctx = Ea(nc.psum_tensor("pctx", [128, 1536], F32))
            ptrA = Ea(nc.psum_tensor("ptrA", [128, 1024], BF16))
            prt = Ea(nc.psum_tensor("prt", [128, 512], F32))

            q_ld = [None] * 8
            sc_seq = []
            xr_ld = [None] * NT
            mask_done = {}
            ctx_done = [None] * NT
            ctx_ev = [None] * NT
            ln2_done = [None] * NT
            ln2T_tr = [None] * NT
            ln2T_cp = [None] * NT
            rt_done = [None] * NT
            gates_done = [None] * NT
            spill_done = [None] * NT
            x2_store = [None] * NT

            def emit_a2_tailA(t):
                def a2_pe_tr(pe, t=t):
                    pe.wait_ge(sGp.h, ln2_done[t])
                    if t >= 1:
                        pe.wait_ge(sVe.h, ln2T_cp[t - 1])
                    for kd in range(KD):
                        mm = pe.transpose(ptrA[:, kd * 128:(kd + 1) * 128],
                                          ln2[:, t % 2, kd * 128:(kd + 1) * 128],
                                          ident[:])
                    ln2T_tr[t] = sPe.inc(mm)
                blk.tensor(a2_pe_tr)

                def a2_ve3_t(v, t=t):
                    v.wait_ge(sPe.h, ln2T_tr[t])
                    if t >= 2:
                        v.wait_ge(sSl[t % 2].h, x2_store[t - 2])
                        v.wait_ge(sPe.h, rt_done[t - 2])
                    ln2T_cp[t] = sVe.inc(
                        v.tensor_copy(ln2Ts[:, t % 2, :], ptrA[:]))
                blk.vector(a2_ve3_t)

                def a2_gp_t(gp, t=t):
                    gc, sub = t // 2, t % 2
                    gp.wait_ge(sVe.h, ln2T_cp[t])
                    dst = ln2t_scr[gc].rearrange("p (kd tc) -> p kd tc", kd=KD)[
                        :, :, sub * 128:sub * 128 + 128]
                    src = ln2Ts[:, t % 2, :].rearrange("p (kd c) -> p kd c", kd=KD)
                    spill_done[t] = sSl[t % 2].inc(gp.dma_start(out=dst, in_=src), 16)
                    gp.wait_ge(sVe.h, V["st2_%d" % t])
                    x2_store[t] = sSl[t % 2].inc(gp.dma_start(
                        out=x2_scr[gc, :, sub * 1024:sub * 1024 + 1024],
                        in_=x2b[:, t % 2, :]), 16)
                blk.gpsimd(a2_gp_t)


            def emit_a2_tailB(t):
                def a2_pe_rt(pe, t=t):
                    pe.wait_ge(sVe.h, ln2T_cp[t])
                    if t >= 1:
                        pe.wait_ge(sVe.h, V["rtcp_%d" % (t - 1)])
                    for kd in range(KD):
                        mm = pe.matmul(prt[:, 0:16],
                                       ln2Ts[:, t % 2, kd * 128:(kd + 1) * 128],
                                       cw[:, kd * 16:kd * 16 + 16],
                                       start=(kd == 0), stop=(kd == KD - 1))
                    rt_done[t] = sPe.inc(mm)
                blk.tensor(a2_pe_rt)

                def a2_ve4_t(v, t=t):
                    v.wait_ge(sPe.h, rt_done[t])
                    v.wait_ge(sAc.h, V["rr_%d" % t])
                    V["rtcp_%d" % t] = sVe.inc(v.tensor_copy(rt16[:], prt[:, 0:16]))
                    v.wait_ge(sVe.h, V["rtcp_%d" % t])
                    rc = 16 + (t % 4)
                    w = sVe.inc(v.scalar_tensor_tensor(
                        logit[:], rt16[:, 0:8], stt[:, rc:rc + 1],
                        rt16[:, 8:16], ALU.mult, ALU.add))
                    v.wait_ge(sVe.h, w)
                    w = sVe.inc(v.max(top8[:], logit[:]))
                    v.wait_ge(sVe.h, w)
                    V["negm1_%d" % t] = sVe.inc(
                        v.tensor_scalar(stt[:, 12:13], top8[:, 0:1], -1.0,
                                        None, ALU.mult))
                blk.vector(a2_ve4_t)

                def a2_ac3_t(s, t=t):
                    s.wait_ge(sVe.h, V["negm1_%d" % t])
                    V["expl_%d" % t] = sAc.inc(
                        s.activation(expl[:], logit[:], AF.Exp, bias=stt[:, 12:13]))
                blk.scalar(a2_ac3_t)

                def a2_ve5_t(v, t=t):
                    v.wait_ge(sAc.h, V["expl_%d" % t])
                    w = sVe.inc(v.tensor_scalar(maskg[:], logit[:], top8[:, 1:2],
                                                None, ALU.is_ge))
                    v.wait_ge(sVe.h, w)
                    w = sVe.inc(v.tensor_tensor(gated[:], expl[:], maskg[:],
                                                ALU.mult))
                    v.wait_ge(sVe.h, w)
                    w = sVe.inc(v.tensor_scalar(expl[:], gated[:], 1.0, 0.0,
                                                ALU.mult, ALU.add,
                                                accum_out=stt[:, 13:14]))
                    v.wait_ge(sVe.h, w)
                    w = sVe.inc(v.reciprocal(stt[:, 14:15], stt[:, 13:14]))
                    v.wait_ge(sVe.h, w)
                    gates_done[t] = sVe.inc(
                        v.tensor_scalar(gates_tm[:, t * 8:(t + 1) * 8], gated[:],
                                        stt[:, 14:15], None, ALU.mult))
                blk.vector(a2_ve5_t)

            for g2 in range(8):
                JN = 4 * g2 + 4

                def a2_sync_q(sync, g2=g2):
                    if g2 <= 1:
                        sync.wait_ge(sAc.h, a1_ac_last)
                        sync.wait_ge(sPe.h, a1_pe_last)
                        sync.wait_ge(sVe.h, a1_last_ev)
                        sync.wait_ge(sSp.h, qspill[15])
                    else:
                        sync.wait_ge(sPe.h, V["sc_last_%d" % (g2 - 2)])
                    q_ld[g2] = sQb[g2 % 2].inc(sync.dma_start(
                        out=qbuf[:, g2 % 2, :, :], in_=qt_scr[g2].rearrange(
                            "p (m tc) -> p m tc", m=KD)), 16)
                blk.sync(a2_sync_q)

                for j in range(JN):
                    jj = len(sc_seq)
                    sc_seq.append((g2, j))

                    def a2_pe_sc(pe, g2=g2, j=j, JN=JN, jj=jj):
                        if j == 0:
                            pe.wait_ge(sQb[g2 % 2].h, q_ld[g2])
                        if jj >= 3:
                            pg, pj = sc_seq[jj - 3]
                            pe.wait_ge(sAc.h, V["exp_%d_%d" % (pg, pj)])
                        for kd in range(KD):
                            mm = pe.matmul(
                                psc[jj % 3][:],
                                arena[:, kd * 4096 + j * 128:kd * 4096 + j * 128 + 128],
                                qbuf[:, g2 % 2, kd, :],
                                start=(kd == 0), stop=(kd == KD - 1))
                        V["sc_%d_%d" % (g2, j)] = sPe.inc(mm)
                        if j == JN - 1:
                            V["sc_last_%d" % g2] = sPe.n
                    blk.tensor(a2_pe_sc)

                    def a2_ac_exp(s, g2=g2, j=j, jj=jj):
                        s.wait_ge(sPe.h, V["sc_%d_%d" % (g2, j)])
                        V["exp_%d_%d" % (g2, j)] = sAc.inc(
                            s.activation(probsT[:, j, :], psc[jj % 3][:],
                                         AF.Exp, scale=1.0 / 32.0))
                    blk.scalar(a2_ac_exp)

                def a2_ve_mask(v, g2=g2):
                    for j in range(4 * g2, 4 * g2 + 4):
                        off = (j - 4 * g2) * 128
                        v.wait_ge(sAc.h, V["exp_%d_%d" % (g2, j)])
                        mask_done[j] = sVe.inc(v.tensor_tensor(
                            probsT[:, j, off:off + 128],
                            probsT[:, j, off:off + 128],
                            cmask[:], ALU.mult))
                blk.vector(a2_ve_mask)

                for tl in range(4):
                    t = 4 * g2 + tl

                    def a2_sync_x(sync, t=t):
                        if t >= 2:
                            sync.wait_ge(sVe.h, V["st2_%d" % (t - 2)])
                        xr_ld[t] = sXr[t % 2].inc(sync.dma_start(
                            out=xr[:, t % 2, :],
                            in_=x_h[t * 128:(t + 1) * 128, :]), 16)
                    blk.sync(a2_sync_x)

                    def a2_pe_pv(pe, t=t, g2=g2, tl=tl):
                        jmax = 4 * g2 + tl
                        pe.wait_ge(sAc.h, V["exp_%d_%d" % (g2, jmax)])
                        pe.wait_ge(sVe.h, mask_done[jmax])
                        if t == 0:
                            pe.wait_ge(sMs.h, V["vones"])
                        if t >= 1:
                            pe.wait_ge(sVe.h, ctx_ev[t - 1])
                        for j in range(jmax + 1):
                            lhs = probsT[:, j, tl * 128:tl * 128 + 128]
                            base = VOFF + j * VSTRIDE
                            st, sp = (j == 0), (j == jmax)
                            pe.matmul(pctx[:, 0:512], lhs,
                                      arena[:, base:base + 512], start=st, stop=sp)
                            pe.matmul(pctx[:, 512:1024], lhs,
                                      arena[:, base + 512:base + 1024],
                                      start=st, stop=sp)
                            mm = pe.matmul(pctx[:, 1024:1025], lhs,
                                           arena[:, ONESCOL:ONESCOL + 1],
                                           start=st, stop=sp)
                        ctx_done[t] = sPe.inc(mm)
                    blk.tensor(a2_pe_pv)

                    def a2_ve_t(v, t=t):
                        v.wait_ge(sPe.h, ctx_done[t])
                        v.wait_ge(sXr[t % 2].h, xr_ld[t])
                        if t >= 2:
                            v.wait_ge(sSl[t % 2].h, x2_store[t - 2])
                            v.wait_ge(sGp.h, ln2_done[t - 2])
                        if t >= 1:
                            v.wait_ge(sAc.h, V["rstd2_%d" % (t - 1)])
                        w = sVe.inc(v.reciprocal(stt[:, 15:16], pctx[:, 1024:1025]))
                        v.wait_ge(sVe.h, w)
                        x2t = x2b[:, t % 2, :]
                        ctx_ev[t] = sVe.inc(
                            v.scalar_tensor_tensor(x2t, pctx[:, 0:1024], stt[:, 15:16],
                                                   xr[:, t % 2, :], ALU.mult, ALU.add))
                        v.wait_ge(sVe.h, ctx_ev[t])
                        v.reduce_sum(stt[:, 0:1], x2t, AX.X)
                        w = sVe.inc(v.scalar_tensor_tensor(
                            xr[:, t % 2, :], x2t, 1.0, x2t, ALU.bypass, ALU.mult,
                            accum_out=stt[:, 1:2]))
                        v.wait_ge(sVe.h, w)
                        w = sVe.inc(v.tensor_scalar(stt[:, 2:3], stt[:, 0:1],
                                                    1.0 / 1024.0, None, ALU.mult))
                        v.wait_ge(sVe.h, w)
                        w = sVe.inc(v.tensor_scalar(stt[:, 7:8], stt[:, 2:3],
                                                    stt[:, 2:3], -EPS, ALU.mult, ALU.add))
                        v.wait_ge(sVe.h, w)
                        V["st2_%d" % t] = sVe.inc(
                            v.tensor_scalar(stt[:, 3:4], stt[:, 1:2], 1.0 / 1024.0,
                                            stt[:, 7:8], ALU.mult, ALU.subtract))
                    blk.vector(a2_ve_t)

                    def a2_ac_t(s, t=t):
                        s.wait_ge(sVe.h, V["st2_%d" % t])
                        if t >= 1:
                            s.wait_ge(sGp.h, ln2_done[t - 1])
                        w = sAc.inc(s.activation(stt[:, 4:5], stt[:, 3:4], AF.Ln))
                        s.wait_ge(sAc.h, w)
                        V["rstd2_%d" % t] = sAc.inc(
                            s.activation(stt[:, 5:6], stt[:, 4:5], AF.Exp, scale=-0.5))
                    blk.scalar(a2_ac_t)

                    def a2_ve2_t(v, t=t):
                        v.wait_ge(sAc.h, V["rstd2_%d" % t])
                        V["negmr2_%d" % t] = sVe.inc(
                            v.tensor_scalar(stt[:, 6:7], stt[:, 2:3], -1.0,
                                            stt[:, 5:6], ALU.mult, ALU.mult))
                        w = sVe.inc(v.tensor_scalar(stt[:, 8:9], stt[:, 3:4], -EPS,
                                                    stt[:, 5:6], ALU.add, ALU.mult))
                        v.wait_ge(sVe.h, w)
                        w = sVe.inc(v.tensor_scalar(stt[:, 9:10], stt[:, 8:9],
                                                    stt[:, 5:6], 1024.0,
                                                    ALU.mult, ALU.mult))
                        v.wait_ge(sVe.h, w)
                        V["t4_%d" % t] = sVe.inc(
                            v.tensor_scalar(stt[:, 9:10], stt[:, 9:10], 1.0, 1e-24,
                                            ALU.mult, ALU.add))
                    blk.vector(a2_ve2_t)

                    def a2_ac2_t(s, t=t):
                        s.wait_ge(sVe.h, V["t4_%d" % t])
                        w = sAc.inc(s.activation(stt[:, 10:11], stt[:, 9:10], AF.Ln))
                        s.wait_ge(sAc.h, w)
                        rc = 16 + (t % 4)
                        V["rr_%d" % t] = sAc.inc(
                            s.activation(stt[:, rc:rc + 1], stt[:, 10:11], AF.Exp,
                                         scale=-0.5))
                    blk.scalar(a2_ac2_t)

                    if t >= 1:
                        emit_a2_tailA(t - 1)

                    def a2_gp_ln2(gp, t=t):
                        gp.wait_ge(sVe.h, V["negmr2_%d" % t])
                        if t >= 2:
                            gp.wait_ge(sPe.h, ln2T_tr[t - 2])
                        ln2_done[t] = sGp.inc(
                            gp.tensor_scalar(ln2[:, t % 2, :], x2b[:, t % 2, :],
                                             stt[:, 5:6], stt[:, 6:7],
                                             ALU.mult, ALU.add))
                    blk.gpsimd(a2_gp_ln2)

                    if t >= 2:
                        emit_a2_tailB(t - 2)

            emit_a2_tailA(NT - 1)

            def c_gp_w1e(gp):
                gp.wait_ge(sPe.h, V["sc_last_7"])
                w1r = w1s_h.rearrange("(kd p) h -> p kd h", p=128)
                a1v = arena[:, 0:32768].rearrange("p (kd h) -> p kd h", kd=KD)
                for part in range(4):
                    V["w1_%d" % part] = sW1p[part].inc(gp.dma_start(
                        out=a1v[:, :, part * 1024:(part + 1) * 1024],
                        in_=w1r[:, :, part * 1024:(part + 1) * 1024]), 16)
            blk.gpsimd(c_gp_w1e)

            emit_a2_tailB(NT - 2)
            emit_a2_tailB(NT - 1)

            def c_gp_w2l(gp):
                gp.wait_ge(sPe.h, ctx_done[NT - 1])
                w2r = w2s_h.rearrange("(hc p) d -> p hc d", p=128)
                a2v = arena[:, VOFF:VOFF + 32768].rearrange(
                    "p (hc d) -> p hc d", hc=32)
                for part in range(4):
                    sW2c.inc(gp.dma_start(
                        out=a2v[:, part * 8:(part + 1) * 8, :],
                        in_=w2r[:, part * 8:(part + 1) * 8, :]), 16)
            blk.gpsimd(c_gp_w2l)

            if debug_taps:
                gdump = nc.dram_tensor("gdump", [128, 256], BF16,
                                       kind="ExternalOutput").ap()
                cwdump = nc.dram_tensor("cwdump", [128, 128], BF16,
                                        kind="ExternalOutput").ap()
                lgdump = nc.dram_tensor("lgdump", [128, 16], F32,
                                        kind="ExternalOutput").ap()

                def dump_g(gp):
                    gp.wait_ge(sVe.h, gates_done[31])
                    sSp.inc(gp.dma_start(out=gdump, in_=gates_tm[:]), 16)
                    sSp.inc(gp.dma_start(out=cwdump, in_=cw[:]), 16)
                    sSp.inc(gp.dma_start(out=lgdump, in_=rt16[:]), 16)
                blk.gpsimd(dump_g)

            a2_pe_end = sPe.n
            a2_ac_end = sAc.n
            a2_ve_end = sVe.n

        with ExitStack() as pc:
            Ec = pc.enter_context
            gT2 = Ec(nc.sbuf_tensor("gT2", [NEXP, 2, 256], BF16))
            gbc = Ec(nc.sbuf_tensor("gbc", [128, 8, 256], BF16))
            ln2Tg = Ec(nc.sbuf_tensor("ln2Tg", [128, 2, KD, 256], BF16))
            x2g = Ec(nc.sbuf_tensor("x2g", [128, 2, 2048], F32))
            outsb = Ec(nc.sbuf_tensor("outsb", [128, 2, D], F32))
            hT = Ec(nc.sbuf_tensor("hT", [128, 2, 32, 256], BF16))
            b2s = Ec(nc.sbuf_tensor("b2_s", [NEXP, D], BF16))
            ph = [Ec(nc.psum_tensor("ph0", [128, 512], F32)),
                  Ec(nc.psum_tensor("ph1", [128, 512], F32))]
            pout = [Ec(nc.psum_tensor("pout0", [128, 1024], F32)),
                    Ec(nc.psum_tensor("pout1", [128, 1024], F32))]
            pbc = Ec(nc.psum_tensor("pbc", [128, 512], F32))
            ptG = Ec(nc.psum_tensor("ptG", [128, 1024], BF16))

            def c_gp_w(gp):
                gp.wait_ge(sPe.h, a2_pe_end)
                gp.wait_ge(sAc.h, a2_ac_end)
                gp.wait_ge(sVe.h, a2_ve_end)
                gp.wait_ge(sSl[0].h, sSl[0].n)
                gp.wait_ge(sSl[1].h, sSl[1].n)
                V["w2b2"] = sB2.inc(gp.dma_start(out=b2s[:], in_=b2_h), 16)
            blk.gpsimd(c_gp_w)

            def c_sync_gate(sync):
                sync.wait_ge(sAc.h, a2_ac_end)
                sync.wait_ge(sVe.h, a2_ve_end)
                sync.wait_ge(sPe.h, a2_pe_end)
            blk.sync(c_sync_gate)

            def _preload_gelu(sc):
                _load_act_set(nc, [AF.Gelu])
            blk.scalar(_preload_gelu)

            ln2g_ld = [None] * 16
            x2g_ld = [None] * 16
            w2_done = [None] * NT
            out_ev = [None] * NT
            out_st = [None] * NT

            def emit_c_gates(g):
                def c_pe_gT(pe, g=g):
                    pe.wait_ge(sVe.h, gates_done[2 * g + 1])
                    if g == 0:
                        pe.wait_ge(sVe.h, a2_ve_end)
                        pe.wait_ge(sAc.h, a2_ac_end)
                    if g >= 1:
                        pe.wait_ge(sVe.h, V["gT2cp_%d" % (g - 1)])
                    for tl in range(2):
                        mm = pe.transpose(ptG[0:8, tl * 128:tl * 128 + 128],
                                          gates_tm[:, (2 * g + tl) * 8:
                                                   (2 * g + tl + 1) * 8],
                                          ident[:])
                    V["gTtr_%d" % g] = sPe.inc(mm)
                blk.tensor(c_pe_gT)

                def c_ve_gT(v, g=g):
                    v.wait_ge(sPe.h, V["gTtr_%d" % g])
                    if g >= 2:
                        v.wait_ge(sPe.h, w2_done[2 * (g - 2) + 1])
                    V["gT2cp_%d" % g] = sVe.inc(
                        v.tensor_copy(gT2[:, g % 2, :], ptG[0:8, 0:256]))
                blk.vector(c_ve_gT)

            def emit_c_tail(gc):
                for tl in range(2):
                    t = 2 * gc + tl

                    def c_pe_w2(pe, gc=gc, tl=tl, t=t):
                        pe.wait_ge(sVe.h, V["gm_last_%d" % gc])
                        if gc == 0 and tl == 0:
                            pe.wait_ge(sW2c.h, sW2c.n)
                            pe.wait_ge(sB2.h, V["w2b2"])
                        if t >= 2:
                            pe.wait_ge(sVe.h, out_ev[t - 2])
                        for hc in range(32):
                            lhs = hT[:, gc % 2, hc, tl * 128:tl * 128 + 128]
                            b = VOFF + hc * 1024
                            pe.matmul(pout[tl][:, 0:512], lhs, arena[:, b:b + 512],
                                      start=(hc == 0), stop=False)
                            pe.matmul(pout[tl][:, 512:1024], lhs,
                                      arena[:, b + 512:b + 1024],
                                      start=(hc == 0), stop=False)
                        pe.matmul(pout[tl][:, 0:512],
                                  gT2[0:8, gc % 2, tl * 128:tl * 128 + 128],
                                  b2s[0:8, 0:512], start=False, stop=True)
                        mm = pe.matmul(pout[tl][:, 512:1024],
                                       gT2[0:8, gc % 2, tl * 128:tl * 128 + 128],
                                       b2s[0:8, 512:1024], start=False, stop=True)
                        w2_done[t] = sPe.inc(mm)
                    blk.tensor(c_pe_w2)

                    def c_ve_out(v, gc=gc, tl=tl, t=t):
                        v.wait_ge(sPe.h, w2_done[t])
                        v.wait_ge(sGx[gc % 2].h, x2g_ld[gc])
                        if t >= 2:
                            v.wait_ge(sF[t % 2].h, out_st[t - 2])
                        out_ev[t] = sVe.inc(v.tensor_tensor(
                            outsb[:, tl, :], pout[tl][:],
                            x2g[:, gc % 2, tl * 1024:(tl + 1) * 1024], ALU.add))
                    blk.vector(c_ve_out)

                    def c_gp_out(gp, t=t, tl=tl):
                        gp.wait_ge(sVe.h, out_ev[t])
                        out_st[t] = sF[t % 2].inc(gp.dma_start(
                            out=out_h[t * 128:(t + 1) * 128, :],
                            in_=outsb[:, tl, :]), 16)
                    blk.gpsimd(c_gp_out)

            emit_c_gates(0)

            for gc in range(16):
                def c_sync_g(sync, gc=gc):
                    sync.wait_ge(sSl[0].h, x2_store[2 * gc])
                    sync.wait_ge(sSl[1].h, x2_store[2 * gc + 1])
                    if gc >= 2:
                        sync.wait_ge(sPe.h, V["h_last_%d" % (gc - 2)])
                    ln2g_ld[gc] = sGl[gc % 2].inc(sync.dma_start(
                        out=ln2Tg[:, gc % 2, :, :],
                        in_=ln2t_scr[gc].rearrange("p (kd tc) -> p kd tc", kd=KD)), 16)
                    if gc >= 2:
                        sync.wait_ge(sVe.h, out_ev[2 * (gc - 2) + 1])
                    x2g_ld[gc] = sGx[gc % 2].inc(sync.dma_start(
                        out=x2g[:, gc % 2, :], in_=x2_scr[gc]), 16)
                blk.sync(c_sync_g)

                gbanks = [pbc, ph[0], ph[1], pbc]
                for pair in range(4):
                    def c_pe_gbc(pe, gc=gc, pair=pair):
                        if pair == 0:
                            pe.wait_ge(sVe.h, V["gT2cp_%d" % gc])
                        if pair == 3:
                            pe.wait_ge(sVe.h, V["gbc_cp_%d_0" % gc])
                        if gc >= 1:
                            if pair == 1:
                                pe.wait_ge(sAc.h, V["gelu_%d_%d" % (gc - 1, 30)])
                            elif pair == 2:
                                pe.wait_ge(sAc.h, V["gelu_%d_%d" % (gc - 1, 31)])
                        for e2 in range(2):
                            e = pair * 2 + e2
                            mm = pe.matmul(gbanks[pair][:, e2 * 256:(e2 + 1) * 256],
                                           sel[0:8, e * 128:(e + 1) * 128],
                                           gT2[0:8, gc % 2, :],
                                           start=True, stop=True)
                        V["gbc_mm_%d_%d" % (gc, pair)] = sPe.inc(mm)
                    blk.tensor(c_pe_gbc)

                    def c_ve_gbc(v, gc=gc, pair=pair):
                        v.wait_ge(sPe.h, V["gbc_mm_%d_%d" % (gc, pair)])
                        V["gbc_cp_%d_%d" % (gc, pair)] = sVe.inc(v.tensor_copy(
                            gbc[:, pair * 2:(pair + 1) * 2, :].rearrange(
                                "p e c -> p (e c)"),
                            gbanks[pair][:, 0:512]))
                    blk.vector(c_ve_gbc)

                for hc in range(32):
                    def c_pe_h(pe, gc=gc, hc=hc):
                        if hc == 0:
                            pe.wait_ge(sGl[gc % 2].h, ln2g_ld[gc])
                            pe.wait_ge(sVe.h, V["gbc_cp_%d_1" % gc])
                        if hc == 1:
                            pe.wait_ge(sVe.h, V["gbc_cp_%d_2" % gc])
                        if gc == 0:
                            pe.wait_ge(sW1p[hc // 8].h, V["w1_%d" % (hc // 8)])
                        if hc >= 2:
                            pe.wait_ge(sAc.h, V["gelu_%d_%d" % (gc, hc - 2)])
                        elif gc >= 1:
                            pe.wait_ge(sAc.h, V["gelu_%d_%d" % (gc - 1, 30 + hc)])
                        for kd in range(KD):
                            mm = pe.matmul(
                                ph[hc % 2][:, 0:256],
                                arena[:, kd * 4096 + hc * 128:
                                      kd * 4096 + hc * 128 + 128],
                                ln2Tg[:, gc % 2, kd, :],
                                start=(kd == 0), stop=(kd == KD - 1))
                        V["h_%d_%d" % (gc, hc)] = sPe.inc(mm)
                        if hc == 31:
                            V["h_last_%d" % gc] = sPe.n
                    blk.tensor(c_pe_h)

                    def c_ac_gelu(s, gc=gc, hc=hc):
                        if gc == 0 and hc == 0:
                            s.wait_ge(sSl[0].h, sSl[0].n)
                            s.wait_ge(sSl[1].h, sSl[1].n)
                            s.wait_ge(sVe.h, a2_ve_end)
                        if hc == 0 and gc >= 2:
                            s.wait_ge(sPe.h, w2_done[2 * (gc - 2) + 1])
                        s.wait_ge(sPe.h, V["h_%d_%d" % (gc, hc)])
                        V["gelu_%d_%d" % (gc, hc)] = sAc.inc(
                            s.activation(hT[:, gc % 2, hc, :], ph[hc % 2][:, 0:256],
                                         AF.Gelu, bias=b1s[:, hc:hc + 1]))
                    blk.scalar(c_ac_gelu)

                    def c_ve_gm(v, gc=gc, hc=hc):
                        v.wait_ge(sAc.h, V["gelu_%d_%d" % (gc, hc)])
                        if hc == 0:
                            v.wait_ge(sVe.h, V["gbc_cp_%d_3" % gc])
                        e = hc // 4
                        V["gm_%d_%d" % (gc, hc)] = sVe.inc(v.tensor_tensor(
                            hT[:, gc % 2, hc, :], hT[:, gc % 2, hc, :],
                            gbc[:, e, :], ALU.mult))
                        if hc == 31:
                            V["gm_last_%d" % gc] = sVe.n
                    blk.vector(c_ve_gm)

                if gc >= 1:
                    emit_c_tail(gc - 1)

                if gc + 1 < 16:
                    emit_c_gates(gc + 1)

            emit_c_tail(15)

            def fin_sync(sync):
                sync.wait_ge(sF[0].h, sF[0].n)
                sync.wait_ge(sF[1].h, sF[1].n)
            blk.sync(fin_sync)

            def fin_gp(gp):
                gp.wait_ge(sF[0].h, sF[0].n)
                gp.wait_ge(sF[1].h, sF[1].n)
            blk.gpsimd(fin_gp)

    return nc


def _host_consts():
    import ml_dtypes
    ident = np.eye(128, dtype=ml_dtypes.bfloat16)
    cmask = np.triu(np.ones((128, 128), dtype=np.float32)).astype(ml_dtypes.bfloat16)
    consts = np.array([CONSTS], dtype=np.float32)
    selm = np.zeros((NEXP, NEXP * 128), dtype=np.float32)
    for e in range(NEXP):
        selm[e, e * 128:(e + 1) * 128] = 1.0
    selm = selm.astype(ml_dtypes.bfloat16)
    return ident, cmask, consts, selm


def kernel(L0_attn_x, L0_attn_Wq, L0_attn_Wk, L0_attn_Wv,
           L0_ffn_w1, L0_ffn_b1, L0_ffn_w2, L0_ffn_b2,
           L0_ffn_centroids, L0_ffn_w_route):
    x = np.ascontiguousarray(np.asarray(L0_attn_x, dtype=np.float32))
    wqT = np.ascontiguousarray(np.asarray(L0_attn_Wq, dtype=np.float32).T)
    wkT = np.ascontiguousarray(np.asarray(L0_attn_Wk, dtype=np.float32).T)
    wvT = np.ascontiguousarray(np.asarray(L0_attn_Wv, dtype=np.float32).T)
    w1s = np.ascontiguousarray(
        np.asarray(L0_ffn_w1, dtype=np.float32).transpose(1, 0, 2).reshape(D, DFF))
    w2s = np.ascontiguousarray(np.asarray(L0_ffn_w2, dtype=np.float32).reshape(DFF, D))
    b1m = np.ascontiguousarray(
        np.asarray(L0_ffn_b1, dtype=np.float32).reshape(DFF).reshape(32, 128).T)
    b2 = np.ascontiguousarray(np.asarray(L0_ffn_b2, dtype=np.float32))
    cent = np.ascontiguousarray(np.asarray(L0_ffn_centroids, dtype=np.float32))
    wrt = np.ascontiguousarray(np.asarray(L0_ffn_w_route, dtype=np.float32))
    ident, cmask, consts, selm = _host_consts()

    nc = build(debug_taps=False)
    shared = dict(wqT=wqT, wkT=wkT, wvT=wvT, w1s=w1s, w2s=w2s, b1m=b1m, b2=b2,
                  cent=cent, wrt=wrt, ident=ident, cmask=cmask,
                  consts=consts, sel=selm)
    in_maps = [dict(shared, x=np.ascontiguousarray(x[b])) for b in range(8)]
    res = run_bass_kernel_spmd(nc, in_maps, core_ids=list(range(8)))
    return np.stack([r["out"] for r in res.results], axis=0)


# version 19
# speedup vs baseline: 1.1476x; 1.0184x over previous
import numpy as np
from contextlib import ExitStack

import concourse.bass as bass
import concourse.mybir as mybir
from concourse.bass_utils import run_bass_kernel_spmd
from concourse.hw_specs import get_activation_tables


def _load_act_set(nc, funcs):
    for idx, fs in enumerate(get_activation_tables(nc.m.arch).values()):
        if all(f in fs for f in funcs):
            nc.scalar.add_instruction(mybir.InstLoadActFuncSet(
                name=nc.get_next_instruction_name(),
                act_func_set_id=idx, ins=[], outs=[]))
            return
    raise ValueError(f"no table set contains {funcs}")

F32 = mybir.dt.float32
BF16 = mybir.dt.bfloat16
AF = mybir.ActivationFunctionType
ALU = mybir.AluOpType
AX = mybir.AxisListType

NTOK = 4096
D = 1024
NT = 32
KD = 8
DFF = 4096
NEXP = 8
EPS = 1e-5

CONSTS = [-0.5, EPS, 1024.0, 1.0 / 32.0, -1.0, 1.0 / 1024.0, 1e-24, -EPS]

VOFF = 32768
VSTRIDE = 1024
ONESCOL = 65536


def build(debug_taps=False):
    nc = bass.Bass("TRN2", target_bir_lowering=False)

    x_h = nc.dram_tensor("x", [NTOK, D], F32, kind="ExternalInput").ap()
    wqT_h = nc.dram_tensor("wqT", [D, D], F32, kind="ExternalInput").ap()
    wkT_h = nc.dram_tensor("wkT", [D, D], F32, kind="ExternalInput").ap()
    wvT_h = nc.dram_tensor("wvT", [D, D], F32, kind="ExternalInput").ap()
    w1s_h = nc.dram_tensor("w1s", [D, DFF], F32, kind="ExternalInput").ap()
    w2s_h = nc.dram_tensor("w2s", [DFF, D], F32, kind="ExternalInput").ap()
    b1m_h = nc.dram_tensor("b1m", [128, 32], F32, kind="ExternalInput").ap()
    b2_h = nc.dram_tensor("b2", [NEXP, D], F32, kind="ExternalInput").ap()
    cent_h = nc.dram_tensor("cent", [NEXP, D], F32, kind="ExternalInput").ap()
    wrt_h = nc.dram_tensor("wrt", [NEXP, D], F32, kind="ExternalInput").ap()
    ident_h = nc.dram_tensor("ident", [128, 128], BF16, kind="ExternalInput").ap()
    sel_h = nc.dram_tensor("sel", [NEXP, NEXP * 128], BF16, kind="ExternalInput").ap()
    cmask_h = nc.dram_tensor("cmask", [128, 128], BF16, kind="ExternalInput").ap()
    consts_h = nc.dram_tensor("consts", [1, len(CONSTS)], F32, kind="ExternalInput").ap()

    skind = "ExternalOutput" if debug_taps else "Internal"
    qt_scr = nc.dram_tensor("qt_scr", [8, 128, 4096], BF16, kind=skind).ap()
    ln2t_scr = nc.dram_tensor("ln2t_scr", [16, 128, 2048], BF16, kind=skind).ap()
    x2_scr = nc.dram_tensor("x2_scr", [16, 128, 2048], F32, kind=skind).ap()

    out_h = nc.dram_tensor("out", [NTOK, D], F32, kind="ExternalOutput").ap()

    V = {}

    with ExitStack() as topctx:
        E = topctx.enter_context

        arena = E(nc.sbuf_tensor("arena", [128, 65544], BF16))
        gates_tm = E(nc.sbuf_tensor("gates_tm", [128, 256], BF16))
        ident = E(nc.sbuf_tensor("ident_s", [128, 128], BF16))
        cmask = E(nc.sbuf_tensor("cmask_s", [128, 128], BF16))
        cw = E(nc.sbuf_tensor("cw_s", [128, 128], BF16))
        sel = E(nc.sbuf_tensor("sel_s", [NEXP, NEXP * 128], BF16))
        b1s = E(nc.sbuf_tensor("b1_s", [128, 32], F32))
        consts = E(nc.sbuf_tensor("consts_s", [1, len(CONSTS)], F32))
        stt = E(nc.sbuf_tensor("stt", [128, 32], F32))
        rt16 = E(nc.sbuf_tensor("rt16", [128, 16], F32))
        logit = E(nc.sbuf_tensor("logit", [128, 8], F32))
        top8 = E(nc.sbuf_tensor("top8", [128, 8], F32))
        expl = E(nc.sbuf_tensor("expl", [128, 8], F32))
        maskg = E(nc.sbuf_tensor("maskg", [128, 8], F32))
        gated = E(nc.sbuf_tensor("gated", [128, 8], F32))
        cstat = E(nc.sbuf_tensor("cstat", [128, 4], F32))

        class Ctr:
            def __init__(self, name):
                self.h = E(nc.semaphore(name))
                self.n = 0

            def inc(self, instr, k=1):
                instr.then_inc(self.h, k)
                self.n += k
                return self.n

        sIn = Ctr("sIn")
        sWt = [Ctr("sWt%d" % i) for i in range(3)]
        sSp = Ctr("sSp")
        sPe = Ctr("sPe")
        sAc = Ctr("sAc")
        sVe = Ctr("sVe")
        sX = [Ctr("sX0"), Ctr("sX1")]
        sQb = [Ctr("sQ0"), Ctr("sQ1")]
        sXr = [Ctr("sXr0"), Ctr("sXr1")]
        sSl = [Ctr("sSl0"), Ctr("sSl1")]
        sGl = [Ctr("sGl0"), Ctr("sGl1")]
        sGx = [Ctr("sGx0"), Ctr("sGx1")]
        sW1p = [Ctr("sW1p%d" % i) for i in range(4)]
        sW2c = Ctr("sW2c")
        sB2 = Ctr("sB2")
        sSel = Ctr("sSel")
        sF = [Ctr("sF0"), Ctr("sF1")]
        sMs = Ctr("sMs")
        sGp = Ctr("sGp")

        blk = E(nc.Block())

        def _preload_lnexp(sc):
            _load_act_set(nc, [AF.Ln, AF.Exp, AF.Identity])
        blk.scalar(_preload_lnexp)

        with ExitStack() as pset:
            Es = pset.enter_context
            centf = Es(nc.sbuf_tensor("centf", [NEXP, D], F32))
            wrtf = Es(nc.sbuf_tensor("wrtf", [NEXP, D], F32))
            cscr = Es(nc.sbuf_tensor("cscr", [NEXP, D], F32))
            cnormbf = Es(nc.sbuf_tensor("cnormbf", [NEXP, D], BF16))
            wrtbf = Es(nc.sbuf_tensor("wrtbf", [NEXP, D], BF16))
            ptmp = Es(nc.psum_tensor("ptmp", [128, 1024], BF16))

            def setup_sync(sync):
                for dst, src in [(ident, ident_h), (cmask, cmask_h),
                                 (b1s, b1m_h), (consts, consts_h),
                                 (centf, cent_h), (wrtf, wrt_h),
                                 (sel, sel_h)]:
                    sIn.inc(sync.dma_start(out=dst[:], in_=src), 16)
                V["setup_in"] = sIn.n
            blk.sync(setup_sync)

            for i, cv in enumerate(CONSTS):
                nc.const_aps.register(F32, cv, consts[0:1, i:i + 1])

            def setup_vec(v):
                v.wait_ge(sIn.h, V["setup_in"])
                w = sVe.inc(v.scalar_tensor_tensor(
                    cscr[:], centf[:], 1.0, centf[:],
                    ALU.bypass, ALU.mult, accum_out=cstat[0:8, 0:1]))
                v.wait_ge(sVe.h, w)
                V["cstat"] = sVe.inc(v.tensor_scalar(
                    cstat[0:8, 3:4], cstat[0:8, 0:1], 1e-24, None, ALU.add))
            blk.vector(setup_vec)

            def setup_scal(s):
                s.wait_ge(sVe.h, V["cstat"])
                w = sAc.inc(s.activation(cstat[0:8, 1:2], cstat[0:8, 3:4], AF.Ln))
                s.wait_ge(sAc.h, w)
                V["c_rc"] = sAc.inc(
                    s.activation(cstat[0:8, 2:3], cstat[0:8, 1:2], AF.Exp, scale=-0.5))
            blk.scalar(setup_scal)

            def setup_vec2(v):
                v.wait_ge(sAc.h, V["c_rc"])
                v.tensor_scalar(cnormbf[:], centf[:], cstat[0:8, 2:3], None, ALU.mult)
                V["cnorm"] = sVe.inc(v.tensor_copy(wrtbf[:], wrtf[:]))
            blk.vector(setup_vec2)

            def cw_pe(pe):
                pe.wait_ge(sVe.h, V["cnorm"])
                for kd in range(KD):
                    pe.transpose(ptmp[:, kd * 16:kd * 16 + 8],
                                 cnormbf[0:8, kd * 128:(kd + 1) * 128], ident[0:8, 0:8])
                    mm = pe.transpose(ptmp[:, kd * 16 + 8:kd * 16 + 16],
                                      wrtbf[0:8, kd * 128:(kd + 1) * 128], ident[0:8, 0:8])
                V["cw_tr"] = sPe.inc(mm)
            blk.tensor(cw_pe)

            def cw_ve(v):
                v.wait_ge(sPe.h, V["cw_tr"])
                V["cw"] = sVe.inc(v.tensor_copy(cw[:], ptmp[:, 0:128]))
            blk.vector(cw_ve)

            if debug_taps:
                cw0dump = nc.dram_tensor("cw0dump", [128, 128], BF16,
                                         kind="ExternalOutput").ap()
                cn0dump = nc.dram_tensor("cn0dump", [NEXP, D], BF16,
                                         kind="ExternalOutput").ap()

                def dump_cw0(gp):
                    gp.wait_ge(sVe.h, V["cw"])
                    sSp.inc(gp.dma_start(out=cw0dump, in_=cw[:]), 16)
                    sSp.inc(gp.dma_start(out=cn0dump, in_=cnormbf[:]), 16)
                    V["cw0d"] = sSp.n
                blk.gpsimd(dump_cw0)

        with ExitStack() as pa1:
            Ea = pa1.enter_context
            wqkv = Ea(nc.sbuf_tensor("wqkv", [128, 3, KD, D], BF16))
            scr = Ea(nc.sbuf_tensor("scr", [128, D], F32))
            xbuf = Ea(nc.sbuf_tensor("xbuf", [128, 2, D], F32))
            xln = Ea(nc.sbuf_tensor("xln", [128, 2, D], BF16))
            xlnT = Ea(nc.sbuf_tensor("xlnT", [128, 2, KD, 256], BF16))
            qstage = Ea(nc.sbuf_tensor("qstage", [128, KD, 256], BF16))
            pqk = [Ea(nc.psum_tensor("pqk0", [128, 512], F32)),
                   Ea(nc.psum_tensor("pqk1", [128, 512], F32))]
            pv = [Ea(nc.psum_tensor("pv0", [128, 1024], F32)),
                  Ea(nc.psum_tensor("pv1", [128, 1024], F32))]
            ptr = [Ea(nc.psum_tensor("ptr0", [128, 1024], BF16)),
                   Ea(nc.psum_tensor("ptr1", [128, 1024], BF16))]

            def a1_gpsimd(g):
                g.wait_ge(sVe.h, V["cw"])
                for i, wh in enumerate([wqT_h, wkT_h, wvT_h]):
                    V["wqkv%d" % i] = sWt[i].inc(g.dma_start(
                        out=wqkv[:, i, :, :],
                        in_=wh.rearrange("(kd p) m -> p kd m", p=128)), 16)
                V["vones"] = sMs.inc(
                    g.memset(arena[:, ONESCOL:ONESCOL + 1], 1.0))
            blk.gpsimd(a1_gpsimd)

            x_ld = [None] * NT
            xln_done = [None] * NT
            sq_done = [None] * NT
            stats_done = [None] * NT
            negmr_done = [None] * NT
            tr_done = [None] * NT
            xlt_done = [None] * NT
            qk_done = {}
            qk_ev = {}
            v_done = [None] * NT
            v_ev = [None] * NT
            qspill = [None] * 16

            def emit_a1_head_sync(t):
                def a1_sync_t(sync, t=t):
                    if t == 0:
                        sync.wait_ge(sVe.h, V["cw"])
                    if t >= 2:
                        sync.wait_ge(sAc.h, xln_done[t - 2])
                    x_ld[t] = sX[t % 2].inc(sync.dma_start(
                        out=xbuf[:, t % 2, :],
                        in_=x_h[t * 128:(t + 1) * 128, :]), 16)
                blk.sync(a1_sync_t)

            def emit_a1_head_v1(g):
                sb = 16 * (g % 2)

                def a1_v1(v, g=g, sb=sb):
                    v.wait_ge(sX[0].h, x_ld[2 * g])
                    v.wait_ge(sX[1].h, x_ld[2 * g + 1])
                    if g >= 2:
                        v.wait_ge(sAc.h, xln_done[2 * (g - 2) + 1])
                    w = sVe.inc(v.reduce_sum(stt[:, sb:sb + 2], xbuf[:, :, :], AX.X))
                    v.wait_ge(sVe.h, w)
                    w = sVe.inc(v.scalar_tensor_tensor(
                        scr[:], xbuf[:, 0, :], 1.0, xbuf[:, 0, :],
                        ALU.bypass, ALU.mult, accum_out=stt[:, sb + 2:sb + 3]))
                    v.wait_ge(sVe.h, w)
                    w = sVe.inc(v.scalar_tensor_tensor(
                        scr[:], xbuf[:, 1, :], 1.0, xbuf[:, 1, :],
                        ALU.bypass, ALU.mult, accum_out=stt[:, sb + 3:sb + 4]))
                    v.wait_ge(sVe.h, w)
                    w = sVe.inc(v.tensor_tensor(stt[:, sb + 4:sb + 6],
                                                stt[:, sb:sb + 2],
                                                stt[:, sb:sb + 2], ALU.mult))
                    v.wait_ge(sVe.h, w)
                    w = sVe.inc(v.tensor_scalar(stt[:, sb + 4:sb + 6],
                                                stt[:, sb + 4:sb + 6],
                                                1.0 / 1048576.0, None, ALU.mult))
                    v.wait_ge(sVe.h, w)
                    w = sVe.inc(v.tensor_scalar(stt[:, sb + 6:sb + 8],
                                                stt[:, sb + 2:sb + 4],
                                                1.0 / 1024.0, EPS,
                                                ALU.mult, ALU.add))
                    v.wait_ge(sVe.h, w)
                    V["var_%d" % g] = sVe.inc(v.tensor_tensor(
                        stt[:, sb + 8:sb + 10], stt[:, sb + 6:sb + 8],
                        stt[:, sb + 4:sb + 6], ALU.subtract))
                blk.vector(a1_v1)

            def emit_a1_head_s1(g):
                sb = 16 * (g % 2)

                def a1_s1(s, g=g, sb=sb):
                    s.wait_ge(sVe.h, V["var_%d" % g])
                    w = sAc.inc(s.activation(stt[:, sb + 10:sb + 12],
                                             stt[:, sb + 8:sb + 10], AF.Ln))
                    s.wait_ge(sAc.h, w)
                    V["rstd_%d" % g] = sAc.inc(
                        s.activation(stt[:, sb + 12:sb + 14],
                                     stt[:, sb + 10:sb + 12], AF.Exp, scale=-0.5))
                blk.scalar(a1_s1)

            def emit_a1_head_v2(g):
                sb = 16 * (g % 2)

                def a1_v2(v, g=g, sb=sb):
                    v.wait_ge(sAc.h, V["rstd_%d" % g])
                    w = sVe.inc(v.tensor_tensor(stt[:, sb + 14:sb + 16],
                                                stt[:, sb:sb + 2],
                                                stt[:, sb + 12:sb + 14], ALU.mult))
                    v.wait_ge(sVe.h, w)
                    V["negmr_%d" % g] = sVe.inc(
                        v.tensor_scalar(stt[:, sb + 14:sb + 16],
                                        stt[:, sb + 14:sb + 16],
                                        -1.0 / 1024.0, None, ALU.mult))
                blk.vector(a1_v2)

            def emit_a1_head_s2(g):
                sb = 16 * (g % 2)

                def a1_s2(s, g=g, sb=sb):
                    s.wait_ge(sVe.h, V["negmr_%d" % g])
                    for tl in range(2):
                        t = 2 * g + tl
                        if t >= 2:
                            s.wait_ge(sPe.h, tr_done[t - 2])
                        xln_done[t] = sAc.inc(
                            s.activation(xln[:, t % 2, :], xbuf[:, t % 2, :],
                                         AF.Identity,
                                         bias=stt[:, sb + 14 + tl:sb + 15 + tl],
                                         scale=stt[:, sb + 12 + tl:sb + 13 + tl]))
                blk.scalar(a1_s2)

            def emit_a1_trx(t):
                def a1_pe_t(pe, t=t):
                    pe.wait_ge(sAc.h, xln_done[t])
                    if t >= 2:
                        pe.wait_ge(sVe.h, xlt_done[t - 2])
                    for kd in range(KD):
                        mm = pe.transpose(ptr[t % 2][:, kd * 128:(kd + 1) * 128],
                                          xln[:, t % 2, kd * 128:(kd + 1) * 128],
                                          ident[:])
                    tr_done[t] = sPe.inc(mm)
                blk.tensor(a1_pe_t)

                def a1_vec3_t(v, t=t):
                    v.wait_ge(sPe.h, tr_done[t])
                    if t // 2 >= 2:
                        v.wait_ge(sPe.h, v_done[2 * (t // 2 - 2) + 1])
                    xlt_done[t] = sVe.inc(v.tensor_copy(
                        xlnT[:, (t // 2) % 2, :, (t % 2) * 128:(t % 2) * 128 + 128],
                        ptr[t % 2][:].rearrange("p (kd c) -> p kd c", kd=KD)))
                blk.vector(a1_vec3_t)

            def emit_a1_vburst(g):
                for tl in range(2):
                    def a1_pe_v(pe, g=g, tl=tl):
                        if g == 0 and tl == 0:
                            pe.wait_ge(sWt[2].h, V["wqkv2"])
                        if g >= 1:
                            pe.wait_ge(sAc.h, v_ev[2 * (g - 1) + tl])
                        for kd in range(KD):
                            for half in range(2):
                                mm = pe.matmul(
                                    pv[tl][:, half * 512:(half + 1) * 512],
                                    xlnT[:, g % 2, kd, tl * 128:(tl + 1) * 128],
                                    wqkv[:, 2, kd, half * 512:(half + 1) * 512],
                                    start=(kd == 0), stop=(kd == KD - 1))
                        v_done[2 * g + tl] = sPe.inc(mm)
                    blk.tensor(a1_pe_v)

                    def a1_sc_v(s, g=g, tl=tl):
                        j = 2 * g + tl
                        s.wait_ge(sPe.h, v_done[j])
                        base = VOFF + j * VSTRIDE
                        v_ev[j] = sAc.inc(
                            s.copy(arena[:, base:base + 1024], pv[tl][:]))
                    blk.scalar(a1_sc_v)

                def a1_gp_g(gp, g=g):
                    gq, sub = g // 2, g % 2
                    gp.wait_ge(sAc.h, qk_ev[(g, 7)])
                    dst = qt_scr[gq].rearrange("p (m tc) -> p m tc", m=KD)[
                        :, :, sub * 256:sub * 256 + 256]
                    qspill[g] = sSp.inc(
                        gp.dma_start(out=dst, in_=qstage[:]), 16)
                blk.gpsimd(a1_gp_g)

            def emit_a1_qkburst(g):
                for u in range(16):
                    proj, m = u // 8, u % 8

                    def a1_pe_u(pe, g=g, u=u, proj=proj, m=m):
                        if u == 0:
                            pe.wait_ge(sVe.h, xlt_done[2 * g + 1])
                        if g == 0 and u % 8 == 0:
                            pe.wait_ge(sWt[proj].h, V["wqkv%d" % proj])
                        prev = qk_ev.get((g, u - 2)) if u >= 2 else (
                            qk_ev.get((g - 1, 14 + u)) if g >= 1 else None)
                        if prev is not None:
                            pe.wait_ge(sAc.h, prev)
                        for kd in range(KD):
                            mm = pe.matmul(
                                pqk[u % 2][:, 0:256],
                                wqkv[:, proj, kd, m * 128:(m + 1) * 128],
                                xlnT[:, g % 2, kd, :],
                                start=(kd == 0), stop=(kd == KD - 1))
                        qk_done[(g, u)] = sPe.inc(mm)
                    blk.tensor(a1_pe_u)

                    def a1_sc_u(s, g=g, u=u, proj=proj, m=m):
                        s.wait_ge(sPe.h, qk_done[(g, u)])
                        if u == 0 and g >= 1:
                            s.wait_ge(sSp.h, qspill[g - 1])
                        if proj == 0:
                            dst = qstage[:, m, :]
                        else:
                            dst = arena[:, m * 4096 + g * 256:
                                        m * 4096 + g * 256 + 256]
                        qk_ev[(g, u)] = sAc.inc(
                            s.copy(dst, pqk[u % 2][:, 0:256]))
                    blk.scalar(a1_sc_u)

            emit_a1_head_sync(0)
            emit_a1_head_sync(1)
            emit_a1_head_v1(0)
            emit_a1_head_s1(0)
            emit_a1_head_v2(0)
            emit_a1_head_s2(0)
            for g in range(16):
                emit_a1_trx(2 * g)
                emit_a1_trx(2 * g + 1)
                if 2 * g + 2 < NT:
                    emit_a1_head_sync(2 * g + 2)
                    emit_a1_head_sync(2 * g + 3)
                    emit_a1_head_v1(g + 1)
                if g >= 1:
                    emit_a1_vburst(g - 1)
                emit_a1_qkburst(g)
                if 2 * g + 2 < NT:
                    emit_a1_head_s1(g + 1)
                    emit_a1_head_v2(g + 1)
                    emit_a1_head_s2(g + 1)
            emit_a1_vburst(15)

            a1_pe_last = sPe.n
            a1_ac_last = sAc.n
            a1_last_ev = sVe.n

        with ExitStack() as pa2:
            Ea = pa2.enter_context
            probsT = Ea(nc.sbuf_tensor("probsT", [128, 32, 512], BF16))
            qbuf = Ea(nc.sbuf_tensor("qbuf", [128, 2, KD, 512], BF16))
            xr = Ea(nc.sbuf_tensor("xr", [128, 2, D], F32))
            x2b = Ea(nc.sbuf_tensor("x2b", [128, 2, D], F32))
            ln2 = Ea(nc.sbuf_tensor("ln2", [128, 2, D], BF16))
            ln2Ts = Ea(nc.sbuf_tensor("ln2Ts", [128, 2, D], BF16))
            psc = [Ea(nc.psum_tensor("psc0", [128, 512], F32)),
                   Ea(nc.psum_tensor("psc1", [128, 512], F32)),
                   Ea(nc.psum_tensor("psc2", [128, 512], F32))]
            pctx = Ea(nc.psum_tensor("pctx", [128, 1536], F32))
            ptrA = Ea(nc.psum_tensor("ptrA", [128, 1024], BF16))
            prt = Ea(nc.psum_tensor("prt", [128, 512], F32))

            q_ld = [None] * 8
            sc_seq = []
            xr_ld = [None] * NT
            mask_done = {}
            ctx_done = [None] * NT
            ctx_ev = [None] * NT
            ln2_done = [None] * NT
            ln2T_tr = [None] * NT
            ln2T_cp = [None] * NT
            rt_done = [None] * NT
            gates_done = [None] * NT
            spill_done = [None] * NT
            x2_store = [None] * NT

            def emit_a2_tailA(t):
                def a2_pe_tr(pe, t=t):
                    pe.wait_ge(sGp.h, ln2_done[t])
                    if t >= 1:
                        pe.wait_ge(sVe.h, ln2T_cp[t - 1])
                    for kd in range(KD):
                        mm = pe.transpose(ptrA[:, kd * 128:(kd + 1) * 128],
                                          ln2[:, t % 2, kd * 128:(kd + 1) * 128],
                                          ident[:])
                    ln2T_tr[t] = sPe.inc(mm)
                blk.tensor(a2_pe_tr)

                def a2_ve3_t(v, t=t):
                    v.wait_ge(sPe.h, ln2T_tr[t])
                    if t >= 2:
                        v.wait_ge(sSl[t % 2].h, x2_store[t - 2])
                        v.wait_ge(sPe.h, rt_done[t - 2])
                    ln2T_cp[t] = sVe.inc(
                        v.tensor_copy(ln2Ts[:, t % 2, :], ptrA[:]))
                blk.vector(a2_ve3_t)

                def a2_gp_t(gp, t=t):
                    gc, sub = t // 2, t % 2
                    gp.wait_ge(sVe.h, ln2T_cp[t])
                    dst = ln2t_scr[gc].rearrange("p (kd tc) -> p kd tc", kd=KD)[
                        :, :, sub * 128:sub * 128 + 128]
                    src = ln2Ts[:, t % 2, :].rearrange("p (kd c) -> p kd c", kd=KD)
                    spill_done[t] = sSl[t % 2].inc(gp.dma_start(out=dst, in_=src), 16)
                    gp.wait_ge(sVe.h, V["st2_%d" % t])
                    x2_store[t] = sSl[t % 2].inc(gp.dma_start(
                        out=x2_scr[gc, :, sub * 1024:sub * 1024 + 1024],
                        in_=x2b[:, t % 2, :]), 16)
                blk.gpsimd(a2_gp_t)


            def emit_a2_tailB(t):
                def a2_pe_rt(pe, t=t):
                    pe.wait_ge(sVe.h, ln2T_cp[t])
                    if t >= 1:
                        pe.wait_ge(sVe.h, V["rtcp_%d" % (t - 1)])
                    for kd in range(KD):
                        mm = pe.matmul(prt[:, 0:16],
                                       ln2Ts[:, t % 2, kd * 128:(kd + 1) * 128],
                                       cw[:, kd * 16:kd * 16 + 16],
                                       start=(kd == 0), stop=(kd == KD - 1))
                    rt_done[t] = sPe.inc(mm)
                blk.tensor(a2_pe_rt)

                def a2_ve4_t(v, t=t):
                    v.wait_ge(sPe.h, rt_done[t])
                    v.wait_ge(sAc.h, V["rr_%d" % t])
                    V["rtcp_%d" % t] = sVe.inc(v.tensor_copy(rt16[:], prt[:, 0:16]))
                    v.wait_ge(sVe.h, V["rtcp_%d" % t])
                    rc = 16 + (t % 4)
                    w = sVe.inc(v.scalar_tensor_tensor(
                        logit[:], rt16[:, 0:8], stt[:, rc:rc + 1],
                        rt16[:, 8:16], ALU.mult, ALU.add))
                    v.wait_ge(sVe.h, w)
                    w = sVe.inc(v.max(top8[:], logit[:]))
                    v.wait_ge(sVe.h, w)
                    V["negm1_%d" % t] = sVe.inc(
                        v.tensor_scalar(stt[:, 12:13], top8[:, 0:1], -1.0,
                                        None, ALU.mult))
                blk.vector(a2_ve4_t)

                def a2_ac3_t(s, t=t):
                    s.wait_ge(sVe.h, V["negm1_%d" % t])
                    V["expl_%d" % t] = sAc.inc(
                        s.activation(expl[:], logit[:], AF.Exp, bias=stt[:, 12:13]))
                blk.scalar(a2_ac3_t)

                def a2_ve5_t(v, t=t):
                    v.wait_ge(sAc.h, V["expl_%d" % t])
                    w = sVe.inc(v.tensor_scalar(maskg[:], logit[:], top8[:, 1:2],
                                                None, ALU.is_ge))
                    v.wait_ge(sVe.h, w)
                    w = sVe.inc(v.tensor_tensor(gated[:], expl[:], maskg[:],
                                                ALU.mult))
                    v.wait_ge(sVe.h, w)
                    w = sVe.inc(v.tensor_scalar(expl[:], gated[:], 1.0, 0.0,
                                                ALU.mult, ALU.add,
                                                accum_out=stt[:, 13:14]))
                    v.wait_ge(sVe.h, w)
                    w = sVe.inc(v.reciprocal(stt[:, 14:15], stt[:, 13:14]))
                    v.wait_ge(sVe.h, w)
                    gates_done[t] = sVe.inc(
                        v.tensor_scalar(gates_tm[:, t * 8:(t + 1) * 8], gated[:],
                                        stt[:, 14:15], None, ALU.mult))
                blk.vector(a2_ve5_t)

            for g2 in range(8):
                JN = 4 * g2 + 4

                def a2_sync_q(sync, g2=g2):
                    if g2 <= 1:
                        sync.wait_ge(sAc.h, a1_ac_last)
                        sync.wait_ge(sPe.h, a1_pe_last)
                        sync.wait_ge(sVe.h, a1_last_ev)
                        sync.wait_ge(sSp.h, qspill[15])
                    else:
                        sync.wait_ge(sPe.h, V["sc_last_%d" % (g2 - 2)])
                    q_ld[g2] = sQb[g2 % 2].inc(sync.dma_start(
                        out=qbuf[:, g2 % 2, :, :], in_=qt_scr[g2].rearrange(
                            "p (m tc) -> p m tc", m=KD)), 16)
                blk.sync(a2_sync_q)

                for j in range(JN):
                    jj = len(sc_seq)
                    sc_seq.append((g2, j))

                    def a2_pe_sc(pe, g2=g2, j=j, JN=JN, jj=jj):
                        if j == 0:
                            pe.wait_ge(sQb[g2 % 2].h, q_ld[g2])
                        if jj >= 3:
                            pg, pj = sc_seq[jj - 3]
                            pe.wait_ge(sAc.h, V["exp_%d_%d" % (pg, pj)])
                        for kd in range(KD):
                            mm = pe.matmul(
                                psc[jj % 3][:],
                                arena[:, kd * 4096 + j * 128:kd * 4096 + j * 128 + 128],
                                qbuf[:, g2 % 2, kd, :],
                                start=(kd == 0), stop=(kd == KD - 1))
                        V["sc_%d_%d" % (g2, j)] = sPe.inc(mm)
                        if j == JN - 1:
                            V["sc_last_%d" % g2] = sPe.n
                    blk.tensor(a2_pe_sc)

                    def a2_ac_exp(s, g2=g2, j=j, jj=jj):
                        s.wait_ge(sPe.h, V["sc_%d_%d" % (g2, j)])
                        V["exp_%d_%d" % (g2, j)] = sAc.inc(
                            s.activation(probsT[:, j, :], psc[jj % 3][:],
                                         AF.Exp, scale=1.0 / 32.0))
                    blk.scalar(a2_ac_exp)

                def a2_ve_mask(v, g2=g2):
                    for j in range(4 * g2, 4 * g2 + 4):
                        off = (j - 4 * g2) * 128
                        v.wait_ge(sAc.h, V["exp_%d_%d" % (g2, j)])
                        mask_done[j] = sVe.inc(v.tensor_tensor(
                            probsT[:, j, off:off + 128],
                            probsT[:, j, off:off + 128],
                            cmask[:], ALU.mult))
                blk.vector(a2_ve_mask)

                for tl in range(4):
                    t = 4 * g2 + tl

                    def a2_sync_x(sync, t=t):
                        if t >= 2:
                            sync.wait_ge(sVe.h, V["st2_%d" % (t - 2)])
                        xr_ld[t] = sXr[t % 2].inc(sync.dma_start(
                            out=xr[:, t % 2, :],
                            in_=x_h[t * 128:(t + 1) * 128, :]), 16)
                    blk.sync(a2_sync_x)

                    def a2_pe_pv(pe, t=t, g2=g2, tl=tl):
                        jmax = 4 * g2 + tl
                        pe.wait_ge(sAc.h, V["exp_%d_%d" % (g2, jmax)])
                        pe.wait_ge(sVe.h, mask_done[jmax])
                        if t == 0:
                            pe.wait_ge(sMs.h, V["vones"])
                        if t >= 1:
                            pe.wait_ge(sVe.h, ctx_ev[t - 1])
                        for j in range(jmax + 1):
                            lhs = probsT[:, j, tl * 128:tl * 128 + 128]
                            base = VOFF + j * VSTRIDE
                            st, sp = (j == 0), (j == jmax)
                            pe.matmul(pctx[:, 0:512], lhs,
                                      arena[:, base:base + 512], start=st, stop=sp)
                            pe.matmul(pctx[:, 512:1024], lhs,
                                      arena[:, base + 512:base + 1024],
                                      start=st, stop=sp)
                            mm = pe.matmul(pctx[:, 1024:1025], lhs,
                                           arena[:, ONESCOL:ONESCOL + 1],
                                           start=st, stop=sp)
                        ctx_done[t] = sPe.inc(mm)
                    blk.tensor(a2_pe_pv)

                    def a2_ve_t(v, t=t):
                        v.wait_ge(sPe.h, ctx_done[t])
                        v.wait_ge(sXr[t % 2].h, xr_ld[t])
                        if t >= 2:
                            v.wait_ge(sSl[t % 2].h, x2_store[t - 2])
                            v.wait_ge(sGp.h, ln2_done[t - 2])
                        if t >= 1:
                            v.wait_ge(sAc.h, V["rstd2_%d" % (t - 1)])
                        w = sVe.inc(v.reciprocal(stt[:, 15:16], pctx[:, 1024:1025]))
                        v.wait_ge(sVe.h, w)
                        x2t = x2b[:, t % 2, :]
                        ctx_ev[t] = sVe.inc(
                            v.scalar_tensor_tensor(x2t, pctx[:, 0:1024], stt[:, 15:16],
                                                   xr[:, t % 2, :], ALU.mult, ALU.add))
                        v.wait_ge(sVe.h, ctx_ev[t])
                        v.reduce_sum(stt[:, 0:1], x2t, AX.X)
                        w = sVe.inc(v.scalar_tensor_tensor(
                            xr[:, t % 2, :], x2t, 1.0, x2t, ALU.bypass, ALU.mult,
                            accum_out=stt[:, 1:2]))
                        v.wait_ge(sVe.h, w)
                        w = sVe.inc(v.tensor_scalar(stt[:, 2:3], stt[:, 0:1],
                                                    1.0 / 1024.0, None, ALU.mult))
                        v.wait_ge(sVe.h, w)
                        w = sVe.inc(v.tensor_scalar(stt[:, 7:8], stt[:, 2:3],
                                                    stt[:, 2:3], -EPS, ALU.mult, ALU.add))
                        v.wait_ge(sVe.h, w)
                        V["st2_%d" % t] = sVe.inc(
                            v.tensor_scalar(stt[:, 3:4], stt[:, 1:2], 1.0 / 1024.0,
                                            stt[:, 7:8], ALU.mult, ALU.subtract))
                    blk.vector(a2_ve_t)

                    def a2_ac_t(s, t=t):
                        s.wait_ge(sVe.h, V["st2_%d" % t])
                        if t >= 1:
                            s.wait_ge(sGp.h, ln2_done[t - 1])
                        w = sAc.inc(s.activation(stt[:, 4:5], stt[:, 3:4], AF.Ln))
                        s.wait_ge(sAc.h, w)
                        V["rstd2_%d" % t] = sAc.inc(
                            s.activation(stt[:, 5:6], stt[:, 4:5], AF.Exp, scale=-0.5))
                    blk.scalar(a2_ac_t)

                    def a2_ve2_t(v, t=t):
                        v.wait_ge(sAc.h, V["rstd2_%d" % t])
                        V["negmr2_%d" % t] = sVe.inc(
                            v.tensor_scalar(stt[:, 6:7], stt[:, 2:3], -1.0,
                                            stt[:, 5:6], ALU.mult, ALU.mult))
                        w = sVe.inc(v.tensor_scalar(stt[:, 8:9], stt[:, 3:4], -EPS,
                                                    stt[:, 5:6], ALU.add, ALU.mult))
                        v.wait_ge(sVe.h, w)
                        w = sVe.inc(v.tensor_scalar(stt[:, 9:10], stt[:, 8:9],
                                                    stt[:, 5:6], 1024.0,
                                                    ALU.mult, ALU.mult))
                        v.wait_ge(sVe.h, w)
                        V["t4_%d" % t] = sVe.inc(
                            v.tensor_scalar(stt[:, 9:10], stt[:, 9:10], 1.0, 1e-24,
                                            ALU.mult, ALU.add))
                    blk.vector(a2_ve2_t)

                    def a2_ac2_t(s, t=t):
                        s.wait_ge(sVe.h, V["t4_%d" % t])
                        w = sAc.inc(s.activation(stt[:, 10:11], stt[:, 9:10], AF.Ln))
                        s.wait_ge(sAc.h, w)
                        rc = 16 + (t % 4)
                        V["rr_%d" % t] = sAc.inc(
                            s.activation(stt[:, rc:rc + 1], stt[:, 10:11], AF.Exp,
                                         scale=-0.5))
                    blk.scalar(a2_ac2_t)

                    if t >= 1:
                        emit_a2_tailA(t - 1)

                    def a2_gp_ln2(gp, t=t):
                        gp.wait_ge(sVe.h, V["negmr2_%d" % t])
                        if t >= 2:
                            gp.wait_ge(sPe.h, ln2T_tr[t - 2])
                        ln2_done[t] = sGp.inc(
                            gp.tensor_scalar(ln2[:, t % 2, :], x2b[:, t % 2, :],
                                             stt[:, 5:6], stt[:, 6:7],
                                             ALU.mult, ALU.add))
                    blk.gpsimd(a2_gp_ln2)

                    if t >= 2:
                        emit_a2_tailB(t - 2)

            emit_a2_tailA(NT - 1)

            def c_gp_w1e(gp):
                gp.wait_ge(sPe.h, V["sc_last_7"])
                w1r = w1s_h.rearrange("(kd p) h -> p kd h", p=128)
                a1v = arena[:, 0:32768].rearrange("p (kd h) -> p kd h", kd=KD)
                for part in range(4):
                    V["w1_%d" % part] = sW1p[part].inc(gp.dma_start(
                        out=a1v[:, :, part * 1024:(part + 1) * 1024],
                        in_=w1r[:, :, part * 1024:(part + 1) * 1024]), 16)
            blk.gpsimd(c_gp_w1e)

            emit_a2_tailB(NT - 2)
            emit_a2_tailB(NT - 1)

            def c_gp_w2l(gp):
                gp.wait_ge(sPe.h, ctx_done[NT - 1])
                w2r = w2s_h.rearrange("(hc p) d -> p hc d", p=128)
                a2v = arena[:, VOFF:VOFF + 32768].rearrange(
                    "p (hc d) -> p hc d", hc=32)
                for part in range(4):
                    sW2c.inc(gp.dma_start(
                        out=a2v[:, part * 8:(part + 1) * 8, :],
                        in_=w2r[:, part * 8:(part + 1) * 8, :]), 16)
            blk.gpsimd(c_gp_w2l)

            if debug_taps:
                gdump = nc.dram_tensor("gdump", [128, 256], BF16,
                                       kind="ExternalOutput").ap()
                cwdump = nc.dram_tensor("cwdump", [128, 128], BF16,
                                        kind="ExternalOutput").ap()
                lgdump = nc.dram_tensor("lgdump", [128, 16], F32,
                                        kind="ExternalOutput").ap()

                def dump_g(gp):
                    gp.wait_ge(sVe.h, gates_done[31])
                    sSp.inc(gp.dma_start(out=gdump, in_=gates_tm[:]), 16)
                    sSp.inc(gp.dma_start(out=cwdump, in_=cw[:]), 16)
                    sSp.inc(gp.dma_start(out=lgdump, in_=rt16[:]), 16)
                blk.gpsimd(dump_g)

            a2_pe_end = sPe.n
            a2_ac_end = sAc.n
            a2_ve_end = sVe.n

        with ExitStack() as pc:
            Ec = pc.enter_context
            gT2 = Ec(nc.sbuf_tensor("gT2", [NEXP, 2, 256], BF16))
            gbc = Ec(nc.sbuf_tensor("gbc", [128, 8, 256], BF16))
            ln2Tg = Ec(nc.sbuf_tensor("ln2Tg", [128, 2, KD, 256], BF16))
            x2g = Ec(nc.sbuf_tensor("x2g", [128, 2, 2048], F32))
            outsb = Ec(nc.sbuf_tensor("outsb", [128, 2, D], F32))
            hT = Ec(nc.sbuf_tensor("hT", [128, 2, 32, 256], BF16))
            b2s = Ec(nc.sbuf_tensor("b2_s", [NEXP, D], BF16))
            ph = [Ec(nc.psum_tensor("ph0", [128, 512], F32)),
                  Ec(nc.psum_tensor("ph1", [128, 512], F32))]
            pout = [Ec(nc.psum_tensor("pout0", [128, 1024], F32)),
                    Ec(nc.psum_tensor("pout1", [128, 1024], F32))]
            pbc = Ec(nc.psum_tensor("pbc", [128, 512], F32))
            ptG = Ec(nc.psum_tensor("ptG", [128, 1024], BF16))

            def c_gp_w(gp):
                gp.wait_ge(sPe.h, a2_pe_end)
                gp.wait_ge(sAc.h, a2_ac_end)
                gp.wait_ge(sVe.h, a2_ve_end)
                gp.wait_ge(sSl[0].h, sSl[0].n)
                gp.wait_ge(sSl[1].h, sSl[1].n)
                V["w2b2"] = sB2.inc(gp.dma_start(out=b2s[:], in_=b2_h), 16)
            blk.gpsimd(c_gp_w)

            def c_sync_gate(sync):
                sync.wait_ge(sAc.h, a2_ac_end)
                sync.wait_ge(sVe.h, a2_ve_end)
                sync.wait_ge(sPe.h, a2_pe_end)
            blk.sync(c_sync_gate)

            def _preload_gelu(sc):
                _load_act_set(nc, [AF.Gelu])
            blk.scalar(_preload_gelu)

            ln2g_ld = [None] * 16
            x2g_ld = [None] * 16
            w2_done = [None] * NT
            out_ev = [None] * NT
            out_st = [None] * NT

            def emit_c_gates(g):
                def c_pe_gT(pe, g=g):
                    pe.wait_ge(sVe.h, gates_done[2 * g + 1])
                    if g == 0:
                        pe.wait_ge(sVe.h, a2_ve_end)
                        pe.wait_ge(sAc.h, a2_ac_end)
                    if g >= 1:
                        pe.wait_ge(sVe.h, V["gT2cp_%d" % (g - 1)])
                    for tl in range(2):
                        mm = pe.transpose(ptG[0:8, tl * 128:tl * 128 + 128],
                                          gates_tm[:, (2 * g + tl) * 8:
                                                   (2 * g + tl + 1) * 8],
                                          ident[:])
                    V["gTtr_%d" % g] = sPe.inc(mm)
                blk.tensor(c_pe_gT)

                def c_ve_gT(v, g=g):
                    v.wait_ge(sPe.h, V["gTtr_%d" % g])
                    if g >= 2:
                        v.wait_ge(sPe.h, w2_done[2 * (g - 2) + 1])
                    V["gT2cp_%d" % g] = sVe.inc(
                        v.tensor_copy(gT2[:, g % 2, :], ptG[0:8, 0:256]))
                blk.vector(c_ve_gT)

            def emit_c_tail(gc):
                for tl in range(2):
                    t = 2 * gc + tl

                    def c_pe_w2(pe, gc=gc, tl=tl, t=t):
                        pe.wait_ge(sVe.h, V["gm_last_%d" % gc])
                        if gc == 0 and tl == 0:
                            pe.wait_ge(sW2c.h, sW2c.n)
                            pe.wait_ge(sB2.h, V["w2b2"])
                        if t >= 2:
                            pe.wait_ge(sVe.h, out_ev[t - 2])
                        for hc in range(32):
                            lhs = hT[:, gc % 2, hc, tl * 128:tl * 128 + 128]
                            b = VOFF + hc * 1024
                            pe.matmul(pout[tl][:, 0:512], lhs, arena[:, b:b + 512],
                                      start=(hc == 0), stop=False)
                            pe.matmul(pout[tl][:, 512:1024], lhs,
                                      arena[:, b + 512:b + 1024],
                                      start=(hc == 0), stop=False)
                        pe.matmul(pout[tl][:, 0:512],
                                  gT2[0:8, gc % 2, tl * 128:tl * 128 + 128],
                                  b2s[0:8, 0:512], start=False, stop=True)
                        mm = pe.matmul(pout[tl][:, 512:1024],
                                       gT2[0:8, gc % 2, tl * 128:tl * 128 + 128],
                                       b2s[0:8, 512:1024], start=False, stop=True)
                        w2_done[t] = sPe.inc(mm)
                    blk.tensor(c_pe_w2)

                    def c_ve_out(v, gc=gc, tl=tl, t=t):
                        v.wait_ge(sPe.h, w2_done[t])
                        v.wait_ge(sGx[gc % 2].h, x2g_ld[gc])
                        if t >= 2:
                            v.wait_ge(sF[t % 2].h, out_st[t - 2])
                        out_ev[t] = sVe.inc(v.tensor_tensor(
                            outsb[:, tl, :], pout[tl][:],
                            x2g[:, gc % 2, tl * 1024:(tl + 1) * 1024], ALU.add))
                    blk.vector(c_ve_out)

                    def c_gp_out(gp, t=t, tl=tl):
                        gp.wait_ge(sVe.h, out_ev[t])
                        out_st[t] = sF[t % 2].inc(gp.dma_start(
                            out=out_h[t * 128:(t + 1) * 128, :],
                            in_=outsb[:, tl, :]), 16)
                    blk.gpsimd(c_gp_out)

            emit_c_gates(0)

            for gc in range(16):
                def c_sync_g(sync, gc=gc):
                    sync.wait_ge(sSl[0].h, x2_store[2 * gc])
                    sync.wait_ge(sSl[1].h, x2_store[2 * gc + 1])
                    if gc >= 2:
                        sync.wait_ge(sPe.h, V["h_last_%d" % (gc - 2)])
                    ln2g_ld[gc] = sGl[gc % 2].inc(sync.dma_start(
                        out=ln2Tg[:, gc % 2, :, :],
                        in_=ln2t_scr[gc].rearrange("p (kd tc) -> p kd tc", kd=KD)), 16)
                    if gc >= 2:
                        sync.wait_ge(sVe.h, out_ev[2 * (gc - 2) + 1])
                    x2g_ld[gc] = sGx[gc % 2].inc(sync.dma_start(
                        out=x2g[:, gc % 2, :], in_=x2_scr[gc]), 16)
                blk.sync(c_sync_g)

                gbanks = [pbc, ph[0], ph[1], pbc]
                for pair in range(4):
                    def c_pe_gbc(pe, gc=gc, pair=pair):
                        if pair == 0:
                            pe.wait_ge(sVe.h, V["gT2cp_%d" % gc])
                        if pair == 3:
                            pe.wait_ge(sVe.h, V["gbc_cp_%d_0" % gc])
                        if gc >= 1:
                            if pair == 1:
                                pe.wait_ge(sAc.h, V["gelu_%d_%d" % (gc - 1, 30)])
                            elif pair == 2:
                                pe.wait_ge(sAc.h, V["gelu_%d_%d" % (gc - 1, 31)])
                        for e2 in range(2):
                            e = pair * 2 + e2
                            mm = pe.matmul(gbanks[pair][:, e2 * 256:(e2 + 1) * 256],
                                           sel[0:8, e * 128:(e + 1) * 128],
                                           gT2[0:8, gc % 2, :],
                                           start=True, stop=True)
                        V["gbc_mm_%d_%d" % (gc, pair)] = sPe.inc(mm)
                    blk.tensor(c_pe_gbc)

                    def c_ve_gbc(v, gc=gc, pair=pair):
                        v.wait_ge(sPe.h, V["gbc_mm_%d_%d" % (gc, pair)])
                        V["gbc_cp_%d_%d" % (gc, pair)] = sVe.inc(v.tensor_copy(
                            gbc[:, pair * 2:(pair + 1) * 2, :].rearrange(
                                "p e c -> p (e c)"),
                            gbanks[pair][:, 0:512]))
                    blk.vector(c_ve_gbc)

                for hc in range(32):
                    def c_pe_h(pe, gc=gc, hc=hc):
                        if hc == 0:
                            pe.wait_ge(sGl[gc % 2].h, ln2g_ld[gc])
                            pe.wait_ge(sVe.h, V["gbc_cp_%d_1" % gc])
                        if hc == 1:
                            pe.wait_ge(sVe.h, V["gbc_cp_%d_2" % gc])
                        if gc == 0:
                            pe.wait_ge(sW1p[hc // 8].h, V["w1_%d" % (hc // 8)])
                        if hc >= 2:
                            pe.wait_ge(sAc.h, V["gelu_%d_%d" % (gc, hc - 2)])
                        elif gc >= 1:
                            pe.wait_ge(sAc.h, V["gelu_%d_%d" % (gc - 1, 30 + hc)])
                        for kd in range(KD):
                            mm = pe.matmul(
                                ph[hc % 2][:, 0:256],
                                arena[:, kd * 4096 + hc * 128:
                                      kd * 4096 + hc * 128 + 128],
                                ln2Tg[:, gc % 2, kd, :],
                                start=(kd == 0), stop=(kd == KD - 1))
                        V["h_%d_%d" % (gc, hc)] = sPe.inc(mm)
                        if hc == 31:
                            V["h_last_%d" % gc] = sPe.n
                    blk.tensor(c_pe_h)

                    def c_ac_gelu(s, gc=gc, hc=hc):
                        if gc == 0 and hc == 0:
                            s.wait_ge(sSl[0].h, sSl[0].n)
                            s.wait_ge(sSl[1].h, sSl[1].n)
                            s.wait_ge(sVe.h, a2_ve_end)
                        if hc == 0 and gc >= 2:
                            s.wait_ge(sPe.h, w2_done[2 * (gc - 2) + 1])
                        s.wait_ge(sPe.h, V["h_%d_%d" % (gc, hc)])
                        V["gelu_%d_%d" % (gc, hc)] = sAc.inc(
                            s.activation(hT[:, gc % 2, hc, :], ph[hc % 2][:, 0:256],
                                         AF.Gelu, bias=b1s[:, hc:hc + 1]))
                    blk.scalar(c_ac_gelu)

                    def c_ve_gm(v, gc=gc, hc=hc):
                        v.wait_ge(sAc.h, V["gelu_%d_%d" % (gc, hc)])
                        if hc == 0:
                            v.wait_ge(sVe.h, V["gbc_cp_%d_3" % gc])
                        e = hc // 4
                        V["gm_%d_%d" % (gc, hc)] = sVe.inc(v.tensor_tensor(
                            hT[:, gc % 2, hc, :], hT[:, gc % 2, hc, :],
                            gbc[:, e, :], ALU.mult))
                        if hc == 31:
                            V["gm_last_%d" % gc] = sVe.n
                    blk.vector(c_ve_gm)

                if gc >= 1:
                    emit_c_tail(gc - 1)

                if gc + 1 < 16:
                    emit_c_gates(gc + 1)

            emit_c_tail(15)

            def fin_sync(sync):
                sync.wait_ge(sF[0].h, sF[0].n)
                sync.wait_ge(sF[1].h, sF[1].n)
            blk.sync(fin_sync)

            def fin_gp(gp):
                gp.wait_ge(sF[0].h, sF[0].n)
                gp.wait_ge(sF[1].h, sF[1].n)
            blk.gpsimd(fin_gp)

    return nc


def _host_consts():
    import ml_dtypes
    ident = np.eye(128, dtype=ml_dtypes.bfloat16)
    cmask = np.triu(np.ones((128, 128), dtype=np.float32)).astype(ml_dtypes.bfloat16)
    consts = np.array([CONSTS], dtype=np.float32)
    selm = np.zeros((NEXP, NEXP * 128), dtype=np.float32)
    for e in range(NEXP):
        selm[e, e * 128:(e + 1) * 128] = 1.0
    selm = selm.astype(ml_dtypes.bfloat16)
    return ident, cmask, consts, selm


def kernel(L0_attn_x, L0_attn_Wq, L0_attn_Wk, L0_attn_Wv,
           L0_ffn_w1, L0_ffn_b1, L0_ffn_w2, L0_ffn_b2,
           L0_ffn_centroids, L0_ffn_w_route):
    x = np.ascontiguousarray(np.asarray(L0_attn_x, dtype=np.float32))
    wqT = np.ascontiguousarray(np.asarray(L0_attn_Wq, dtype=np.float32).T)
    wkT = np.ascontiguousarray(np.asarray(L0_attn_Wk, dtype=np.float32).T)
    wvT = np.ascontiguousarray(np.asarray(L0_attn_Wv, dtype=np.float32).T)
    w1s = np.ascontiguousarray(
        np.asarray(L0_ffn_w1, dtype=np.float32).transpose(1, 0, 2).reshape(D, DFF))
    w2s = np.ascontiguousarray(np.asarray(L0_ffn_w2, dtype=np.float32).reshape(DFF, D))
    b1m = np.ascontiguousarray(
        np.asarray(L0_ffn_b1, dtype=np.float32).reshape(DFF).reshape(32, 128).T)
    b2 = np.ascontiguousarray(np.asarray(L0_ffn_b2, dtype=np.float32))
    cent = np.ascontiguousarray(np.asarray(L0_ffn_centroids, dtype=np.float32))
    wrt = np.ascontiguousarray(np.asarray(L0_ffn_w_route, dtype=np.float32))
    ident, cmask, consts, selm = _host_consts()

    nc = build(debug_taps=False)
    shared = dict(wqT=wqT, wkT=wkT, wvT=wvT, w1s=w1s, w2s=w2s, b1m=b1m, b2=b2,
                  cent=cent, wrt=wrt, ident=ident, cmask=cmask,
                  consts=consts, sel=selm)
    in_maps = [dict(shared, x=np.ascontiguousarray(x[b])) for b in range(8)]
    res = run_bass_kernel_spmd(nc, in_maps, core_ids=list(range(8)))
    return np.stack([r["out"] for r in res.results], axis=0)


# version 20 (final)
# speedup vs baseline: 1.1802x; 1.0284x over previous
import numpy as np
from contextlib import ExitStack

import concourse.bass as bass
import concourse.mybir as mybir
from concourse.bass_utils import run_bass_kernel_spmd
from concourse.hw_specs import get_activation_tables


def _load_act_set(nc, funcs):
    for idx, fs in enumerate(get_activation_tables(nc.m.arch).values()):
        if all(f in fs for f in funcs):
            nc.scalar.add_instruction(mybir.InstLoadActFuncSet(
                name=nc.get_next_instruction_name(),
                act_func_set_id=idx, ins=[], outs=[]))
            return
    raise ValueError(f"no table set contains {funcs}")

F32 = mybir.dt.float32
BF16 = mybir.dt.bfloat16
AF = mybir.ActivationFunctionType
ALU = mybir.AluOpType
AX = mybir.AxisListType

NTOK = 4096
D = 1024
NT = 32
KD = 8
DFF = 4096
NEXP = 8
EPS = 1e-5

CONSTS = [-0.5, EPS, 1024.0, 1.0 / 32.0, -1.0, 1.0 / 1024.0, 1e-24, -EPS]

VOFF = 32768
VSTRIDE = 1024
ONESCOL = 65536


def build(debug_taps=False):
    nc = bass.Bass("TRN2", target_bir_lowering=False)

    x_h = nc.dram_tensor("x", [NTOK, D], F32, kind="ExternalInput").ap()
    wqT_h = nc.dram_tensor("wqT", [D, D], F32, kind="ExternalInput").ap()
    wkT_h = nc.dram_tensor("wkT", [D, D], F32, kind="ExternalInput").ap()
    wvT_h = nc.dram_tensor("wvT", [D, D], F32, kind="ExternalInput").ap()
    w1s_h = nc.dram_tensor("w1s", [D, DFF], F32, kind="ExternalInput").ap()
    w2s_h = nc.dram_tensor("w2s", [DFF, D], F32, kind="ExternalInput").ap()
    b1m_h = nc.dram_tensor("b1m", [128, 32], F32, kind="ExternalInput").ap()
    b2_h = nc.dram_tensor("b2", [NEXP, D], F32, kind="ExternalInput").ap()
    cent_h = nc.dram_tensor("cent", [NEXP, D], F32, kind="ExternalInput").ap()
    wrt_h = nc.dram_tensor("wrt", [NEXP, D], F32, kind="ExternalInput").ap()
    ident_h = nc.dram_tensor("ident", [128, 128], BF16, kind="ExternalInput").ap()
    sel_h = nc.dram_tensor("sel", [NEXP, NEXP * 128], BF16, kind="ExternalInput").ap()
    cmask_h = nc.dram_tensor("cmask", [128, 128], BF16, kind="ExternalInput").ap()
    consts_h = nc.dram_tensor("consts", [1, len(CONSTS)], F32, kind="ExternalInput").ap()

    skind = "ExternalOutput" if debug_taps else "Internal"
    qt_scr = nc.dram_tensor("qt_scr", [8, 128, 4096], BF16, kind=skind).ap()
    ln2t_scr = nc.dram_tensor("ln2t_scr", [16, 128, 2048], BF16, kind=skind).ap()
    x2_scr = nc.dram_tensor("x2_scr", [16, 128, 2048], F32, kind=skind).ap()

    out_h = nc.dram_tensor("out", [NTOK, D], F32, kind="ExternalOutput").ap()

    V = {}

    with ExitStack() as topctx:
        E = topctx.enter_context

        arena = E(nc.sbuf_tensor("arena", [128, 65544], BF16))
        gates_tm = E(nc.sbuf_tensor("gates_tm", [128, 256], BF16))
        ident = E(nc.sbuf_tensor("ident_s", [128, 128], BF16))
        cmask = E(nc.sbuf_tensor("cmask_s", [128, 128], BF16))
        cw = E(nc.sbuf_tensor("cw_s", [128, 128], BF16))
        sel = E(nc.sbuf_tensor("sel_s", [NEXP, NEXP * 128], BF16))
        b1s = E(nc.sbuf_tensor("b1_s", [128, 32], F32))
        consts = E(nc.sbuf_tensor("consts_s", [1, len(CONSTS)], F32))
        stt = E(nc.sbuf_tensor("stt", [128, 32], F32))
        rt16 = E(nc.sbuf_tensor("rt16", [128, 16], F32))
        logit = E(nc.sbuf_tensor("logit", [128, 8], F32))
        top8 = E(nc.sbuf_tensor("top8", [128, 8], F32))
        expl = E(nc.sbuf_tensor("expl", [128, 8], F32))
        maskg = E(nc.sbuf_tensor("maskg", [128, 8], F32))
        gated = E(nc.sbuf_tensor("gated", [128, 8], F32))
        cstat = E(nc.sbuf_tensor("cstat", [128, 4], F32))

        class Ctr:
            def __init__(self, name):
                self.h = E(nc.semaphore(name))
                self.n = 0

            def inc(self, instr, k=1):
                instr.then_inc(self.h, k)
                self.n += k
                return self.n

        sIn = Ctr("sIn")
        sWt = [Ctr("sWt%d" % i) for i in range(3)]
        sSp = Ctr("sSp")
        sPe = Ctr("sPe")
        sAc = Ctr("sAc")
        sVe = Ctr("sVe")
        sX = [Ctr("sX0"), Ctr("sX1")]
        sQb = [Ctr("sQ0"), Ctr("sQ1")]
        sXr = [Ctr("sXr0"), Ctr("sXr1")]
        sSl = [Ctr("sSl0"), Ctr("sSl1")]
        sGl = [Ctr("sGl0"), Ctr("sGl1")]
        sGx = [Ctr("sGx0"), Ctr("sGx1")]
        sW1p = [Ctr("sW1p%d" % i) for i in range(4)]
        sW2c = Ctr("sW2c")
        sB2 = Ctr("sB2")
        sSel = Ctr("sSel")
        sF = [Ctr("sF0"), Ctr("sF1")]
        sMs = Ctr("sMs")
        sGp = Ctr("sGp")

        blk = E(nc.Block())

        def _preload_lnexp(sc):
            _load_act_set(nc, [AF.Ln, AF.Exp, AF.Identity])
        blk.scalar(_preload_lnexp)

        with ExitStack() as pset:
            Es = pset.enter_context
            centf = Es(nc.sbuf_tensor("centf", [NEXP, D], F32))
            wrtf = Es(nc.sbuf_tensor("wrtf", [NEXP, D], F32))
            cscr = Es(nc.sbuf_tensor("cscr", [NEXP, D], F32))
            cnormbf = Es(nc.sbuf_tensor("cnormbf", [NEXP, D], BF16))
            wrtbf = Es(nc.sbuf_tensor("wrtbf", [NEXP, D], BF16))
            ptmp = Es(nc.psum_tensor("ptmp", [128, 1024], BF16))

            def setup_sync(sync):
                for dst, src in [(ident, ident_h), (cmask, cmask_h),
                                 (b1s, b1m_h), (consts, consts_h),
                                 (centf, cent_h), (wrtf, wrt_h),
                                 (sel, sel_h)]:
                    sIn.inc(sync.dma_start(out=dst[:], in_=src), 16)
                V["setup_in"] = sIn.n
            blk.sync(setup_sync)

            for i, cv in enumerate(CONSTS):
                nc.const_aps.register(F32, cv, consts[0:1, i:i + 1])

            def setup_vec(v):
                v.wait_ge(sIn.h, V["setup_in"])
                w = sVe.inc(v.scalar_tensor_tensor(
                    cscr[:], centf[:], 1.0, centf[:],
                    ALU.bypass, ALU.mult, accum_out=cstat[0:8, 0:1]))
                v.wait_ge(sVe.h, w)
                V["cstat"] = sVe.inc(v.tensor_scalar(
                    cstat[0:8, 3:4], cstat[0:8, 0:1], 1e-24, None, ALU.add))
            blk.vector(setup_vec)

            def setup_scal(s):
                s.wait_ge(sVe.h, V["cstat"])
                w = sAc.inc(s.activation(cstat[0:8, 1:2], cstat[0:8, 3:4], AF.Ln))
                s.wait_ge(sAc.h, w)
                V["c_rc"] = sAc.inc(
                    s.activation(cstat[0:8, 2:3], cstat[0:8, 1:2], AF.Exp, scale=-0.5))
            blk.scalar(setup_scal)

            def setup_vec2(v):
                v.wait_ge(sAc.h, V["c_rc"])
                v.tensor_scalar(cnormbf[:], centf[:], cstat[0:8, 2:3], None, ALU.mult)
                V["cnorm"] = sVe.inc(v.tensor_copy(wrtbf[:], wrtf[:]))
            blk.vector(setup_vec2)

            def cw_pe(pe):
                pe.wait_ge(sVe.h, V["cnorm"])
                for kd in range(KD):
                    pe.transpose(ptmp[:, kd * 16:kd * 16 + 8],
                                 cnormbf[0:8, kd * 128:(kd + 1) * 128], ident[0:8, 0:8])
                    mm = pe.transpose(ptmp[:, kd * 16 + 8:kd * 16 + 16],
                                      wrtbf[0:8, kd * 128:(kd + 1) * 128], ident[0:8, 0:8])
                V["cw_tr"] = sPe.inc(mm)
            blk.tensor(cw_pe)

            def cw_ve(v):
                v.wait_ge(sPe.h, V["cw_tr"])
                V["cw"] = sVe.inc(v.tensor_copy(cw[:], ptmp[:, 0:128]))
            blk.vector(cw_ve)

            if debug_taps:
                cw0dump = nc.dram_tensor("cw0dump", [128, 128], BF16,
                                         kind="ExternalOutput").ap()
                cn0dump = nc.dram_tensor("cn0dump", [NEXP, D], BF16,
                                         kind="ExternalOutput").ap()

                def dump_cw0(gp):
                    gp.wait_ge(sVe.h, V["cw"])
                    sSp.inc(gp.dma_start(out=cw0dump, in_=cw[:]), 16)
                    sSp.inc(gp.dma_start(out=cn0dump, in_=cnormbf[:]), 16)
                    V["cw0d"] = sSp.n
                blk.gpsimd(dump_cw0)

        with ExitStack() as pa1:
            Ea = pa1.enter_context
            wqkv = Ea(nc.sbuf_tensor("wqkv", [128, 3, KD, D], BF16))
            scr = Ea(nc.sbuf_tensor("scr", [128, D], F32))
            xbuf = Ea(nc.sbuf_tensor("xbuf", [128, 2, D], F32))
            xln = Ea(nc.sbuf_tensor("xln", [128, 2, D], BF16))
            xlnT = Ea(nc.sbuf_tensor("xlnT", [128, 2, KD, 256], BF16))
            qstage = Ea(nc.sbuf_tensor("qstage", [128, KD, 256], BF16))
            pqk = [Ea(nc.psum_tensor("pqk0", [128, 512], F32)),
                   Ea(nc.psum_tensor("pqk1", [128, 512], F32))]
            pv = [Ea(nc.psum_tensor("pv0", [128, 1024], F32)),
                  Ea(nc.psum_tensor("pv1", [128, 1024], F32))]
            ptr = [Ea(nc.psum_tensor("ptr0", [128, 1024], BF16)),
                   Ea(nc.psum_tensor("ptr1", [128, 1024], BF16))]

            def a1_gpsimd(g):
                g.wait_ge(sVe.h, V["cw"])
                for i, wh in enumerate([wqT_h, wkT_h, wvT_h]):
                    V["wqkv%d" % i] = sWt[i].inc(g.dma_start(
                        out=wqkv[:, i, :, :],
                        in_=wh.rearrange("(kd p) m -> p kd m", p=128)), 16)
                V["vones"] = sMs.inc(
                    g.memset(arena[:, ONESCOL:ONESCOL + 1], 1.0))
            blk.gpsimd(a1_gpsimd)

            x_ld = [None] * NT
            xln_done = [None] * NT
            sq_done = [None] * NT
            stats_done = [None] * NT
            negmr_done = [None] * NT
            tr_done = [None] * NT
            xlt_done = [None] * NT
            qk_done = {}
            qk_ev = {}
            v_done = [None] * NT
            v_ev = [None] * NT
            qspill = [None] * 16

            def emit_a1_head_sync(t):
                def a1_sync_t(sync, t=t):
                    if t == 0:
                        sync.wait_ge(sVe.h, V["cw"])
                    if t >= 2:
                        sync.wait_ge(sGp.h, xln_done[t - 2])
                    x_ld[t] = sX[t % 2].inc(sync.dma_start(
                        out=xbuf[:, t % 2, :],
                        in_=x_h[t * 128:(t + 1) * 128, :]), 16)
                blk.sync(a1_sync_t)

            def emit_a1_head_v1(g):
                sb = 16 * (g % 2)

                def a1_v1(v, g=g, sb=sb):
                    v.wait_ge(sX[0].h, x_ld[2 * g])
                    v.wait_ge(sX[1].h, x_ld[2 * g + 1])
                    if g >= 2:
                        v.wait_ge(sGp.h, xln_done[2 * (g - 2) + 1])
                    w = sVe.inc(v.reduce_sum(stt[:, sb:sb + 2], xbuf[:, :, :], AX.X))
                    v.wait_ge(sVe.h, w)
                    w = sVe.inc(v.scalar_tensor_tensor(
                        scr[:], xbuf[:, 0, :], 1.0, xbuf[:, 0, :],
                        ALU.bypass, ALU.mult, accum_out=stt[:, sb + 2:sb + 3]))
                    v.wait_ge(sVe.h, w)
                    w = sVe.inc(v.scalar_tensor_tensor(
                        scr[:], xbuf[:, 1, :], 1.0, xbuf[:, 1, :],
                        ALU.bypass, ALU.mult, accum_out=stt[:, sb + 3:sb + 4]))
                    v.wait_ge(sVe.h, w)
                    w = sVe.inc(v.tensor_tensor(stt[:, sb + 4:sb + 6],
                                                stt[:, sb:sb + 2],
                                                stt[:, sb:sb + 2], ALU.mult))
                    v.wait_ge(sVe.h, w)
                    w = sVe.inc(v.tensor_scalar(stt[:, sb + 4:sb + 6],
                                                stt[:, sb + 4:sb + 6],
                                                1.0 / 1048576.0, None, ALU.mult))
                    v.wait_ge(sVe.h, w)
                    w = sVe.inc(v.tensor_scalar(stt[:, sb + 6:sb + 8],
                                                stt[:, sb + 2:sb + 4],
                                                1.0 / 1024.0, EPS,
                                                ALU.mult, ALU.add))
                    v.wait_ge(sVe.h, w)
                    V["var_%d" % g] = sVe.inc(v.tensor_tensor(
                        stt[:, sb + 8:sb + 10], stt[:, sb + 6:sb + 8],
                        stt[:, sb + 4:sb + 6], ALU.subtract))
                blk.vector(a1_v1)

            def emit_a1_head_s1(g):
                sb = 16 * (g % 2)

                def a1_s1(s, g=g, sb=sb):
                    s.wait_ge(sVe.h, V["var_%d" % g])
                    w = sAc.inc(s.activation(stt[:, sb + 10:sb + 12],
                                             stt[:, sb + 8:sb + 10], AF.Ln))
                    s.wait_ge(sAc.h, w)
                    V["rstd_%d" % g] = sAc.inc(
                        s.activation(stt[:, sb + 12:sb + 14],
                                     stt[:, sb + 10:sb + 12], AF.Exp, scale=-0.5))
                blk.scalar(a1_s1)

            def emit_a1_head_v2(g):
                sb = 16 * (g % 2)

                def a1_v2(v, g=g, sb=sb):
                    v.wait_ge(sAc.h, V["rstd_%d" % g])
                    w = sVe.inc(v.tensor_tensor(stt[:, sb + 14:sb + 16],
                                                stt[:, sb:sb + 2],
                                                stt[:, sb + 12:sb + 14], ALU.mult))
                    v.wait_ge(sVe.h, w)
                    V["negmr_%d" % g] = sVe.inc(
                        v.tensor_scalar(stt[:, sb + 14:sb + 16],
                                        stt[:, sb + 14:sb + 16],
                                        -1.0 / 1024.0, None, ALU.mult))
                blk.vector(a1_v2)

            def emit_a1_head_s2(g):
                sb = 16 * (g % 2)

                def a1_s2(gp, g=g, sb=sb):
                    gp.wait_ge(sVe.h, V["negmr_%d" % g])
                    for tl in range(2):
                        t = 2 * g + tl
                        if t >= 2:
                            gp.wait_ge(sPe.h, tr_done[t - 2])
                        xln_done[t] = sGp.inc(
                            gp.tensor_scalar(xln[:, t % 2, :], xbuf[:, t % 2, :],
                                             stt[:, sb + 12 + tl:sb + 13 + tl],
                                             stt[:, sb + 14 + tl:sb + 15 + tl],
                                             ALU.mult, ALU.add))
                blk.gpsimd(a1_s2)

            def emit_a1_trx(t):
                def a1_pe_t(pe, t=t):
                    pe.wait_ge(sGp.h, xln_done[t])
                    if t >= 2:
                        pe.wait_ge(sVe.h, xlt_done[t - 2])
                    for kd in range(KD):
                        mm = pe.transpose(ptr[t % 2][:, kd * 128:(kd + 1) * 128],
                                          xln[:, t % 2, kd * 128:(kd + 1) * 128],
                                          ident[:])
                    tr_done[t] = sPe.inc(mm)
                blk.tensor(a1_pe_t)

                def a1_vec3_t(v, t=t):
                    v.wait_ge(sPe.h, tr_done[t])
                    if t // 2 >= 2:
                        v.wait_ge(sPe.h, v_done[2 * (t // 2 - 2) + 1])
                    xlt_done[t] = sVe.inc(v.tensor_copy(
                        xlnT[:, (t // 2) % 2, :, (t % 2) * 128:(t % 2) * 128 + 128],
                        ptr[t % 2][:].rearrange("p (kd c) -> p kd c", kd=KD)))
                blk.vector(a1_vec3_t)

            def emit_a1_vburst(g):
                for tl in range(2):
                    def a1_pe_v(pe, g=g, tl=tl):
                        if g == 0 and tl == 0:
                            pe.wait_ge(sWt[2].h, V["wqkv2"])
                        if g >= 1:
                            pe.wait_ge(sAc.h, v_ev[2 * (g - 1) + tl])
                        for kd in range(KD):
                            for half in range(2):
                                mm = pe.matmul(
                                    pv[tl][:, half * 512:(half + 1) * 512],
                                    xlnT[:, g % 2, kd, tl * 128:(tl + 1) * 128],
                                    wqkv[:, 2, kd, half * 512:(half + 1) * 512],
                                    start=(kd == 0), stop=(kd == KD - 1))
                        v_done[2 * g + tl] = sPe.inc(mm)
                    blk.tensor(a1_pe_v)

                    def a1_sc_v(s, g=g, tl=tl):
                        j = 2 * g + tl
                        s.wait_ge(sPe.h, v_done[j])
                        base = VOFF + j * VSTRIDE
                        v_ev[j] = sAc.inc(
                            s.copy(arena[:, base:base + 1024], pv[tl][:]))
                    blk.scalar(a1_sc_v)

                def a1_gp_g(gp, g=g):
                    gq, sub = g // 2, g % 2
                    gp.wait_ge(sAc.h, qk_ev[(g, 7)])
                    dst = qt_scr[gq].rearrange("p (m tc) -> p m tc", m=KD)[
                        :, :, sub * 256:sub * 256 + 256]
                    qspill[g] = sSp.inc(
                        gp.dma_start(out=dst, in_=qstage[:]), 16)
                blk.gpsimd(a1_gp_g)

            def emit_a1_qkburst(g):
                for u in range(16):
                    proj, m = u // 8, u % 8

                    def a1_pe_u(pe, g=g, u=u, proj=proj, m=m):
                        if u == 0:
                            pe.wait_ge(sVe.h, xlt_done[2 * g + 1])
                        if g == 0 and u % 8 == 0:
                            pe.wait_ge(sWt[proj].h, V["wqkv%d" % proj])
                        prev = qk_ev.get((g, u - 2)) if u >= 2 else (
                            qk_ev.get((g - 1, 14 + u)) if g >= 1 else None)
                        if prev is not None:
                            pe.wait_ge(sAc.h, prev)
                        for kd in range(KD):
                            mm = pe.matmul(
                                pqk[u % 2][:, 0:256],
                                wqkv[:, proj, kd, m * 128:(m + 1) * 128],
                                xlnT[:, g % 2, kd, :],
                                start=(kd == 0), stop=(kd == KD - 1))
                        qk_done[(g, u)] = sPe.inc(mm)
                    blk.tensor(a1_pe_u)

                    def a1_sc_u(s, g=g, u=u, proj=proj, m=m):
                        s.wait_ge(sPe.h, qk_done[(g, u)])
                        if u == 0 and g >= 1:
                            s.wait_ge(sSp.h, qspill[g - 1])
                        if proj == 0:
                            dst = qstage[:, m, :]
                        else:
                            dst = arena[:, m * 4096 + g * 256:
                                        m * 4096 + g * 256 + 256]
                        qk_ev[(g, u)] = sAc.inc(
                            s.copy(dst, pqk[u % 2][:, 0:256]))
                    blk.scalar(a1_sc_u)

            emit_a1_head_sync(0)
            emit_a1_head_sync(1)
            emit_a1_head_v1(0)
            emit_a1_head_s1(0)
            emit_a1_head_v2(0)
            emit_a1_head_s2(0)
            for g in range(16):
                emit_a1_trx(2 * g)
                emit_a1_trx(2 * g + 1)
                if 2 * g + 2 < NT:
                    emit_a1_head_sync(2 * g + 2)
                    emit_a1_head_sync(2 * g + 3)
                    emit_a1_head_v1(g + 1)
                if g >= 1:
                    emit_a1_vburst(g - 1)
                if 2 * g + 2 < NT:
                    emit_a1_head_s1(g + 1)
                    emit_a1_head_v2(g + 1)
                    emit_a1_head_s2(g + 1)
                emit_a1_qkburst(g)
            emit_a1_vburst(15)

            a1_pe_last = sPe.n
            a1_ac_last = sAc.n
            a1_last_ev = sVe.n

        with ExitStack() as pa2:
            Ea = pa2.enter_context
            probsT = Ea(nc.sbuf_tensor("probsT", [128, 32, 512], BF16))
            qbuf = Ea(nc.sbuf_tensor("qbuf", [128, 2, KD, 512], BF16))
            xr = Ea(nc.sbuf_tensor("xr", [128, 2, D], F32))
            x2b = Ea(nc.sbuf_tensor("x2b", [128, 2, D], F32))
            ln2 = Ea(nc.sbuf_tensor("ln2", [128, 2, D], BF16))
            ln2Ts = Ea(nc.sbuf_tensor("ln2Ts", [128, 2, D], BF16))
            psc = [Ea(nc.psum_tensor("psc0", [128, 512], F32)),
                   Ea(nc.psum_tensor("psc1", [128, 512], F32)),
                   Ea(nc.psum_tensor("psc2", [128, 512], F32))]
            pctx = Ea(nc.psum_tensor("pctx", [128, 1536], F32))
            ptrA = Ea(nc.psum_tensor("ptrA", [128, 1024], BF16))
            prt = Ea(nc.psum_tensor("prt", [128, 512], F32))

            q_ld = [None] * 8
            sc_seq = []
            xr_ld = [None] * NT
            mask_done = {}
            ctx_done = [None] * NT
            ctx_ev = [None] * NT
            ln2_done = [None] * NT
            ln2T_tr = [None] * NT
            ln2T_cp = [None] * NT
            rt_done = [None] * NT
            gates_done = [None] * NT
            spill_done = [None] * NT
            x2_store = [None] * NT

            def emit_a2_tailA(t):
                def a2_pe_tr(pe, t=t):
                    pe.wait_ge(sGp.h, ln2_done[t])
                    if t >= 1:
                        pe.wait_ge(sVe.h, ln2T_cp[t - 1])
                    for kd in range(KD):
                        mm = pe.transpose(ptrA[:, kd * 128:(kd + 1) * 128],
                                          ln2[:, t % 2, kd * 128:(kd + 1) * 128],
                                          ident[:])
                    ln2T_tr[t] = sPe.inc(mm)
                blk.tensor(a2_pe_tr)

                def a2_ve3_t(v, t=t):
                    v.wait_ge(sPe.h, ln2T_tr[t])
                    if t >= 2:
                        v.wait_ge(sSl[t % 2].h, x2_store[t - 2])
                        v.wait_ge(sPe.h, rt_done[t - 2])
                    ln2T_cp[t] = sVe.inc(
                        v.tensor_copy(ln2Ts[:, t % 2, :], ptrA[:]))
                blk.vector(a2_ve3_t)

                def a2_gp_t(gp, t=t):
                    gc, sub = t // 2, t % 2
                    gp.wait_ge(sVe.h, ln2T_cp[t])
                    dst = ln2t_scr[gc].rearrange("p (kd tc) -> p kd tc", kd=KD)[
                        :, :, sub * 128:sub * 128 + 128]
                    src = ln2Ts[:, t % 2, :].rearrange("p (kd c) -> p kd c", kd=KD)
                    spill_done[t] = sSl[t % 2].inc(gp.dma_start(out=dst, in_=src), 16)
                    gp.wait_ge(sVe.h, V["st2_%d" % t])
                    x2_store[t] = sSl[t % 2].inc(gp.dma_start(
                        out=x2_scr[gc, :, sub * 1024:sub * 1024 + 1024],
                        in_=x2b[:, t % 2, :]), 16)
                blk.gpsimd(a2_gp_t)


            def emit_a2_tailB(t):
                def a2_pe_rt(pe, t=t):
                    pe.wait_ge(sVe.h, ln2T_cp[t])
                    if t >= 1:
                        pe.wait_ge(sVe.h, V["rtcp_%d" % (t - 1)])
                    for kd in range(KD):
                        mm = pe.matmul(prt[:, 0:16],
                                       ln2Ts[:, t % 2, kd * 128:(kd + 1) * 128],
                                       cw[:, kd * 16:kd * 16 + 16],
                                       start=(kd == 0), stop=(kd == KD - 1))
                    rt_done[t] = sPe.inc(mm)
                blk.tensor(a2_pe_rt)

                def a2_ve4_t(v, t=t):
                    v.wait_ge(sPe.h, rt_done[t])
                    v.wait_ge(sAc.h, V["rr_%d" % t])
                    V["rtcp_%d" % t] = sVe.inc(v.tensor_copy(rt16[:], prt[:, 0:16]))
                    v.wait_ge(sVe.h, V["rtcp_%d" % t])
                    rc = 16 + (t % 4)
                    w = sVe.inc(v.scalar_tensor_tensor(
                        logit[:], rt16[:, 0:8], stt[:, rc:rc + 1],
                        rt16[:, 8:16], ALU.mult, ALU.add))
                    v.wait_ge(sVe.h, w)
                    w = sVe.inc(v.max(top8[:], logit[:]))
                    v.wait_ge(sVe.h, w)
                    V["negm1_%d" % t] = sVe.inc(
                        v.tensor_scalar(stt[:, 12:13], top8[:, 0:1], -1.0,
                                        None, ALU.mult))
                blk.vector(a2_ve4_t)

                def a2_ac3_t(s, t=t):
                    s.wait_ge(sVe.h, V["negm1_%d" % t])
                    V["expl_%d" % t] = sAc.inc(
                        s.activation(expl[:], logit[:], AF.Exp, bias=stt[:, 12:13]))
                blk.scalar(a2_ac3_t)

                def a2_ve5_t(v, t=t):
                    v.wait_ge(sAc.h, V["expl_%d" % t])
                    w = sVe.inc(v.tensor_scalar(maskg[:], logit[:], top8[:, 1:2],
                                                None, ALU.is_ge))
                    v.wait_ge(sVe.h, w)
                    w = sVe.inc(v.tensor_tensor(gated[:], expl[:], maskg[:],
                                                ALU.mult))
                    v.wait_ge(sVe.h, w)
                    w = sVe.inc(v.tensor_scalar(expl[:], gated[:], 1.0, 0.0,
                                                ALU.mult, ALU.add,
                                                accum_out=stt[:, 13:14]))
                    v.wait_ge(sVe.h, w)
                    w = sVe.inc(v.reciprocal(stt[:, 14:15], stt[:, 13:14]))
                    v.wait_ge(sVe.h, w)
                    gates_done[t] = sVe.inc(
                        v.tensor_scalar(gates_tm[:, t * 8:(t + 1) * 8], gated[:],
                                        stt[:, 14:15], None, ALU.mult))
                blk.vector(a2_ve5_t)

            for g2 in range(8):
                JN = 4 * g2 + 4

                def a2_sync_q(sync, g2=g2):
                    if g2 <= 1:
                        sync.wait_ge(sAc.h, a1_ac_last)
                        sync.wait_ge(sPe.h, a1_pe_last)
                        sync.wait_ge(sVe.h, a1_last_ev)
                        sync.wait_ge(sGp.h, xln_done[NT - 1])
                        sync.wait_ge(sSp.h, qspill[15])
                    else:
                        sync.wait_ge(sPe.h, V["sc_last_%d" % (g2 - 2)])
                    q_ld[g2] = sQb[g2 % 2].inc(sync.dma_start(
                        out=qbuf[:, g2 % 2, :, :], in_=qt_scr[g2].rearrange(
                            "p (m tc) -> p m tc", m=KD)), 16)
                blk.sync(a2_sync_q)

                for j in range(JN):
                    jj = len(sc_seq)
                    sc_seq.append((g2, j))

                    def a2_pe_sc(pe, g2=g2, j=j, JN=JN, jj=jj):
                        if j == 0:
                            pe.wait_ge(sQb[g2 % 2].h, q_ld[g2])
                        if jj >= 3:
                            pg, pj = sc_seq[jj - 3]
                            pe.wait_ge(sAc.h, V["exp_%d_%d" % (pg, pj)])
                        for kd in range(KD):
                            mm = pe.matmul(
                                psc[jj % 3][:],
                                arena[:, kd * 4096 + j * 128:kd * 4096 + j * 128 + 128],
                                qbuf[:, g2 % 2, kd, :],
                                start=(kd == 0), stop=(kd == KD - 1))
                        V["sc_%d_%d" % (g2, j)] = sPe.inc(mm)
                        if j == JN - 1:
                            V["sc_last_%d" % g2] = sPe.n
                    blk.tensor(a2_pe_sc)

                    def a2_ac_exp(s, g2=g2, j=j, jj=jj):
                        s.wait_ge(sPe.h, V["sc_%d_%d" % (g2, j)])
                        V["exp_%d_%d" % (g2, j)] = sAc.inc(
                            s.activation(probsT[:, j, :], psc[jj % 3][:],
                                         AF.Exp, scale=1.0 / 32.0))
                    blk.scalar(a2_ac_exp)

                def a2_ve_mask(v, g2=g2):
                    for j in range(4 * g2, 4 * g2 + 4):
                        off = (j - 4 * g2) * 128
                        v.wait_ge(sAc.h, V["exp_%d_%d" % (g2, j)])
                        mask_done[j] = sVe.inc(v.tensor_tensor(
                            probsT[:, j, off:off + 128],
                            probsT[:, j, off:off + 128],
                            cmask[:], ALU.mult))
                blk.vector(a2_ve_mask)

                for tl in range(4):
                    t = 4 * g2 + tl

                    def a2_sync_x(sync, t=t):
                        if t >= 2:
                            sync.wait_ge(sVe.h, V["st2_%d" % (t - 2)])
                        xr_ld[t] = sXr[t % 2].inc(sync.dma_start(
                            out=xr[:, t % 2, :],
                            in_=x_h[t * 128:(t + 1) * 128, :]), 16)
                    blk.sync(a2_sync_x)

                    def a2_pe_pv(pe, t=t, g2=g2, tl=tl):
                        jmax = 4 * g2 + tl
                        pe.wait_ge(sAc.h, V["exp_%d_%d" % (g2, jmax)])
                        pe.wait_ge(sVe.h, mask_done[jmax])
                        if t == 0:
                            pe.wait_ge(sMs.h, V["vones"])
                        if t >= 1:
                            pe.wait_ge(sVe.h, ctx_ev[t - 1])
                        for j in range(jmax + 1):
                            lhs = probsT[:, j, tl * 128:tl * 128 + 128]
                            base = VOFF + j * VSTRIDE
                            st, sp = (j == 0), (j == jmax)
                            pe.matmul(pctx[:, 0:512], lhs,
                                      arena[:, base:base + 512], start=st, stop=sp)
                            pe.matmul(pctx[:, 512:1024], lhs,
                                      arena[:, base + 512:base + 1024],
                                      start=st, stop=sp)
                            mm = pe.matmul(pctx[:, 1024:1025], lhs,
                                           arena[:, ONESCOL:ONESCOL + 1],
                                           start=st, stop=sp)
                        ctx_done[t] = sPe.inc(mm)
                    blk.tensor(a2_pe_pv)

                    def a2_ve_t(v, t=t):
                        v.wait_ge(sPe.h, ctx_done[t])
                        v.wait_ge(sXr[t % 2].h, xr_ld[t])
                        if t >= 2:
                            v.wait_ge(sSl[t % 2].h, x2_store[t - 2])
                            v.wait_ge(sGp.h, ln2_done[t - 2])
                        if t >= 1:
                            v.wait_ge(sAc.h, V["rstd2_%d" % (t - 1)])
                        w = sVe.inc(v.reciprocal(stt[:, 15:16], pctx[:, 1024:1025]))
                        v.wait_ge(sVe.h, w)
                        x2t = x2b[:, t % 2, :]
                        ctx_ev[t] = sVe.inc(
                            v.scalar_tensor_tensor(x2t, pctx[:, 0:1024], stt[:, 15:16],
                                                   xr[:, t % 2, :], ALU.mult, ALU.add))
                        v.wait_ge(sVe.h, ctx_ev[t])
                        v.reduce_sum(stt[:, 0:1], x2t, AX.X)
                        w = sVe.inc(v.scalar_tensor_tensor(
                            xr[:, t % 2, :], x2t, 1.0, x2t, ALU.bypass, ALU.mult,
                            accum_out=stt[:, 1:2]))
                        v.wait_ge(sVe.h, w)
                        w = sVe.inc(v.tensor_scalar(stt[:, 2:3], stt[:, 0:1],
                                                    1.0 / 1024.0, None, ALU.mult))
                        v.wait_ge(sVe.h, w)
                        w = sVe.inc(v.tensor_scalar(stt[:, 7:8], stt[:, 2:3],
                                                    stt[:, 2:3], -EPS, ALU.mult, ALU.add))
                        v.wait_ge(sVe.h, w)
                        V["st2_%d" % t] = sVe.inc(
                            v.tensor_scalar(stt[:, 3:4], stt[:, 1:2], 1.0 / 1024.0,
                                            stt[:, 7:8], ALU.mult, ALU.subtract))
                    blk.vector(a2_ve_t)

                    def a2_ac_t(s, t=t):
                        s.wait_ge(sVe.h, V["st2_%d" % t])
                        if t >= 1:
                            s.wait_ge(sGp.h, ln2_done[t - 1])
                        w = sAc.inc(s.activation(stt[:, 4:5], stt[:, 3:4], AF.Ln))
                        s.wait_ge(sAc.h, w)
                        V["rstd2_%d" % t] = sAc.inc(
                            s.activation(stt[:, 5:6], stt[:, 4:5], AF.Exp, scale=-0.5))
                    blk.scalar(a2_ac_t)

                    def a2_ve2_t(v, t=t):
                        v.wait_ge(sAc.h, V["rstd2_%d" % t])
                        V["negmr2_%d" % t] = sVe.inc(
                            v.tensor_scalar(stt[:, 6:7], stt[:, 2:3], -1.0,
                                            stt[:, 5:6], ALU.mult, ALU.mult))
                        w = sVe.inc(v.tensor_scalar(stt[:, 8:9], stt[:, 3:4], -EPS,
                                                    stt[:, 5:6], ALU.add, ALU.mult))
                        v.wait_ge(sVe.h, w)
                        w = sVe.inc(v.tensor_scalar(stt[:, 9:10], stt[:, 8:9],
                                                    stt[:, 5:6], 1024.0,
                                                    ALU.mult, ALU.mult))
                        v.wait_ge(sVe.h, w)
                        V["t4_%d" % t] = sVe.inc(
                            v.tensor_scalar(stt[:, 9:10], stt[:, 9:10], 1.0, 1e-24,
                                            ALU.mult, ALU.add))
                    blk.vector(a2_ve2_t)

                    def a2_ac2_t(s, t=t):
                        s.wait_ge(sVe.h, V["t4_%d" % t])
                        w = sAc.inc(s.activation(stt[:, 10:11], stt[:, 9:10], AF.Ln))
                        s.wait_ge(sAc.h, w)
                        rc = 16 + (t % 4)
                        V["rr_%d" % t] = sAc.inc(
                            s.activation(stt[:, rc:rc + 1], stt[:, 10:11], AF.Exp,
                                         scale=-0.5))
                    blk.scalar(a2_ac2_t)

                    if t >= 1:
                        emit_a2_tailA(t - 1)

                    def a2_gp_ln2(gp, t=t):
                        gp.wait_ge(sVe.h, V["negmr2_%d" % t])
                        if t >= 2:
                            gp.wait_ge(sPe.h, ln2T_tr[t - 2])
                        ln2_done[t] = sGp.inc(
                            gp.tensor_scalar(ln2[:, t % 2, :], x2b[:, t % 2, :],
                                             stt[:, 5:6], stt[:, 6:7],
                                             ALU.mult, ALU.add))
                    blk.gpsimd(a2_gp_ln2)

                    if t >= 2:
                        emit_a2_tailB(t - 2)

            emit_a2_tailA(NT - 1)

            def c_gp_w1e(gp):
                gp.wait_ge(sPe.h, V["sc_last_7"])
                w1r = w1s_h.rearrange("(kd p) h -> p kd h", p=128)
                a1v = arena[:, 0:32768].rearrange("p (kd h) -> p kd h", kd=KD)
                for part in range(4):
                    V["w1_%d" % part] = sW1p[part].inc(gp.dma_start(
                        out=a1v[:, :, part * 1024:(part + 1) * 1024],
                        in_=w1r[:, :, part * 1024:(part + 1) * 1024]), 16)
            blk.gpsimd(c_gp_w1e)

            emit_a2_tailB(NT - 2)
            emit_a2_tailB(NT - 1)

            def c_gp_w2l(gp):
                gp.wait_ge(sPe.h, ctx_done[NT - 1])
                w2r = w2s_h.rearrange("(hc p) d -> p hc d", p=128)
                a2v = arena[:, VOFF:VOFF + 32768].rearrange(
                    "p (hc d) -> p hc d", hc=32)
                for part in range(4):
                    sW2c.inc(gp.dma_start(
                        out=a2v[:, part * 8:(part + 1) * 8, :],
                        in_=w2r[:, part * 8:(part + 1) * 8, :]), 16)
            blk.gpsimd(c_gp_w2l)

            if debug_taps:
                gdump = nc.dram_tensor("gdump", [128, 256], BF16,
                                       kind="ExternalOutput").ap()
                cwdump = nc.dram_tensor("cwdump", [128, 128], BF16,
                                        kind="ExternalOutput").ap()
                lgdump = nc.dram_tensor("lgdump", [128, 16], F32,
                                        kind="ExternalOutput").ap()

                def dump_g(gp):
                    gp.wait_ge(sVe.h, gates_done[31])
                    sSp.inc(gp.dma_start(out=gdump, in_=gates_tm[:]), 16)
                    sSp.inc(gp.dma_start(out=cwdump, in_=cw[:]), 16)
                    sSp.inc(gp.dma_start(out=lgdump, in_=rt16[:]), 16)
                blk.gpsimd(dump_g)

            a2_pe_end = sPe.n
            a2_ac_end = sAc.n
            a2_ve_end = sVe.n

        with ExitStack() as pc:
            Ec = pc.enter_context
            gT2 = Ec(nc.sbuf_tensor("gT2", [NEXP, 2, 256], BF16))
            gbc = Ec(nc.sbuf_tensor("gbc", [128, 8, 256], BF16))
            ln2Tg = Ec(nc.sbuf_tensor("ln2Tg", [128, 2, KD, 256], BF16))
            x2g = Ec(nc.sbuf_tensor("x2g", [128, 2, 2048], F32))
            outsb = Ec(nc.sbuf_tensor("outsb", [128, 2, D], F32))
            hT = Ec(nc.sbuf_tensor("hT", [128, 2, 32, 256], BF16))
            b2s = Ec(nc.sbuf_tensor("b2_s", [NEXP, D], BF16))
            ph = [Ec(nc.psum_tensor("ph0", [128, 512], F32)),
                  Ec(nc.psum_tensor("ph1", [128, 512], F32))]
            pout = [Ec(nc.psum_tensor("pout0", [128, 1024], F32)),
                    Ec(nc.psum_tensor("pout1", [128, 1024], F32))]
            pbc = Ec(nc.psum_tensor("pbc", [128, 512], F32))
            ptG = Ec(nc.psum_tensor("ptG", [128, 1024], BF16))

            def c_gp_w(gp):
                gp.wait_ge(sPe.h, a2_pe_end)
                gp.wait_ge(sAc.h, a2_ac_end)
                gp.wait_ge(sVe.h, a2_ve_end)
                gp.wait_ge(sSl[0].h, sSl[0].n)
                gp.wait_ge(sSl[1].h, sSl[1].n)
                V["w2b2"] = sB2.inc(gp.dma_start(out=b2s[:], in_=b2_h), 16)
            blk.gpsimd(c_gp_w)

            def c_sync_gate(sync):
                sync.wait_ge(sAc.h, a2_ac_end)
                sync.wait_ge(sVe.h, a2_ve_end)
                sync.wait_ge(sPe.h, a2_pe_end)
            blk.sync(c_sync_gate)

            def _preload_gelu(sc):
                _load_act_set(nc, [AF.Gelu])
            blk.scalar(_preload_gelu)

            ln2g_ld = [None] * 16
            x2g_ld = [None] * 16
            w2_done = [None] * NT
            out_ev = [None] * NT
            out_st = [None] * NT

            def emit_c_gates(g):
                def c_pe_gT(pe, g=g):
                    pe.wait_ge(sVe.h, gates_done[2 * g + 1])
                    if g == 0:
                        pe.wait_ge(sVe.h, a2_ve_end)
                        pe.wait_ge(sAc.h, a2_ac_end)
                    if g >= 1:
                        pe.wait_ge(sVe.h, V["gT2cp_%d" % (g - 1)])
                    for tl in range(2):
                        mm = pe.transpose(ptG[0:8, tl * 128:tl * 128 + 128],
                                          gates_tm[:, (2 * g + tl) * 8:
                                                   (2 * g + tl + 1) * 8],
                                          ident[:])
                    V["gTtr_%d" % g] = sPe.inc(mm)
                blk.tensor(c_pe_gT)

                def c_ve_gT(v, g=g):
                    v.wait_ge(sPe.h, V["gTtr_%d" % g])
                    if g >= 2:
                        v.wait_ge(sPe.h, w2_done[2 * (g - 2) + 1])
                    V["gT2cp_%d" % g] = sVe.inc(
                        v.tensor_copy(gT2[:, g % 2, :], ptG[0:8, 0:256]))
                blk.vector(c_ve_gT)

            def emit_c_tail(gc):
                for tl in range(2):
                    t = 2 * gc + tl

                    def c_pe_w2(pe, gc=gc, tl=tl, t=t):
                        pe.wait_ge(sVe.h, V["gm_last_%d" % gc])
                        if gc == 0 and tl == 0:
                            pe.wait_ge(sW2c.h, sW2c.n)
                            pe.wait_ge(sB2.h, V["w2b2"])
                        if t >= 2:
                            pe.wait_ge(sVe.h, out_ev[t - 2])
                        for hc in range(32):
                            lhs = hT[:, gc % 2, hc, tl * 128:tl * 128 + 128]
                            b = VOFF + hc * 1024
                            pe.matmul(pout[tl][:, 0:512], lhs, arena[:, b:b + 512],
                                      start=(hc == 0), stop=False)
                            pe.matmul(pout[tl][:, 512:1024], lhs,
                                      arena[:, b + 512:b + 1024],
                                      start=(hc == 0), stop=False)
                        pe.matmul(pout[tl][:, 0:512],
                                  gT2[0:8, gc % 2, tl * 128:tl * 128 + 128],
                                  b2s[0:8, 0:512], start=False, stop=True)
                        mm = pe.matmul(pout[tl][:, 512:1024],
                                       gT2[0:8, gc % 2, tl * 128:tl * 128 + 128],
                                       b2s[0:8, 512:1024], start=False, stop=True)
                        w2_done[t] = sPe.inc(mm)
                    blk.tensor(c_pe_w2)

                    def c_ve_out(v, gc=gc, tl=tl, t=t):
                        v.wait_ge(sPe.h, w2_done[t])
                        v.wait_ge(sGx[gc % 2].h, x2g_ld[gc])
                        if t >= 2:
                            v.wait_ge(sF[t % 2].h, out_st[t - 2])
                        out_ev[t] = sVe.inc(v.tensor_tensor(
                            outsb[:, tl, :], pout[tl][:],
                            x2g[:, gc % 2, tl * 1024:(tl + 1) * 1024], ALU.add))
                    blk.vector(c_ve_out)

                    def c_gp_out(gp, t=t, tl=tl):
                        gp.wait_ge(sVe.h, out_ev[t])
                        out_st[t] = sF[t % 2].inc(gp.dma_start(
                            out=out_h[t * 128:(t + 1) * 128, :],
                            in_=outsb[:, tl, :]), 16)
                    blk.gpsimd(c_gp_out)

            emit_c_gates(0)

            for gc in range(16):
                def c_sync_g(sync, gc=gc):
                    sync.wait_ge(sSl[0].h, x2_store[2 * gc])
                    sync.wait_ge(sSl[1].h, x2_store[2 * gc + 1])
                    if gc >= 2:
                        sync.wait_ge(sPe.h, V["h_last_%d" % (gc - 2)])
                    ln2g_ld[gc] = sGl[gc % 2].inc(sync.dma_start(
                        out=ln2Tg[:, gc % 2, :, :],
                        in_=ln2t_scr[gc].rearrange("p (kd tc) -> p kd tc", kd=KD)), 16)
                    if gc >= 2:
                        sync.wait_ge(sVe.h, out_ev[2 * (gc - 2) + 1])
                    x2g_ld[gc] = sGx[gc % 2].inc(sync.dma_start(
                        out=x2g[:, gc % 2, :], in_=x2_scr[gc]), 16)
                blk.sync(c_sync_g)

                gbanks = [pbc, ph[0], ph[1], pbc]
                for pair in range(4):
                    def c_pe_gbc(pe, gc=gc, pair=pair):
                        if pair == 0:
                            pe.wait_ge(sVe.h, V["gT2cp_%d" % gc])
                        if pair == 3:
                            pe.wait_ge(sVe.h, V["gbc_cp_%d_0" % gc])
                        if gc >= 1:
                            if pair == 1:
                                pe.wait_ge(sAc.h, V["gelu_%d_%d" % (gc - 1, 30)])
                            elif pair == 2:
                                pe.wait_ge(sAc.h, V["gelu_%d_%d" % (gc - 1, 31)])
                        for e2 in range(2):
                            e = pair * 2 + e2
                            mm = pe.matmul(gbanks[pair][:, e2 * 256:(e2 + 1) * 256],
                                           sel[0:8, e * 128:(e + 1) * 128],
                                           gT2[0:8, gc % 2, :],
                                           start=True, stop=True)
                        V["gbc_mm_%d_%d" % (gc, pair)] = sPe.inc(mm)
                    blk.tensor(c_pe_gbc)

                    def c_ve_gbc(v, gc=gc, pair=pair):
                        v.wait_ge(sPe.h, V["gbc_mm_%d_%d" % (gc, pair)])
                        V["gbc_cp_%d_%d" % (gc, pair)] = sVe.inc(v.tensor_copy(
                            gbc[:, pair * 2:(pair + 1) * 2, :].rearrange(
                                "p e c -> p (e c)"),
                            gbanks[pair][:, 0:512]))
                    blk.vector(c_ve_gbc)

                for hc in range(32):
                    def c_pe_h(pe, gc=gc, hc=hc):
                        if hc == 0:
                            pe.wait_ge(sGl[gc % 2].h, ln2g_ld[gc])
                            pe.wait_ge(sVe.h, V["gbc_cp_%d_1" % gc])
                        if hc == 1:
                            pe.wait_ge(sVe.h, V["gbc_cp_%d_2" % gc])
                        if gc == 0:
                            pe.wait_ge(sW1p[hc // 8].h, V["w1_%d" % (hc // 8)])
                        if hc >= 2:
                            pe.wait_ge(sAc.h, V["gelu_%d_%d" % (gc, hc - 2)])
                        elif gc >= 1:
                            pe.wait_ge(sAc.h, V["gelu_%d_%d" % (gc - 1, 30 + hc)])
                        for kd in range(KD):
                            mm = pe.matmul(
                                ph[hc % 2][:, 0:256],
                                arena[:, kd * 4096 + hc * 128:
                                      kd * 4096 + hc * 128 + 128],
                                ln2Tg[:, gc % 2, kd, :],
                                start=(kd == 0), stop=(kd == KD - 1))
                        V["h_%d_%d" % (gc, hc)] = sPe.inc(mm)
                        if hc == 31:
                            V["h_last_%d" % gc] = sPe.n
                    blk.tensor(c_pe_h)

                    def c_ac_gelu(s, gc=gc, hc=hc):
                        if gc == 0 and hc == 0:
                            s.wait_ge(sSl[0].h, sSl[0].n)
                            s.wait_ge(sSl[1].h, sSl[1].n)
                            s.wait_ge(sVe.h, a2_ve_end)
                        if hc == 0 and gc >= 2:
                            s.wait_ge(sPe.h, w2_done[2 * (gc - 2) + 1])
                        s.wait_ge(sPe.h, V["h_%d_%d" % (gc, hc)])
                        V["gelu_%d_%d" % (gc, hc)] = sAc.inc(
                            s.activation(hT[:, gc % 2, hc, :], ph[hc % 2][:, 0:256],
                                         AF.Gelu, bias=b1s[:, hc:hc + 1]))
                    blk.scalar(c_ac_gelu)

                    def c_ve_gm(v, gc=gc, hc=hc):
                        v.wait_ge(sAc.h, V["gelu_%d_%d" % (gc, hc)])
                        if hc == 0:
                            v.wait_ge(sVe.h, V["gbc_cp_%d_3" % gc])
                        e = hc // 4
                        V["gm_%d_%d" % (gc, hc)] = sVe.inc(v.tensor_tensor(
                            hT[:, gc % 2, hc, :], hT[:, gc % 2, hc, :],
                            gbc[:, e, :], ALU.mult))
                        if hc == 31:
                            V["gm_last_%d" % gc] = sVe.n
                    blk.vector(c_ve_gm)

                if gc >= 1:
                    emit_c_tail(gc - 1)

                if gc + 1 < 16:
                    emit_c_gates(gc + 1)

            emit_c_tail(15)

            def fin_sync(sync):
                sync.wait_ge(sF[0].h, sF[0].n)
                sync.wait_ge(sF[1].h, sF[1].n)
            blk.sync(fin_sync)

            def fin_gp(gp):
                gp.wait_ge(sF[0].h, sF[0].n)
                gp.wait_ge(sF[1].h, sF[1].n)
            blk.gpsimd(fin_gp)

    return nc


def _host_consts():
    import ml_dtypes
    ident = np.eye(128, dtype=ml_dtypes.bfloat16)
    cmask = np.triu(np.ones((128, 128), dtype=np.float32)).astype(ml_dtypes.bfloat16)
    consts = np.array([CONSTS], dtype=np.float32)
    selm = np.zeros((NEXP, NEXP * 128), dtype=np.float32)
    for e in range(NEXP):
        selm[e, e * 128:(e + 1) * 128] = 1.0
    selm = selm.astype(ml_dtypes.bfloat16)
    return ident, cmask, consts, selm


def kernel(L0_attn_x, L0_attn_Wq, L0_attn_Wk, L0_attn_Wv,
           L0_ffn_w1, L0_ffn_b1, L0_ffn_w2, L0_ffn_b2,
           L0_ffn_centroids, L0_ffn_w_route):
    x = np.ascontiguousarray(np.asarray(L0_attn_x, dtype=np.float32))
    wqT = np.ascontiguousarray(np.asarray(L0_attn_Wq, dtype=np.float32).T)
    wkT = np.ascontiguousarray(np.asarray(L0_attn_Wk, dtype=np.float32).T)
    wvT = np.ascontiguousarray(np.asarray(L0_attn_Wv, dtype=np.float32).T)
    w1s = np.ascontiguousarray(
        np.asarray(L0_ffn_w1, dtype=np.float32).transpose(1, 0, 2).reshape(D, DFF))
    w2s = np.ascontiguousarray(np.asarray(L0_ffn_w2, dtype=np.float32).reshape(DFF, D))
    b1m = np.ascontiguousarray(
        np.asarray(L0_ffn_b1, dtype=np.float32).reshape(DFF).reshape(32, 128).T)
    b2 = np.ascontiguousarray(np.asarray(L0_ffn_b2, dtype=np.float32))
    cent = np.ascontiguousarray(np.asarray(L0_ffn_centroids, dtype=np.float32))
    wrt = np.ascontiguousarray(np.asarray(L0_ffn_w_route, dtype=np.float32))
    ident, cmask, consts, selm = _host_consts()

    nc = build(debug_taps=False)
    shared = dict(wqT=wqT, wkT=wkT, wvT=wvT, w1s=w1s, w2s=w2s, b1m=b1m, b2=b2,
                  cent=cent, wrt=wrt, ident=ident, cmask=cmask,
                  consts=consts, sel=selm)
    in_maps = [dict(shared, x=np.ascontiguousarray(x[b])) for b in range(8)]
    res = run_bass_kernel_spmd(nc, in_maps, core_ids=list(range(8)))
    return np.stack([r["out"] for r in res.results], axis=0)


# version 21
# speedup vs baseline: 1.1822x; 1.0017x over previous
import numpy as np
from contextlib import ExitStack

import concourse.bass as bass
import concourse.mybir as mybir
from concourse.bass_utils import run_bass_kernel_spmd
from concourse.hw_specs import get_activation_tables


def _load_act_set(nc, funcs):
    for idx, fs in enumerate(get_activation_tables(nc.m.arch).values()):
        if all(f in fs for f in funcs):
            nc.scalar.add_instruction(mybir.InstLoadActFuncSet(
                name=nc.get_next_instruction_name(),
                act_func_set_id=idx, ins=[], outs=[]))
            return
    raise ValueError(f"no table set contains {funcs}")

F32 = mybir.dt.float32
BF16 = mybir.dt.bfloat16
AF = mybir.ActivationFunctionType
ALU = mybir.AluOpType
AX = mybir.AxisListType

NTOK = 4096
D = 1024
NT = 32
KD = 8
DFF = 4096
NEXP = 8
EPS = 1e-5

CONSTS = [-0.5, EPS, 1024.0, 1.0 / 32.0, -1.0, 1.0 / 1024.0, 1e-24, -EPS]

VOFF = 32768
VSTRIDE = 1024
ONESCOL = 65536


def build(debug_taps=False):
    nc = bass.Bass("TRN2", target_bir_lowering=False)

    x_h = nc.dram_tensor("x", [NTOK, D], F32, kind="ExternalInput").ap()
    wqT_h = nc.dram_tensor("wqT", [D, D], F32, kind="ExternalInput").ap()
    wkT_h = nc.dram_tensor("wkT", [D, D], F32, kind="ExternalInput").ap()
    wvT_h = nc.dram_tensor("wvT", [D, D], F32, kind="ExternalInput").ap()
    w1s_h = nc.dram_tensor("w1s", [D, DFF], F32, kind="ExternalInput").ap()
    w2s_h = nc.dram_tensor("w2s", [DFF, D], F32, kind="ExternalInput").ap()
    b1m_h = nc.dram_tensor("b1m", [128, 32], F32, kind="ExternalInput").ap()
    b2_h = nc.dram_tensor("b2", [NEXP, D], F32, kind="ExternalInput").ap()
    cent_h = nc.dram_tensor("cent", [NEXP, D], F32, kind="ExternalInput").ap()
    wrt_h = nc.dram_tensor("wrt", [NEXP, D], F32, kind="ExternalInput").ap()
    ident_h = nc.dram_tensor("ident", [128, 128], BF16, kind="ExternalInput").ap()
    sel_h = nc.dram_tensor("sel", [NEXP, NEXP * 128], BF16, kind="ExternalInput").ap()
    cmask_h = nc.dram_tensor("cmask", [128, 128], BF16, kind="ExternalInput").ap()
    consts_h = nc.dram_tensor("consts", [1, len(CONSTS)], F32, kind="ExternalInput").ap()

    skind = "ExternalOutput" if debug_taps else "Internal"
    qt_scr = nc.dram_tensor("qt_scr", [8, 128, 4096], BF16, kind=skind).ap()
    ln2t_scr = nc.dram_tensor("ln2t_scr", [16, 128, 2048], BF16, kind=skind).ap()
    x2_scr = nc.dram_tensor("x2_scr", [16, 128, 2048], F32, kind=skind).ap()

    out_h = nc.dram_tensor("out", [NTOK, D], F32, kind="ExternalOutput").ap()

    V = {}

    with ExitStack() as topctx:
        E = topctx.enter_context

        arena = E(nc.sbuf_tensor("arena", [128, 65544], BF16))
        gates_tm = E(nc.sbuf_tensor("gates_tm", [128, 256], BF16))
        ident = E(nc.sbuf_tensor("ident_s", [128, 128], BF16))
        cmask = E(nc.sbuf_tensor("cmask_s", [128, 128], BF16))
        cw = E(nc.sbuf_tensor("cw_s", [128, 128], BF16))
        sel = E(nc.sbuf_tensor("sel_s", [NEXP, NEXP * 128], BF16))
        b1s = E(nc.sbuf_tensor("b1_s", [128, 32], F32))
        consts = E(nc.sbuf_tensor("consts_s", [1, len(CONSTS)], F32))
        stt = E(nc.sbuf_tensor("stt", [128, 32], F32))
        rt16 = E(nc.sbuf_tensor("rt16", [128, 16], F32))
        logit = E(nc.sbuf_tensor("logit", [128, 8], F32))
        top8 = E(nc.sbuf_tensor("top8", [128, 8], F32))
        expl = E(nc.sbuf_tensor("expl", [128, 8], F32))
        maskg = E(nc.sbuf_tensor("maskg", [128, 8], F32))
        gated = E(nc.sbuf_tensor("gated", [128, 8], F32))
        cstat = E(nc.sbuf_tensor("cstat", [128, 4], F32))

        class Ctr:
            def __init__(self, name):
                self.h = E(nc.semaphore(name))
                self.n = 0

            def inc(self, instr, k=1):
                instr.then_inc(self.h, k)
                self.n += k
                return self.n

        sIn = Ctr("sIn")
        sWt = [Ctr("sWt%d" % i) for i in range(3)]
        sSp = Ctr("sSp")
        sPe = Ctr("sPe")
        sAc = Ctr("sAc")
        sVe = Ctr("sVe")
        sX = [Ctr("sX0"), Ctr("sX1")]
        sQb = [Ctr("sQ0"), Ctr("sQ1")]
        sXr = [Ctr("sXr0"), Ctr("sXr1")]
        sSl = [Ctr("sSl0"), Ctr("sSl1")]
        sGl = [Ctr("sGl0"), Ctr("sGl1")]
        sGx = [Ctr("sGx0"), Ctr("sGx1")]
        sW1p = [Ctr("sW1p%d" % i) for i in range(4)]
        sW2c = Ctr("sW2c")
        sB2 = Ctr("sB2")
        sSel = Ctr("sSel")
        sF = [Ctr("sF0"), Ctr("sF1")]
        sMs = Ctr("sMs")
        sGp = Ctr("sGp")

        blk = E(nc.Block())

        def _preload_lnexp(sc):
            _load_act_set(nc, [AF.Ln, AF.Exp, AF.Identity])
        blk.scalar(_preload_lnexp)

        with ExitStack() as pset:
            Es = pset.enter_context
            centf = Es(nc.sbuf_tensor("centf", [NEXP, D], F32))
            wrtf = Es(nc.sbuf_tensor("wrtf", [NEXP, D], F32))
            cscr = Es(nc.sbuf_tensor("cscr", [NEXP, D], F32))
            cnormbf = Es(nc.sbuf_tensor("cnormbf", [NEXP, D], BF16))
            wrtbf = Es(nc.sbuf_tensor("wrtbf", [NEXP, D], BF16))
            ptmp = Es(nc.psum_tensor("ptmp", [128, 1024], BF16))

            def setup_sync(sync):
                for dst, src in [(ident, ident_h), (cmask, cmask_h),
                                 (b1s, b1m_h), (consts, consts_h),
                                 (centf, cent_h), (wrtf, wrt_h),
                                 (sel, sel_h)]:
                    sIn.inc(sync.dma_start(out=dst[:], in_=src), 16)
                V["setup_in"] = sIn.n
            blk.sync(setup_sync)

            for i, cv in enumerate(CONSTS):
                nc.const_aps.register(F32, cv, consts[0:1, i:i + 1])

            def setup_vec(v):
                v.wait_ge(sIn.h, V["setup_in"])
                w = sVe.inc(v.scalar_tensor_tensor(
                    cscr[:], centf[:], 1.0, centf[:],
                    ALU.bypass, ALU.mult, accum_out=cstat[0:8, 0:1]))
                v.wait_ge(sVe.h, w)
                V["cstat"] = sVe.inc(v.tensor_scalar(
                    cstat[0:8, 3:4], cstat[0:8, 0:1], 1e-24, None, ALU.add))
            blk.vector(setup_vec)

            def setup_scal(s):
                s.wait_ge(sVe.h, V["cstat"])
                w = sAc.inc(s.activation(cstat[0:8, 1:2], cstat[0:8, 3:4], AF.Ln))
                s.wait_ge(sAc.h, w)
                V["c_rc"] = sAc.inc(
                    s.activation(cstat[0:8, 2:3], cstat[0:8, 1:2], AF.Exp, scale=-0.5))
            blk.scalar(setup_scal)

            def setup_vec2(v):
                v.wait_ge(sAc.h, V["c_rc"])
                v.tensor_scalar(cnormbf[:], centf[:], cstat[0:8, 2:3], None, ALU.mult)
                V["cnorm"] = sVe.inc(v.tensor_copy(wrtbf[:], wrtf[:]))
            blk.vector(setup_vec2)

            def cw_pe(pe):
                pe.wait_ge(sVe.h, V["cnorm"])
                for kd in range(KD):
                    pe.transpose(ptmp[:, kd * 16:kd * 16 + 8],
                                 cnormbf[0:8, kd * 128:(kd + 1) * 128], ident[0:8, 0:8])
                    mm = pe.transpose(ptmp[:, kd * 16 + 8:kd * 16 + 16],
                                      wrtbf[0:8, kd * 128:(kd + 1) * 128], ident[0:8, 0:8])
                V["cw_tr"] = sPe.inc(mm)
            blk.tensor(cw_pe)

            def cw_ve(v):
                v.wait_ge(sPe.h, V["cw_tr"])
                V["cw"] = sVe.inc(v.tensor_copy(cw[:], ptmp[:, 0:128]))
            blk.vector(cw_ve)

            if debug_taps:
                cw0dump = nc.dram_tensor("cw0dump", [128, 128], BF16,
                                         kind="ExternalOutput").ap()
                cn0dump = nc.dram_tensor("cn0dump", [NEXP, D], BF16,
                                         kind="ExternalOutput").ap()

                def dump_cw0(gp):
                    gp.wait_ge(sVe.h, V["cw"])
                    sSp.inc(gp.dma_start(out=cw0dump, in_=cw[:]), 16)
                    sSp.inc(gp.dma_start(out=cn0dump, in_=cnormbf[:]), 16)
                    V["cw0d"] = sSp.n
                blk.gpsimd(dump_cw0)

        with ExitStack() as pa1:
            Ea = pa1.enter_context
            wqkv = Ea(nc.sbuf_tensor("wqkv", [128, 3, KD, D], BF16))
            scr = Ea(nc.sbuf_tensor("scr", [128, D], F32))
            xbuf = Ea(nc.sbuf_tensor("xbuf", [128, 2, D], F32))
            xln = Ea(nc.sbuf_tensor("xln", [128, 2, D], BF16))
            xlnT = Ea(nc.sbuf_tensor("xlnT", [128, 2, KD, 256], BF16))
            qstage = Ea(nc.sbuf_tensor("qstage", [128, KD, 256], BF16))
            pqk = [Ea(nc.psum_tensor("pqk0", [128, 512], F32)),
                   Ea(nc.psum_tensor("pqk1", [128, 512], F32))]
            pv = [Ea(nc.psum_tensor("pv0", [128, 1024], F32)),
                  Ea(nc.psum_tensor("pv1", [128, 1024], F32))]
            ptr = [Ea(nc.psum_tensor("ptr0", [128, 1024], BF16)),
                   Ea(nc.psum_tensor("ptr1", [128, 1024], BF16))]

            def a1_gpsimd(g):
                g.wait_ge(sVe.h, V["cw"])
                for i, wh in enumerate([wqT_h, wkT_h, wvT_h]):
                    V["wqkv%d" % i] = sWt[i].inc(g.dma_start(
                        out=wqkv[:, i, :, :],
                        in_=wh.rearrange("(kd p) m -> p kd m", p=128)), 16)
                V["vones"] = sMs.inc(
                    g.memset(arena[:, ONESCOL:ONESCOL + 1], 1.0))
            blk.gpsimd(a1_gpsimd)

            x_ld = [None] * NT
            xln_done = [None] * NT
            sq_done = [None] * NT
            stats_done = [None] * NT
            negmr_done = [None] * NT
            tr_done = [None] * NT
            xlt_done = [None] * NT
            qk_done = {}
            qk_ev = {}
            v_done = [None] * NT
            v_ev = [None] * NT
            qspill = [None] * 16

            def emit_a1_head_sync(t):
                def a1_sync_t(sync, t=t):
                    if t == 0:
                        sync.wait_ge(sVe.h, V["cw"])
                    if t >= 2:
                        sync.wait_ge(sGp.h, xln_done[t - 2])
                    x_ld[t] = sX[t % 2].inc(sync.dma_start(
                        out=xbuf[:, t % 2, :],
                        in_=x_h[t * 128:(t + 1) * 128, :]), 16)
                blk.sync(a1_sync_t)

            def emit_a1_head_v1(g):
                sb = 16 * (g % 2)

                def a1_v1(v, g=g, sb=sb):
                    v.wait_ge(sX[0].h, x_ld[2 * g])
                    v.wait_ge(sX[1].h, x_ld[2 * g + 1])
                    if g >= 2:
                        v.wait_ge(sGp.h, xln_done[2 * (g - 2) + 1])
                    w = sVe.inc(v.reduce_sum(stt[:, sb:sb + 2], xbuf[:, :, :], AX.X))
                    v.wait_ge(sVe.h, w)
                    w = sVe.inc(v.scalar_tensor_tensor(
                        scr[:], xbuf[:, 0, :], 1.0, xbuf[:, 0, :],
                        ALU.bypass, ALU.mult, accum_out=stt[:, sb + 2:sb + 3]))
                    v.wait_ge(sVe.h, w)
                    w = sVe.inc(v.scalar_tensor_tensor(
                        scr[:], xbuf[:, 1, :], 1.0, xbuf[:, 1, :],
                        ALU.bypass, ALU.mult, accum_out=stt[:, sb + 3:sb + 4]))
                    v.wait_ge(sVe.h, w)
                    w = sVe.inc(v.tensor_tensor(stt[:, sb + 4:sb + 6],
                                                stt[:, sb:sb + 2],
                                                stt[:, sb:sb + 2], ALU.mult))
                    v.wait_ge(sVe.h, w)
                    w = sVe.inc(v.tensor_scalar(stt[:, sb + 4:sb + 6],
                                                stt[:, sb + 4:sb + 6],
                                                1.0 / 1048576.0, None, ALU.mult))
                    v.wait_ge(sVe.h, w)
                    w = sVe.inc(v.tensor_scalar(stt[:, sb + 6:sb + 8],
                                                stt[:, sb + 2:sb + 4],
                                                1.0 / 1024.0, EPS,
                                                ALU.mult, ALU.add))
                    v.wait_ge(sVe.h, w)
                    V["var_%d" % g] = sVe.inc(v.tensor_tensor(
                        stt[:, sb + 8:sb + 10], stt[:, sb + 6:sb + 8],
                        stt[:, sb + 4:sb + 6], ALU.subtract))
                blk.vector(a1_v1)

            def emit_a1_head_s1(g):
                sb = 16 * (g % 2)

                def a1_s1(s, g=g, sb=sb):
                    s.wait_ge(sVe.h, V["var_%d" % g])
                    w = sAc.inc(s.activation(stt[:, sb + 10:sb + 12],
                                             stt[:, sb + 8:sb + 10], AF.Ln))
                    s.wait_ge(sAc.h, w)
                    V["rstd_%d" % g] = sAc.inc(
                        s.activation(stt[:, sb + 12:sb + 14],
                                     stt[:, sb + 10:sb + 12], AF.Exp, scale=-0.5))
                blk.scalar(a1_s1)

            def emit_a1_head_v2(g):
                sb = 16 * (g % 2)

                def a1_v2(v, g=g, sb=sb):
                    v.wait_ge(sAc.h, V["rstd_%d" % g])
                    w = sVe.inc(v.tensor_tensor(stt[:, sb + 14:sb + 16],
                                                stt[:, sb:sb + 2],
                                                stt[:, sb + 12:sb + 14], ALU.mult))
                    v.wait_ge(sVe.h, w)
                    V["negmr_%d" % g] = sVe.inc(
                        v.tensor_scalar(stt[:, sb + 14:sb + 16],
                                        stt[:, sb + 14:sb + 16],
                                        -1.0 / 1024.0, None, ALU.mult))
                blk.vector(a1_v2)

            def emit_a1_head_s2(g):
                sb = 16 * (g % 2)

                def a1_s2(gp, g=g, sb=sb):
                    gp.wait_ge(sVe.h, V["negmr_%d" % g])
                    for tl in range(2):
                        t = 2 * g + tl
                        if t >= 2:
                            gp.wait_ge(sPe.h, tr_done[t - 2])
                        xln_done[t] = sGp.inc(
                            gp.tensor_scalar(xln[:, t % 2, :], xbuf[:, t % 2, :],
                                             stt[:, sb + 12 + tl:sb + 13 + tl],
                                             stt[:, sb + 14 + tl:sb + 15 + tl],
                                             ALU.mult, ALU.add))
                blk.gpsimd(a1_s2)

            def emit_a1_trx(t):
                def a1_pe_t(pe, t=t):
                    pe.wait_ge(sGp.h, xln_done[t])
                    if t >= 2:
                        pe.wait_ge(sVe.h, xlt_done[t - 2])
                    for kd in range(KD):
                        mm = pe.transpose(ptr[t % 2][:, kd * 128:(kd + 1) * 128],
                                          xln[:, t % 2, kd * 128:(kd + 1) * 128],
                                          ident[:])
                    tr_done[t] = sPe.inc(mm)
                blk.tensor(a1_pe_t)

                def a1_vec3_t(v, t=t):
                    v.wait_ge(sPe.h, tr_done[t])
                    if t // 2 >= 2:
                        v.wait_ge(sPe.h, v_done[2 * (t // 2 - 2) + 1])
                    xlt_done[t] = sVe.inc(v.tensor_copy(
                        xlnT[:, (t // 2) % 2, :, (t % 2) * 128:(t % 2) * 128 + 128],
                        ptr[t % 2][:].rearrange("p (kd c) -> p kd c", kd=KD)))
                blk.vector(a1_vec3_t)

            def emit_a1_vburst(g):
                for tl in range(2):
                    def a1_pe_v(pe, g=g, tl=tl):
                        if g == 0 and tl == 0:
                            pe.wait_ge(sWt[2].h, V["wqkv2"])
                        if g >= 1:
                            pe.wait_ge(sAc.h, v_ev[2 * (g - 1) + tl])
                        for kd in range(KD):
                            for half in range(2):
                                mm = pe.matmul(
                                    pv[tl][:, half * 512:(half + 1) * 512],
                                    xlnT[:, g % 2, kd, tl * 128:(tl + 1) * 128],
                                    wqkv[:, 2, kd, half * 512:(half + 1) * 512],
                                    start=(kd == 0), stop=(kd == KD - 1))
                        v_done[2 * g + tl] = sPe.inc(mm)
                    blk.tensor(a1_pe_v)

                    def a1_sc_v(s, g=g, tl=tl):
                        j = 2 * g + tl
                        s.wait_ge(sPe.h, v_done[j])
                        base = VOFF + j * VSTRIDE
                        v_ev[j] = sAc.inc(
                            s.copy(arena[:, base:base + 1024], pv[tl][:]))
                    blk.scalar(a1_sc_v)

                def a1_gp_g(gp, g=g):
                    gq, sub = g // 2, g % 2
                    gp.wait_ge(sAc.h, qk_ev[(g, 7)])
                    dst = qt_scr[gq].rearrange("p (m tc) -> p m tc", m=KD)[
                        :, :, sub * 256:sub * 256 + 256]
                    qspill[g] = sSp.inc(
                        gp.dma_start(out=dst, in_=qstage[:]), 16)
                blk.gpsimd(a1_gp_g)

            def emit_a1_qkburst(g):
                for u in range(16):
                    proj, m = u // 8, u % 8

                    def a1_pe_u(pe, g=g, u=u, proj=proj, m=m):
                        if u == 0:
                            pe.wait_ge(sVe.h, xlt_done[2 * g + 1])
                        if g == 0 and u % 8 == 0:
                            pe.wait_ge(sWt[proj].h, V["wqkv%d" % proj])
                        prev = qk_ev.get((g, u - 2)) if u >= 2 else (
                            qk_ev.get((g - 1, 14 + u)) if g >= 1 else None)
                        if prev is not None:
                            pe.wait_ge(sAc.h, prev)
                        for kd in range(KD):
                            mm = pe.matmul(
                                pqk[u % 2][:, 0:256],
                                wqkv[:, proj, kd, m * 128:(m + 1) * 128],
                                xlnT[:, g % 2, kd, :],
                                start=(kd == 0), stop=(kd == KD - 1))
                        qk_done[(g, u)] = sPe.inc(mm)
                    blk.tensor(a1_pe_u)

                    def a1_sc_u(s, g=g, u=u, proj=proj, m=m):
                        s.wait_ge(sPe.h, qk_done[(g, u)])
                        if u == 0 and g >= 1:
                            s.wait_ge(sSp.h, qspill[g - 1])
                        if proj == 0:
                            dst = qstage[:, m, :]
                        else:
                            dst = arena[:, m * 4096 + g * 256:
                                        m * 4096 + g * 256 + 256]
                        qk_ev[(g, u)] = sAc.inc(
                            s.copy(dst, pqk[u % 2][:, 0:256]))
                    blk.scalar(a1_sc_u)

            emit_a1_head_sync(0)
            emit_a1_head_sync(1)
            emit_a1_head_v1(0)
            emit_a1_head_s1(0)
            emit_a1_head_v2(0)
            emit_a1_head_s2(0)
            for g in range(16):
                emit_a1_trx(2 * g)
                emit_a1_trx(2 * g + 1)
                if 2 * g + 2 < NT:
                    emit_a1_head_sync(2 * g + 2)
                    emit_a1_head_sync(2 * g + 3)
                    emit_a1_head_v1(g + 1)
                if g >= 1:
                    emit_a1_vburst(g - 1)
                if 2 * g + 2 < NT:
                    emit_a1_head_s1(g + 1)
                    emit_a1_head_v2(g + 1)
                    emit_a1_head_s2(g + 1)
                emit_a1_qkburst(g)
            emit_a1_vburst(15)

            a1_pe_last = sPe.n
            a1_ac_last = sAc.n
            a1_last_ev = sVe.n

        with ExitStack() as pa2:
            Ea = pa2.enter_context
            probsT = Ea(nc.sbuf_tensor("probsT", [128, 32, 512], BF16))
            qbuf = Ea(nc.sbuf_tensor("qbuf", [128, 2, KD, 512], BF16))
            xr = Ea(nc.sbuf_tensor("xr", [128, 2, D], F32))
            x2b = Ea(nc.sbuf_tensor("x2b", [128, 2, D], F32))
            ln2 = Ea(nc.sbuf_tensor("ln2", [128, 2, D], BF16))
            ln2Ts = Ea(nc.sbuf_tensor("ln2Ts", [128, 2, D], BF16))
            psc = [Ea(nc.psum_tensor("psc0", [128, 512], F32)),
                   Ea(nc.psum_tensor("psc1", [128, 512], F32)),
                   Ea(nc.psum_tensor("psc2", [128, 512], F32))]
            pctx = Ea(nc.psum_tensor("pctx", [128, 1536], F32))
            ptrA = Ea(nc.psum_tensor("ptrA", [128, 1024], BF16))
            prt = Ea(nc.psum_tensor("prt", [128, 512], F32))

            q_ld = [None] * 8
            sc_seq = []
            xr_ld = [None] * NT
            mask_done = {}
            ctx_done = [None] * NT
            ctx_ev = [None] * NT
            ln2_done = [None] * NT
            ln2T_tr = [None] * NT
            ln2T_cp = [None] * NT
            rt_done = [None] * NT
            gates_done = [None] * NT
            spill_done = [None] * NT
            x2_store = [None] * NT

            def emit_a2_tailA(t):
                def a2_pe_tr(pe, t=t):
                    pe.wait_ge(sGp.h, ln2_done[t])
                    if t >= 1:
                        pe.wait_ge(sVe.h, ln2T_cp[t - 1])
                    for kd in range(KD):
                        mm = pe.transpose(ptrA[:, kd * 128:(kd + 1) * 128],
                                          ln2[:, t % 2, kd * 128:(kd + 1) * 128],
                                          ident[:])
                    ln2T_tr[t] = sPe.inc(mm)
                blk.tensor(a2_pe_tr)

                def a2_ve3_t(v, t=t):
                    v.wait_ge(sPe.h, ln2T_tr[t])
                    if t >= 2:
                        v.wait_ge(sSl[t % 2].h, x2_store[t - 2])
                        v.wait_ge(sPe.h, rt_done[t - 2])
                    ln2T_cp[t] = sVe.inc(
                        v.tensor_copy(ln2Ts[:, t % 2, :], ptrA[:]))
                blk.vector(a2_ve3_t)

                def a2_gp_t(gp, t=t):
                    gc, sub = t // 2, t % 2
                    gp.wait_ge(sVe.h, ln2T_cp[t])
                    dst = ln2t_scr[gc].rearrange("p (kd tc) -> p kd tc", kd=KD)[
                        :, :, sub * 128:sub * 128 + 128]
                    src = ln2Ts[:, t % 2, :].rearrange("p (kd c) -> p kd c", kd=KD)
                    spill_done[t] = sSl[t % 2].inc(gp.dma_start(out=dst, in_=src), 16)
                    gp.wait_ge(sVe.h, V["st2_%d" % t])
                    x2_store[t] = sSl[t % 2].inc(gp.dma_start(
                        out=x2_scr[gc, :, sub * 1024:sub * 1024 + 1024],
                        in_=x2b[:, t % 2, :]), 16)
                blk.gpsimd(a2_gp_t)


            def emit_a2_tailB(t):
                def a2_pe_rt(pe, t=t):
                    pe.wait_ge(sVe.h, ln2T_cp[t])
                    if t >= 1:
                        pe.wait_ge(sVe.h, V["rtcp_%d" % (t - 1)])
                    for kd in range(KD):
                        mm = pe.matmul(prt[:, 0:16],
                                       ln2Ts[:, t % 2, kd * 128:(kd + 1) * 128],
                                       cw[:, kd * 16:kd * 16 + 16],
                                       start=(kd == 0), stop=(kd == KD - 1))
                    rt_done[t] = sPe.inc(mm)
                blk.tensor(a2_pe_rt)

                def a2_ve4_t(v, t=t):
                    v.wait_ge(sPe.h, rt_done[t])
                    v.wait_ge(sAc.h, V["rr_%d" % t])
                    V["rtcp_%d" % t] = sVe.inc(v.tensor_copy(rt16[:], prt[:, 0:16]))
                    v.wait_ge(sVe.h, V["rtcp_%d" % t])
                    rc = 16 + (t % 4)
                    w = sVe.inc(v.scalar_tensor_tensor(
                        logit[:], rt16[:, 0:8], stt[:, rc:rc + 1],
                        rt16[:, 8:16], ALU.mult, ALU.add))
                    v.wait_ge(sVe.h, w)
                    w = sVe.inc(v.max(top8[:], logit[:]))
                    v.wait_ge(sVe.h, w)
                    V["negm1_%d" % t] = sVe.inc(
                        v.tensor_scalar(stt[:, 12:13], top8[:, 0:1], -1.0,
                                        None, ALU.mult))
                blk.vector(a2_ve4_t)

                def a2_ac3_t(s, t=t):
                    s.wait_ge(sVe.h, V["negm1_%d" % t])
                    V["expl_%d" % t] = sAc.inc(
                        s.activation(expl[:], logit[:], AF.Exp, bias=stt[:, 12:13]))
                blk.scalar(a2_ac3_t)

                def a2_ve5_t(v, t=t):
                    v.wait_ge(sAc.h, V["expl_%d" % t])
                    w = sVe.inc(v.tensor_scalar(maskg[:], logit[:], top8[:, 1:2],
                                                None, ALU.is_ge))
                    v.wait_ge(sVe.h, w)
                    w = sVe.inc(v.tensor_tensor(gated[:], expl[:], maskg[:],
                                                ALU.mult))
                    v.wait_ge(sVe.h, w)
                    w = sVe.inc(v.tensor_scalar(expl[:], gated[:], 1.0, 0.0,
                                                ALU.mult, ALU.add,
                                                accum_out=stt[:, 13:14]))
                    v.wait_ge(sVe.h, w)
                    w = sVe.inc(v.reciprocal(stt[:, 14:15], stt[:, 13:14]))
                    v.wait_ge(sVe.h, w)
                    gates_done[t] = sVe.inc(
                        v.tensor_scalar(gates_tm[:, t * 8:(t + 1) * 8], gated[:],
                                        stt[:, 14:15], None, ALU.mult))
                blk.vector(a2_ve5_t)

            for g2 in range(8):
                JN = 4 * g2 + 4

                def a2_sync_q(sync, g2=g2):
                    if g2 <= 1:
                        sync.wait_ge(sAc.h, a1_ac_last)
                        sync.wait_ge(sPe.h, a1_pe_last)
                        sync.wait_ge(sVe.h, a1_last_ev)
                        sync.wait_ge(sGp.h, xln_done[NT - 1])
                        sync.wait_ge(sSp.h, qspill[15])
                    else:
                        sync.wait_ge(sPe.h, V["sc_last_%d" % (g2 - 2)])
                    q_ld[g2] = sQb[g2 % 2].inc(sync.dma_start(
                        out=qbuf[:, g2 % 2, :, :], in_=qt_scr[g2].rearrange(
                            "p (m tc) -> p m tc", m=KD)), 16)
                blk.sync(a2_sync_q)

                for j in range(JN):
                    jj = len(sc_seq)
                    sc_seq.append((g2, j))

                    def a2_pe_sc(pe, g2=g2, j=j, JN=JN, jj=jj):
                        if j == 0:
                            pe.wait_ge(sQb[g2 % 2].h, q_ld[g2])
                        if jj >= 3:
                            pg, pj = sc_seq[jj - 3]
                            pe.wait_ge(sAc.h, V["exp_%d_%d" % (pg, pj)])
                        for kd in range(KD):
                            mm = pe.matmul(
                                psc[jj % 3][:],
                                arena[:, kd * 4096 + j * 128:kd * 4096 + j * 128 + 128],
                                qbuf[:, g2 % 2, kd, :],
                                start=(kd == 0), stop=(kd == KD - 1))
                        V["sc_%d_%d" % (g2, j)] = sPe.inc(mm)
                        if j == JN - 1:
                            V["sc_last_%d" % g2] = sPe.n
                    blk.tensor(a2_pe_sc)

                    def a2_ac_exp(s, g2=g2, j=j, jj=jj):
                        s.wait_ge(sPe.h, V["sc_%d_%d" % (g2, j)])
                        V["exp_%d_%d" % (g2, j)] = sAc.inc(
                            s.activation(probsT[:, j, :], psc[jj % 3][:],
                                         AF.Exp, scale=1.0 / 32.0))
                    blk.scalar(a2_ac_exp)

                def a2_ve_mask(v, g2=g2):
                    for j in range(4 * g2, 4 * g2 + 4):
                        off = (j - 4 * g2) * 128
                        v.wait_ge(sAc.h, V["exp_%d_%d" % (g2, j)])
                        mask_done[j] = sVe.inc(v.tensor_tensor(
                            probsT[:, j, off:off + 128],
                            probsT[:, j, off:off + 128],
                            cmask[:], ALU.mult))
                blk.vector(a2_ve_mask)

                for tl in range(4):
                    t = 4 * g2 + tl

                    def a2_sync_x(sync, t=t):
                        if t >= 2:
                            sync.wait_ge(sVe.h, V["st2_%d" % (t - 2)])
                        xr_ld[t] = sXr[t % 2].inc(sync.dma_start(
                            out=xr[:, t % 2, :],
                            in_=x_h[t * 128:(t + 1) * 128, :]), 16)
                    blk.sync(a2_sync_x)

                    def a2_pe_pv(pe, t=t, g2=g2, tl=tl):
                        jmax = 4 * g2 + tl
                        pe.wait_ge(sAc.h, V["exp_%d_%d" % (g2, jmax)])
                        pe.wait_ge(sVe.h, mask_done[jmax])
                        if t == 0:
                            pe.wait_ge(sMs.h, V["vones"])
                        if t >= 1:
                            pe.wait_ge(sVe.h, ctx_ev[t - 1])
                        for j in range(jmax + 1):
                            lhs = probsT[:, j, tl * 128:tl * 128 + 128]
                            base = VOFF + j * VSTRIDE
                            st, sp = (j == 0), (j == jmax)
                            pe.matmul(pctx[:, 0:512], lhs,
                                      arena[:, base:base + 512], start=st, stop=sp)
                            pe.matmul(pctx[:, 512:1024], lhs,
                                      arena[:, base + 512:base + 1024],
                                      start=st, stop=sp)
                            mm = pe.matmul(pctx[:, 1024:1025], lhs,
                                           arena[:, ONESCOL:ONESCOL + 1],
                                           start=st, stop=sp)
                        ctx_done[t] = sPe.inc(mm)
                    blk.tensor(a2_pe_pv)

                    def a2_ve_t(v, t=t):
                        v.wait_ge(sPe.h, ctx_done[t])
                        v.wait_ge(sXr[t % 2].h, xr_ld[t])
                        if t >= 2:
                            v.wait_ge(sSl[t % 2].h, x2_store[t - 2])
                            v.wait_ge(sGp.h, ln2_done[t - 2])
                        if t >= 1:
                            v.wait_ge(sAc.h, V["rstd2_%d" % (t - 1)])
                        w = sVe.inc(v.reciprocal(stt[:, 15:16], pctx[:, 1024:1025]))
                        v.wait_ge(sVe.h, w)
                        x2t = x2b[:, t % 2, :]
                        ctx_ev[t] = sVe.inc(
                            v.scalar_tensor_tensor(x2t, pctx[:, 0:1024], stt[:, 15:16],
                                                   xr[:, t % 2, :], ALU.mult, ALU.add))
                        v.wait_ge(sVe.h, ctx_ev[t])
                        v.reduce_sum(stt[:, 0:1], x2t, AX.X)
                        w = sVe.inc(v.scalar_tensor_tensor(
                            xr[:, t % 2, :], x2t, 1.0, x2t, ALU.bypass, ALU.mult,
                            accum_out=stt[:, 1:2]))
                        v.wait_ge(sVe.h, w)
                        w = sVe.inc(v.tensor_scalar(stt[:, 2:3], stt[:, 0:1],
                                                    1.0 / 1024.0, None, ALU.mult))
                        v.wait_ge(sVe.h, w)
                        w = sVe.inc(v.tensor_scalar(stt[:, 7:8], stt[:, 2:3],
                                                    stt[:, 2:3], -EPS, ALU.mult, ALU.add))
                        v.wait_ge(sVe.h, w)
                        V["st2_%d" % t] = sVe.inc(
                            v.tensor_scalar(stt[:, 3:4], stt[:, 1:2], 1.0 / 1024.0,
                                            stt[:, 7:8], ALU.mult, ALU.subtract))
                    blk.vector(a2_ve_t)

                    def a2_ac_t(s, t=t):
                        s.wait_ge(sVe.h, V["st2_%d" % t])
                        if t >= 1:
                            s.wait_ge(sGp.h, ln2_done[t - 1])
                        w = sAc.inc(s.activation(stt[:, 4:5], stt[:, 3:4], AF.Ln))
                        s.wait_ge(sAc.h, w)
                        V["rstd2_%d" % t] = sAc.inc(
                            s.activation(stt[:, 5:6], stt[:, 4:5], AF.Exp, scale=-0.5))
                    blk.scalar(a2_ac_t)

                    def a2_ve2_t(v, t=t):
                        v.wait_ge(sAc.h, V["rstd2_%d" % t])
                        V["negmr2_%d" % t] = sVe.inc(
                            v.tensor_scalar(stt[:, 6:7], stt[:, 2:3], -1.0,
                                            stt[:, 5:6], ALU.mult, ALU.mult))
                        w = sVe.inc(v.tensor_scalar(stt[:, 8:9], stt[:, 3:4], -EPS,
                                                    stt[:, 5:6], ALU.add, ALU.mult))
                        v.wait_ge(sVe.h, w)
                        w = sVe.inc(v.tensor_scalar(stt[:, 9:10], stt[:, 8:9],
                                                    stt[:, 5:6], 1024.0,
                                                    ALU.mult, ALU.mult))
                        v.wait_ge(sVe.h, w)
                        V["t4_%d" % t] = sVe.inc(
                            v.tensor_scalar(stt[:, 9:10], stt[:, 9:10], 1.0, 1e-24,
                                            ALU.mult, ALU.add))
                    blk.vector(a2_ve2_t)

                    def a2_ac2_t(s, t=t):
                        s.wait_ge(sVe.h, V["t4_%d" % t])
                        w = sAc.inc(s.activation(stt[:, 10:11], stt[:, 9:10], AF.Ln))
                        s.wait_ge(sAc.h, w)
                        rc = 16 + (t % 4)
                        V["rr_%d" % t] = sAc.inc(
                            s.activation(stt[:, rc:rc + 1], stt[:, 10:11], AF.Exp,
                                         scale=-0.5))
                    blk.scalar(a2_ac2_t)

                    if t >= 1:
                        emit_a2_tailA(t - 1)

                    def a2_gp_ln2(gp, t=t):
                        gp.wait_ge(sVe.h, V["negmr2_%d" % t])
                        if t >= 2:
                            gp.wait_ge(sPe.h, ln2T_tr[t - 2])
                        ln2_done[t] = sGp.inc(
                            gp.tensor_scalar(ln2[:, t % 2, :], x2b[:, t % 2, :],
                                             stt[:, 5:6], stt[:, 6:7],
                                             ALU.mult, ALU.add))
                    blk.gpsimd(a2_gp_ln2)

                    if t >= 2:
                        emit_a2_tailB(t - 2)

            emit_a2_tailA(NT - 1)

            def c_gp_w1e(gp):
                gp.wait_ge(sPe.h, V["sc_last_7"])
                w1r = w1s_h.rearrange("(kd p) h -> p kd h", p=128)
                a1v = arena[:, 0:32768].rearrange("p (kd h) -> p kd h", kd=KD)
                for part in range(4):
                    V["w1_%d" % part] = sW1p[part].inc(gp.dma_start(
                        out=a1v[:, :, part * 1024:(part + 1) * 1024],
                        in_=w1r[:, :, part * 1024:(part + 1) * 1024]), 16)
            blk.gpsimd(c_gp_w1e)

            emit_a2_tailB(NT - 2)
            emit_a2_tailB(NT - 1)

            def c_gp_w2l(gp):
                gp.wait_ge(sPe.h, ctx_done[NT - 1])
                w2r = w2s_h.rearrange("(hc p) d -> p hc d", p=128)
                a2v = arena[:, VOFF:VOFF + 32768].rearrange(
                    "p (hc d) -> p hc d", hc=32)
                for part in range(4):
                    sW2c.inc(gp.dma_start(
                        out=a2v[:, part * 8:(part + 1) * 8, :],
                        in_=w2r[:, part * 8:(part + 1) * 8, :]), 16)
            blk.gpsimd(c_gp_w2l)

            if debug_taps:
                gdump = nc.dram_tensor("gdump", [128, 256], BF16,
                                       kind="ExternalOutput").ap()
                cwdump = nc.dram_tensor("cwdump", [128, 128], BF16,
                                        kind="ExternalOutput").ap()
                lgdump = nc.dram_tensor("lgdump", [128, 16], F32,
                                        kind="ExternalOutput").ap()

                def dump_g(gp):
                    gp.wait_ge(sVe.h, gates_done[31])
                    sSp.inc(gp.dma_start(out=gdump, in_=gates_tm[:]), 16)
                    sSp.inc(gp.dma_start(out=cwdump, in_=cw[:]), 16)
                    sSp.inc(gp.dma_start(out=lgdump, in_=rt16[:]), 16)
                blk.gpsimd(dump_g)

            a2_pe_end = sPe.n
            a2_ac_end = sAc.n
            a2_ve_end = sVe.n

        with ExitStack() as pc:
            Ec = pc.enter_context
            gT2 = Ec(nc.sbuf_tensor("gT2", [NEXP, 2, 256], BF16))
            gbc = Ec(nc.sbuf_tensor("gbc", [128, 8, 256], BF16))
            ln2Tg = Ec(nc.sbuf_tensor("ln2Tg", [128, 2, KD, 256], BF16))
            x2g = Ec(nc.sbuf_tensor("x2g", [128, 2, 2048], F32))
            outsb = Ec(nc.sbuf_tensor("outsb", [128, 2, D], F32))
            hT = Ec(nc.sbuf_tensor("hT", [128, 2, 32, 256], BF16))
            b2s = Ec(nc.sbuf_tensor("b2_s", [NEXP, D], BF16))
            ph = [Ec(nc.psum_tensor("ph0", [128, 512], F32)),
                  Ec(nc.psum_tensor("ph1", [128, 512], F32))]
            pout = [Ec(nc.psum_tensor("pout0", [128, 1024], F32)),
                    Ec(nc.psum_tensor("pout1", [128, 1024], F32))]
            pbc = Ec(nc.psum_tensor("pbc", [128, 512], F32))
            ptG = Ec(nc.psum_tensor("ptG", [128, 1024], BF16))

            def c_gp_w(gp):
                gp.wait_ge(sPe.h, a2_pe_end)
                gp.wait_ge(sAc.h, a2_ac_end)
                gp.wait_ge(sVe.h, a2_ve_end)
                gp.wait_ge(sSl[0].h, sSl[0].n)
                gp.wait_ge(sSl[1].h, sSl[1].n)
                V["w2b2"] = sB2.inc(gp.dma_start(out=b2s[:], in_=b2_h), 16)
            blk.gpsimd(c_gp_w)

            def c_sync_gate(sync):
                sync.wait_ge(sPe.h, ctx_done[NT - 1])
            blk.sync(c_sync_gate)

            def _preload_gelu(sc):
                _load_act_set(nc, [AF.Gelu])
            blk.scalar(_preload_gelu)

            ln2g_ld = [None] * 16
            x2g_ld = [None] * 16
            w2_done = [None] * NT
            out_ev = [None] * NT
            out_st = [None] * NT

            def emit_c_gates(g):
                def c_pe_gT(pe, g=g):
                    pe.wait_ge(sVe.h, gates_done[2 * g + 1])
                    if g == 0:
                        pe.wait_ge(sVe.h, V["rtcp_%d" % (NT - 1)])
                    if g >= 1:
                        pe.wait_ge(sVe.h, V["gT2cp_%d" % (g - 1)])
                    for tl in range(2):
                        mm = pe.transpose(ptG[0:8, tl * 128:tl * 128 + 128],
                                          gates_tm[:, (2 * g + tl) * 8:
                                                   (2 * g + tl + 1) * 8],
                                          ident[:])
                    V["gTtr_%d" % g] = sPe.inc(mm)
                blk.tensor(c_pe_gT)

                def c_ve_gT(v, g=g):
                    v.wait_ge(sPe.h, V["gTtr_%d" % g])
                    if g >= 2:
                        v.wait_ge(sPe.h, w2_done[2 * (g - 2) + 1])
                    V["gT2cp_%d" % g] = sVe.inc(
                        v.tensor_copy(gT2[:, g % 2, :], ptG[0:8, 0:256]))
                blk.vector(c_ve_gT)

            def emit_c_tail(gc):
                for tl in range(2):
                    t = 2 * gc + tl

                    def c_pe_w2(pe, gc=gc, tl=tl, t=t):
                        pe.wait_ge(sVe.h, V["gm_last_%d" % gc])
                        if gc == 0 and tl == 0:
                            pe.wait_ge(sW2c.h, sW2c.n)
                            pe.wait_ge(sB2.h, V["w2b2"])
                        if t >= 2:
                            pe.wait_ge(sVe.h, out_ev[t - 2])
                        for hc in range(32):
                            lhs = hT[:, gc % 2, hc, tl * 128:tl * 128 + 128]
                            b = VOFF + hc * 1024
                            pe.matmul(pout[tl][:, 0:512], lhs, arena[:, b:b + 512],
                                      start=(hc == 0), stop=False)
                            pe.matmul(pout[tl][:, 512:1024], lhs,
                                      arena[:, b + 512:b + 1024],
                                      start=(hc == 0), stop=False)
                        pe.matmul(pout[tl][:, 0:512],
                                  gT2[0:8, gc % 2, tl * 128:tl * 128 + 128],
                                  b2s[0:8, 0:512], start=False, stop=True)
                        mm = pe.matmul(pout[tl][:, 512:1024],
                                       gT2[0:8, gc % 2, tl * 128:tl * 128 + 128],
                                       b2s[0:8, 512:1024], start=False, stop=True)
                        w2_done[t] = sPe.inc(mm)
                    blk.tensor(c_pe_w2)

                    def c_ve_out(v, gc=gc, tl=tl, t=t):
                        v.wait_ge(sPe.h, w2_done[t])
                        v.wait_ge(sGx[gc % 2].h, x2g_ld[gc])
                        if t >= 2:
                            v.wait_ge(sF[t % 2].h, out_st[t - 2])
                        out_ev[t] = sVe.inc(v.tensor_tensor(
                            outsb[:, tl, :], pout[tl][:],
                            x2g[:, gc % 2, tl * 1024:(tl + 1) * 1024], ALU.add))
                    blk.vector(c_ve_out)

                    def c_gp_out(gp, t=t, tl=tl):
                        gp.wait_ge(sVe.h, out_ev[t])
                        out_st[t] = sF[t % 2].inc(gp.dma_start(
                            out=out_h[t * 128:(t + 1) * 128, :],
                            in_=outsb[:, tl, :]), 16)
                    blk.gpsimd(c_gp_out)

            emit_c_gates(0)

            for gc in range(16):
                def c_sync_g(sync, gc=gc):
                    sync.wait_ge(sSl[0].h, x2_store[2 * gc])
                    sync.wait_ge(sSl[1].h, x2_store[2 * gc + 1])
                    if gc >= 2:
                        sync.wait_ge(sPe.h, V["h_last_%d" % (gc - 2)])
                    ln2g_ld[gc] = sGl[gc % 2].inc(sync.dma_start(
                        out=ln2Tg[:, gc % 2, :, :],
                        in_=ln2t_scr[gc].rearrange("p (kd tc) -> p kd tc", kd=KD)), 16)
                    if gc >= 2:
                        sync.wait_ge(sVe.h, out_ev[2 * (gc - 2) + 1])
                    x2g_ld[gc] = sGx[gc % 2].inc(sync.dma_start(
                        out=x2g[:, gc % 2, :], in_=x2_scr[gc]), 16)
                blk.sync(c_sync_g)

                gbanks = [pbc, ph[0], ph[1], pbc]
                for pair in range(4):
                    def c_pe_gbc(pe, gc=gc, pair=pair):
                        if pair == 0:
                            pe.wait_ge(sVe.h, V["gT2cp_%d" % gc])
                            if gc == 0:
                                pe.wait_ge(sVe.h, ln2T_cp[NT - 1])
                        if pair == 3:
                            pe.wait_ge(sVe.h, V["gbc_cp_%d_0" % gc])
                        if gc == 0:
                            if pair in (1, 2):
                                pe.wait_ge(sAc.h, V["exp_7_31"])
                        else:
                            if pair == 1:
                                pe.wait_ge(sAc.h, V["gelu_%d_%d" % (gc - 1, 30)])
                            elif pair == 2:
                                pe.wait_ge(sAc.h, V["gelu_%d_%d" % (gc - 1, 31)])
                        for e2 in range(2):
                            e = pair * 2 + e2
                            mm = pe.matmul(gbanks[pair][:, e2 * 256:(e2 + 1) * 256],
                                           sel[0:8, e * 128:(e + 1) * 128],
                                           gT2[0:8, gc % 2, :],
                                           start=True, stop=True)
                        V["gbc_mm_%d_%d" % (gc, pair)] = sPe.inc(mm)
                    blk.tensor(c_pe_gbc)

                    def c_ve_gbc(v, gc=gc, pair=pair):
                        v.wait_ge(sPe.h, V["gbc_mm_%d_%d" % (gc, pair)])
                        V["gbc_cp_%d_%d" % (gc, pair)] = sVe.inc(v.tensor_copy(
                            gbc[:, pair * 2:(pair + 1) * 2, :].rearrange(
                                "p e c -> p (e c)"),
                            gbanks[pair][:, 0:512]))
                    blk.vector(c_ve_gbc)

                for hc in range(32):
                    def c_pe_h(pe, gc=gc, hc=hc):
                        if hc == 0:
                            pe.wait_ge(sGl[gc % 2].h, ln2g_ld[gc])
                            pe.wait_ge(sVe.h, V["gbc_cp_%d_1" % gc])
                        if hc == 1:
                            pe.wait_ge(sVe.h, V["gbc_cp_%d_2" % gc])
                        if gc == 0:
                            pe.wait_ge(sW1p[hc // 8].h, V["w1_%d" % (hc // 8)])
                        if hc >= 2:
                            pe.wait_ge(sAc.h, V["gelu_%d_%d" % (gc, hc - 2)])
                        elif gc >= 1:
                            pe.wait_ge(sAc.h, V["gelu_%d_%d" % (gc - 1, 30 + hc)])
                        for kd in range(KD):
                            mm = pe.matmul(
                                ph[hc % 2][:, 0:256],
                                arena[:, kd * 4096 + hc * 128:
                                      kd * 4096 + hc * 128 + 128],
                                ln2Tg[:, gc % 2, kd, :],
                                start=(kd == 0), stop=(kd == KD - 1))
                        V["h_%d_%d" % (gc, hc)] = sPe.inc(mm)
                        if hc == 31:
                            V["h_last_%d" % gc] = sPe.n
                    blk.tensor(c_pe_h)

                    def c_ac_gelu(s, gc=gc, hc=hc):
                        if gc == 0 and hc == 0:
                            s.wait_ge(sSl[0].h, sSl[0].n)
                            s.wait_ge(sSl[1].h, sSl[1].n)
                            s.wait_ge(sVe.h, a2_ve_end)
                        if hc == 0 and gc >= 2:
                            s.wait_ge(sPe.h, w2_done[2 * (gc - 2) + 1])
                        s.wait_ge(sPe.h, V["h_%d_%d" % (gc, hc)])
                        V["gelu_%d_%d" % (gc, hc)] = sAc.inc(
                            s.activation(hT[:, gc % 2, hc, :], ph[hc % 2][:, 0:256],
                                         AF.Gelu, bias=b1s[:, hc:hc + 1]))
                    blk.scalar(c_ac_gelu)

                    def c_ve_gm(v, gc=gc, hc=hc):
                        v.wait_ge(sAc.h, V["gelu_%d_%d" % (gc, hc)])
                        if hc == 0:
                            v.wait_ge(sVe.h, V["gbc_cp_%d_3" % gc])
                        e = hc // 4
                        V["gm_%d_%d" % (gc, hc)] = sVe.inc(v.tensor_tensor(
                            hT[:, gc % 2, hc, :], hT[:, gc % 2, hc, :],
                            gbc[:, e, :], ALU.mult))
                        if hc == 31:
                            V["gm_last_%d" % gc] = sVe.n
                    blk.vector(c_ve_gm)

                if gc >= 1:
                    emit_c_tail(gc - 1)

                if gc + 1 < 16:
                    emit_c_gates(gc + 1)

            emit_c_tail(15)

            def fin_sync(sync):
                sync.wait_ge(sF[0].h, sF[0].n)
                sync.wait_ge(sF[1].h, sF[1].n)
            blk.sync(fin_sync)

            def fin_gp(gp):
                gp.wait_ge(sF[0].h, sF[0].n)
                gp.wait_ge(sF[1].h, sF[1].n)
            blk.gpsimd(fin_gp)

    return nc


def _host_consts():
    import ml_dtypes
    ident = np.eye(128, dtype=ml_dtypes.bfloat16)
    cmask = np.triu(np.ones((128, 128), dtype=np.float32)).astype(ml_dtypes.bfloat16)
    consts = np.array([CONSTS], dtype=np.float32)
    selm = np.zeros((NEXP, NEXP * 128), dtype=np.float32)
    for e in range(NEXP):
        selm[e, e * 128:(e + 1) * 128] = 1.0
    selm = selm.astype(ml_dtypes.bfloat16)
    return ident, cmask, consts, selm


def kernel(L0_attn_x, L0_attn_Wq, L0_attn_Wk, L0_attn_Wv,
           L0_ffn_w1, L0_ffn_b1, L0_ffn_w2, L0_ffn_b2,
           L0_ffn_centroids, L0_ffn_w_route):
    x = np.ascontiguousarray(np.asarray(L0_attn_x, dtype=np.float32))
    wqT = np.ascontiguousarray(np.asarray(L0_attn_Wq, dtype=np.float32).T)
    wkT = np.ascontiguousarray(np.asarray(L0_attn_Wk, dtype=np.float32).T)
    wvT = np.ascontiguousarray(np.asarray(L0_attn_Wv, dtype=np.float32).T)
    w1s = np.ascontiguousarray(
        np.asarray(L0_ffn_w1, dtype=np.float32).transpose(1, 0, 2).reshape(D, DFF))
    w2s = np.ascontiguousarray(np.asarray(L0_ffn_w2, dtype=np.float32).reshape(DFF, D))
    b1m = np.ascontiguousarray(
        np.asarray(L0_ffn_b1, dtype=np.float32).reshape(DFF).reshape(32, 128).T)
    b2 = np.ascontiguousarray(np.asarray(L0_ffn_b2, dtype=np.float32))
    cent = np.ascontiguousarray(np.asarray(L0_ffn_centroids, dtype=np.float32))
    wrt = np.ascontiguousarray(np.asarray(L0_ffn_w_route, dtype=np.float32))
    ident, cmask, consts, selm = _host_consts()

    nc = build(debug_taps=False)
    shared = dict(wqT=wqT, wkT=wkT, wvT=wvT, w1s=w1s, w2s=w2s, b1m=b1m, b2=b2,
                  cent=cent, wrt=wrt, ident=ident, cmask=cmask,
                  consts=consts, sel=selm)
    in_maps = [dict(shared, x=np.ascontiguousarray(x[b])) for b in range(8)]
    res = run_bass_kernel_spmd(nc, in_maps, core_ids=list(range(8)))
    return np.stack([r["out"] for r in res.results], axis=0)
